# version 02
# baseline (speedup 1.0000x reference)
import math
import numpy as np
import ml_dtypes
import concourse.bass as bass
import concourse.mybir as mybir
from concourse.bass_utils import run_bass_kernel_spmd
from concourse.expressions import make_scalar_value

F32 = mybir.dt.float32
BF16 = mybir.dt.bfloat16
I32 = mybir.dt.int32
AF = mybir.ActivationFunctionType
ALU = mybir.AluOpType
AX = mybir.AxisListType.X

NCORE = 8
B, T, DM = 4, 4096, 2048
NT = 2048
H, HD = 16, 128
HPC = 8
NS = T // 128
NSH = NT // 128
SC0 = 1.0 / math.sqrt(HD)
SC1 = HD ** (-0.5)
THETA = 10000.0
HS = 2816
HF = 1408
NEXP = 8
NHS = 110
RANK = 64
EPS = 1e-5
BIGM = 3.0e9

bf16 = ml_dtypes.bfloat16



def _rope_tables():
    half = HD // 2
    inv_freq = 1.0 / THETA ** (np.arange(half, dtype=np.float32) / half)
    ang = np.arange(T, dtype=np.float32)[:, None] * inv_freq[None, :]
    cos = np.cos(ang).astype(np.float32).T
    sin = np.sin(ang).astype(np.float32).T
    cos2 = np.concatenate([cos, cos], axis=0)
    sins = np.concatenate([-sin, sin], axis=0)
    return np.ascontiguousarray(cos2), np.ascontiguousarray(sins)


def _l1_delta_tiles():
    tiles = np.zeros((6, 128, 512), dtype=np.float32)
    ii = np.arange(512)[None, :]
    jj = np.arange(128)[:, None]
    for k, R in enumerate(range(-3, 3)):
        delta = 128 * R + ii - jj
        t = np.where(delta < 0, BIGM, delta / SC1)
        tiles[k] = t.astype(np.float32)
    return tiles


def _diag_masks():
    jj = np.arange(128)[:, None]
    ii = np.arange(128)[None, :]
    strict = np.where(jj < ii, 0.0, BIGM).astype(np.float32)
    incl = np.where(jj <= ii, 0.0, BIGM).astype(np.float32)
    return strict, incl


def make_core_inputs(c, inp, nonce):
    p, h = c // 2, c % 2
    hc = slice(h * 1024, (h + 1) * 1024)
    hh = slice(h * 8, (h + 1) * 8)
    tk = slice(h * NT, (h + 1) * NT)
    x = inp["L0_attn_hidden_states"][p]

    cos2, sins = _rope_tables()
    dstrict, dincl = _diag_masks()

    d = {
        "xT": np.ascontiguousarray(x.T),
        "x0h": np.ascontiguousarray(x[tk]),
        "wqkvb": np.ascontiguousarray(np.concatenate([
            inp["L0_attn_Wq"][:, hc], inp["L0_attn_Wk"][:, hc],
            inp["L0_attn_Wv"][:, hc], inp["L0_attn_Wb"][:, hh]], axis=1)),
        "bb8c": np.ascontiguousarray(inp["L0_attn_bb"][hh][:, None]),
        "wo": np.ascontiguousarray(inp["L0_attn_Wo"]),
        "gwT": np.ascontiguousarray(inp["L0_ffn_shared_gw"].T),
        "sdwT": np.ascontiguousarray(inp["L0_ffn_shared_dw"].T),
        "gateT": np.ascontiguousarray(inp["L0_ffn_gate_w"].T),
        "gbias": np.ascontiguousarray(
            inp["L0_ffn_gate_bias"][None, :]),
        "upT": np.ascontiguousarray(
            inp["L0_ffn_up_w"].transpose(0, 2, 1)),
        "wblk": stage_wblk(inp),
        "dnT": np.ascontiguousarray(
            inp["L0_ffn_down_w"].transpose(0, 2, 1)),
        "qwT": np.ascontiguousarray(inp["L1_attn_q_w"].T[:, hc]),
        "kwT": np.ascontiguousarray(inp["L1_attn_k_w"].T[:, hc]),
        "vwT": np.ascontiguousarray(inp["L1_attn_v_w"].T[:, hc]),
        "qb": np.ascontiguousarray(
            inp["L1_attn_q_b"][hc].reshape(8, 128).T),
        "kb": np.ascontiguousarray(
            inp["L1_attn_k_b"][hc].reshape(8, 128).T),
        "vbrow": np.ascontiguousarray(inp["L1_attn_v_b"][hc][None, :]),
        "owT": np.ascontiguousarray(inp["L1_attn_o_w"].T),
        "obrow": np.ascontiguousarray(inp["L1_attn_o_b"][None, :]),
        "AT": np.ascontiguousarray(inp["L1_ffn_A"].transpose(0, 2, 1)),
        "BT": np.ascontiguousarray(inp["L1_ffn_B"].transpose(0, 2, 1)),
        "wrT": np.ascontiguousarray(inp["L1_ffn_Wr"].T),
        "br": np.ascontiguousarray(inp["L1_ffn_br"][None, :]),
        "br8c": np.ascontiguousarray(inp["L1_ffn_br"][:, None]),
        "cos2": cos2, "sins": sins,
        "ident": np.eye(128, dtype=np.float32),
        "dstrict": dstrict, "dincl": dincl,
        "l1delta": _l1_delta_tiles(),
        "iota8": np.arange(8, dtype=np.float32)[None, :],
        "iotaB8": (np.arange(8, dtype=np.float32) + 100.0)[None, :],
        "meta": np.array([[c, c ^ 1, nonce, 0, 0, 0, 0, 0]], dtype=np.int32),
    }
    return d



class Ctx:

    def __init__(self, nc, marks=None):
        self.nc = nc
        self.sems = {}
        self.cnt = {}
        self.final = marks
        self.marks = {}

    def sem(self, name):
        if name not in self.sems:
            self.sems[name] = self.nc.alloc_semaphore(name)
            self.cnt[name] = 0
        return self.sems[name]

    def inc(self, inst, name, n):
        inst.then_inc(self.sem(name), n)
        self.cnt[name] += n
        return self.cnt[name]

    def wait(self, eng, name, at=None):
        eng.wait_ge(self.sem(name), self.cnt[name] if at is None else at)

    def val(self, name):
        self.sem(name)
        return self.cnt[name]

    def mark(self, name, label):
        key = (name, label)
        assert key not in self.marks, f"duplicate mark {key}"
        self.marks[key] = self.cnt.get(name, 0)

    def wait_mark(self, eng, name, label):
        key = (name, label)
        if self.final is None:
            tgt = self.marks.get(key, 0)
        else:
            assert key in self.final, f"missing mark {key} in pass 1"
            tgt = self.final[key]
        eng.wait_ge(self.sem(name), tgt)

    def check_pass2(self):
        if self.final is not None:
            assert self.marks == self.final, "pass-2 marks diverged from pass-1"



def emit_p1(nc, ctx, io, ten):
    xT, wqkvb = io["xT"], io["wqkvb"]
    bb8c, cos2, sins, ident = io["bb8c"], io["cos2"], io["sins"], io["ident"]
    kT_s, qT_s, v_s = io["kT_s"], io["qT_s"], io["v_s"]
    beta_s, betaT_s = io["beta_s"], io["betaT_s"]

    NCHUNK = 8

    with (
        nc.sbuf_tensor("p1_w", [128, 16, 3080], BF16) as wsb,
        nc.sbuf_tensor("p1_x", [128, 2, 16, 512], BF16) as xsb,
        nc.sbuf_tensor("p1_cos", [128, T], F32) as cossb,
        nc.sbuf_tensor("p1_sin", [128, T], F32) as sinsb,
        nc.sbuf_tensor("p1_bb", [8, 1], F32) as bbsb,
        nc.sbuf_tensor("p1_id", [128, 128], F32) as idsb,
        nc.sbuf_tensor("p1_tmpA", [128, 512], F32) as tmpA,
        nc.sbuf_tensor("p1_tmpB", [128, 512], F32) as tmpB,
        nc.sbuf_tensor("p1_out", [128, 2, 512], BF16) as outb,
        nc.sbuf_tensor("p1_beta", [8, T], F32) as betasb,
        nc.sbuf_tensor("p1_bT", [128, 32 * 8], F32) as bTsb,
        nc.sbuf_tensor("p1_v", [128, 2, 8, 132], BF16) as vsb,
        nc.psum_tensor("p1_ps0", [128, 512], F32) as ps0,
        nc.psum_tensor("p1_ps1", [128, 512], F32) as ps1,
        nc.psum_tensor("p1_psv", [128, 512], F32) as psv,
        nc.psum_tensor("p1_psb", [8, 512], F32) as psb,
        nc.psum_tensor("p1_pst", [128, 8], F32) as pst,
        nc.Block("p1") as blk,
    ):
        @blk.gpsimd
        def _(g):
            for s in range(16):
                ctx.inc(g.dma_start(out=wsb[:, s, :],
                                    in_=wqkvb[s * 128:(s + 1) * 128, :]),
                        "p1_w", 16)
            xTr = xT.rearrange("(s p) t -> p s t", p=128)
            for ch in range(NCHUNK):
                if ch >= 2:
                    ctx.wait(g, "p1_mmx", at=ch - 1)
                ctx.inc(g.dma_start(
                    out=xsb[:, ch % 2, :, :],
                    in_=xTr[:, :, ch * 512:(ch + 1) * 512]),
                    f"p1_x{ch % 2}", 16)

        @blk.vector
        def _(v):
            ctx.inc(v.memset(vsb[:, 0, :, 128:132], 1.0), "p1_ones", 1)
            ctx.inc(v.memset(vsb[:, 1, :, 128:132], 1.0), "p1_ones", 1)

        @blk.tensor
        def _(t):
            ctx.wait(t, "p1_w")
            ctx.wait(t, "p1_c")
            for ch in range(NCHUNK):
                ctx.wait(t, f"p1_x{ch % 2}", at=(ch // 2 + 1) * 16)
                xb = xsb[:, ch % 2, :, :]
                for bi in range(16):
                    j = ch * 16 + bi
                    ps = (ps0, ps1)[j % 2]
                    if j >= 2:
                        ctx.wait(t, "p1_rope", at=j - 1)
                    for s in range(16):
                        inst = t.matmul(ps[:],
                                        wsb[:, s, bi * 128:(bi + 1) * 128],
                                        xb[:, s, :],
                                        start=(s == 0), stop=(s == 15))
                    ctx.inc(inst, "p1_mmqk", 1)
                for ts in range(4):
                    for dc in range(2):
                        idx = ch * 8 + ts * 2 + dc
                        if idx >= 1:
                            ctx.wait(t, "p1_evv", at=idx)
                        for s in range(16):
                            inst = t.matmul(
                                psv[:],
                                xb[:, s, ts * 128:(ts + 1) * 128],
                                wsb[:, s, 2048 + dc * 512:2048 + (dc + 1) * 512],
                                start=(s == 0), stop=(s == 15))
                        ctx.inc(inst, "p1_mmv", 1)
                if ch >= 1:
                    ctx.wait(t, "p1_evb", at=ch)
                for s in range(16):
                    inst = t.matmul(psb[:], wsb[:, s, 3072:3080], xb[:, s, :],
                                    start=(s == 0), stop=(s == 15))
                ctx.inc(inst, "p1_mmb", 1)
                ctx.inc(t.nop(), "p1_mmx", 1)
            ctx.wait(t, "p1_evb", at=NCHUNK)
            for st in range(32):
                if st >= 1:
                    ctx.wait(t, "p1_evbT", at=st)
                inst = t.transpose(pst[:], betasb[:, st * 128:(st + 1) * 128],
                                   idsb[0:8, 0:8])
                ctx.inc(inst, "p1_mmbT", 1)

        @blk.vector
        def _(v):
            v.wait_ge(ctx.sem("p1_c"), 64)
            for ch in range(NCHUNK):
                csl = slice(ch * 512, (ch + 1) * 512)
                for bi in range(16):
                    j = ch * 16 + bi
                    ps = (ps0, ps1)[j % 2]
                    ctx.wait(v, "p1_mmqk", at=j + 1)
                    if j >= 2:
                        ctx.wait(v, f"p1_st{j % 2}", at=16 * (j // 2))
                    v.tensor_mul(tmpA[:], ps[:], cossb[:, csl])
                    v.tensor_mul(tmpB[0:64, :], ps[64:128, :],
                                 sinsb[0:64, csl])
                    v.tensor_mul(tmpB[64:128, :], ps[0:64, :],
                                 sinsb[64:128, csl])
                    inst = v.tensor_add(outb[:, j % 2, :], tmpA[:], tmpB[:])
                    ctx.inc(inst, "p1_rope", 1)
                for ts in range(4):
                    gt = ch * 4 + ts
                    ctx.wait(v, "p1_ones")
                    if gt >= 2:
                        ctx.wait(v, f"p1_stv{gt % 2}", at=16 * (gt // 2))
                    for dc in range(2):
                        idx = ch * 8 + ts * 2 + dc
                        ctx.wait(v, "p1_mmv", at=idx + 1)
                        for hh4 in range(4):
                            hidx = dc * 4 + hh4
                            inst = v.tensor_copy(
                                vsb[:, gt % 2, hidx, 0:128],
                                psv[:, hh4 * 128:(hh4 + 1) * 128])
                        ctx.inc(inst, "p1_evv", 1)
            for st in range(32):
                ctx.wait(v, "p1_mmbT", at=st + 1)
                inst = v.tensor_copy(bTsb[:, st * 8:(st + 1) * 8], pst[:])
                ctx.inc(inst, "p1_evbT", 1)

        @blk.scalar
        def _(s):
            for ch in range(NCHUNK):
                ctx.wait(s, "p1_mmb", at=ch + 1)
                ctx.wait(s, "p1_c", at=48)
                inst = s.activation(out=betasb[:, ch * 512:(ch + 1) * 512],
                                    in_=psb[:], func=AF.Identity,
                                    bias=bbsb[:, 0:1])
                ctx.inc(inst, "p1_evb", 1)

        @blk.sync
        def _(s):
            ctx.inc(s.dma_start(out=cossb[:], in_=cos2[:, :]), "p1_c", 16)
            ctx.inc(s.dma_start(out=sinsb[:], in_=sins[:, :]), "p1_c", 16)
            ctx.inc(s.dma_start(out=bbsb[:], in_=bb8c[:, :]), "p1_c", 16)
            ctx.inc(s.dma_start(out=idsb[:], in_=ident[:, :]), "p1_c", 16)
            for ch in range(NCHUNK):
                csl = slice(ch * 512, (ch + 1) * 512)
                for bi in range(16):
                    j = ch * 16 + bi
                    ctx.wait(s, "p1_rope", at=j + 1)
                    dst = qT_s if bi < 8 else kT_s
                    row = (bi % 8) * 128
                    ctx.inc(s.dma_start(out=dst[row:row + 128, csl],
                                        in_=outb[:, j % 2, :]),
                            f"p1_st{j % 2}", 16)
                for ts in range(4):
                    gt = ch * 4 + ts
                    ctx.wait(s, "p1_evv", at=2 * (gt + 1))
                    ctx.inc(s.dma_start(
                        out=v_s[gt * 128:(gt + 1) * 128, :, :],
                        in_=vsb[:, gt % 2, :, :]), f"p1_stv{gt % 2}", 16)
            ctx.wait(s, "p1_evb", at=NCHUNK)
            ctx.inc(s.dma_start(out=beta_s[:, :], in_=betasb[:]), "p1_st", 16)
            ctx.wait(s, "p1_evbT", at=32)
            bT = bTsb.ap().rearrange("p (s e) -> p s e", e=8)
            ctx.inc(s.dma_start(
                out=betaT_s.rearrange("(s p) e -> p s e", p=128)[:, :, :],
                in_=bT[:, :, :]), "p1_st", 16)
            for nm in ("p1_st0", "p1_st1", "p1_stv0", "p1_stv1", "p1_st"):
                ctx.wait(s, nm)



P2_NH = 8
P2_NS = NS
P2_DEBUG = False
P2_DBG_I = 0
P4_LIMIT = None


def emit_p2(nc, ctx, io, ten):
    kT_s, qT_s, v_s = io["kT_s"], io["qT_s"], io["v_s"]
    beta_s, betaT_s = io["beta_s"], io["betaT_s"]
    dstrict, dincl, ident = io["dstrict"], io["dincl"], io["ident"]
    oT0_s = io["oT0_s"]

    sched = []
    last_m = {}
    gid = 0
    for h in range(P2_NH):
        for i in range(P2_NS):
            for phase in (0, 1):
                gls = [list(range(g, min(g + 4, i))) for g in range(0, i, 4)]
                last_m[(h, i, phase)] = (gid + len(gls) - 1) if gls else None
                for gl in gls:
                    sched.append((gid, h, i, phase, gl))
                    gid += 1
    by_hi = {}
    for rec in sched:
        by_hi.setdefault((rec[1], rec[2], rec[3]), []).append(rec)

    from contextlib import ExitStack
    with ExitStack() as _st:
        ksb = _st.enter_context(nc.sbuf_tensor("p2_k", [128, 2, T], BF16))
        qsb = _st.enter_context(nc.sbuf_tensor("p2_q", [128, 2, T], BF16))
        vsb = _st.enter_context(nc.sbuf_tensor("p2_v", [128, 2, 32, 132], BF16))
        usb = _st.enter_context(nc.sbuf_tensor("p2_u", [128, 2, 32, 132], BF16))
        betarow = _st.enter_context(nc.sbuf_tensor("p2_brow", [1, 2, T], F32))
        bTsb = _st.enter_context(nc.sbuf_tensor("p2_bT", [128, 32, 8], F32))
        idsb = _st.enter_context(nc.sbuf_tensor("p2_id", [128, 128], F32))
        idbsb = _st.enter_context(nc.sbuf_tensor("p2_idb", [128, 128], BF16))
        dstrsb = _st.enter_context(nc.sbuf_tensor("p2_dstr", [128, 128], F32))
        dincsb = _st.enter_context(nc.sbuf_tensor("p2_dinc", [128, 128], F32))
        esb = _st.enter_context(nc.sbuf_tensor("p2_e", [128, 2, 512], BF16))
        edsb = _st.enter_context(nc.sbuf_tensor("p2_ed", [128, 128], BF16))
        ed2sb = _st.enter_context(nc.sbuf_tensor("p2_ed2", [128, 128], BF16))
        zsb = _st.enter_context(nc.sbuf_tensor("p2_z", [128, 128], F32))
        densb = _st.enter_context(nc.sbuf_tensor("p2_den", [128, 4], F32))
        nsb = _st.enter_context(nc.sbuf_tensor("p2_n", [128, 128], BF16))
        Ssb = _st.enter_context(nc.sbuf_tensor("p2_S", [128, 2, 128], BF16))
        Psb = _st.enter_context(nc.sbuf_tensor("p2_P", [128, 2, 128], BF16))
        ysb = _st.enter_context(nc.sbuf_tensor("p2_y", [128, 2, 128], BF16))
        osb = _st.enter_context(nc.sbuf_tensor("p2_o", [128, 128], BF16))
        otsb = _st.enter_context(nc.sbuf_tensor("p2_ot", [128, 2, 128], BF16))
        zosb = _st.enter_context(nc.sbuf_tensor("p2_zo", [128, 132], BF16))
        dbgT = _st.enter_context(nc.sbuf_tensor("p2_dbgT", [128, 128], F32))
        dbgD = _st.enter_context(nc.sbuf_tensor("p2_dbgD", [128, 8], F32))
        psE0 = _st.enter_context(nc.psum_tensor("p2_psE0", [128, 512], F32))
        psE1 = _st.enter_context(nc.psum_tensor("p2_psE1", [128, 512], F32))
        psM = _st.enter_context(nc.psum_tensor("p2_psM", [128, 132], F32))
        psA = _st.enter_context(nc.psum_tensor("p2_psA", [128, 128], F32))
        psT = _st.enter_context(nc.psum_tensor("p2_psT", [128, 128], BF16))
        blk = _st.enter_context(nc.Block("p2"))
        psE = (psE0, psE1)

        @blk.tensor
        def _(t):
            ctx.wait_mark(t, "p2_ev", "init")
            ctx.wait_mark(t, "p2_gld", "done")
            for h in range(P2_NH):
                hp = h % 2
                nld = 16 * (h // 2 + 1)
                t.wait_ge(ctx.sem(f"p2_ldk{hp}"), nld)
                t.wait_ge(ctx.sem(f"p2_ldq{hp}"), nld)
                kh = ksb[:, hp, :]
                qh = qsb[:, hp, :]
                for i in range(P2_NS):
                    isl = slice(i * 128, (i + 1) * 128)
                    for phase in (0, 1):
                        rh = kh if phase == 0 else qh
                        if phase == 0 and i > 0:
                            ctx.wait_mark(t, "p2_ev", ("yk", h, i - 1, 6))
                            ctx.wait_mark(t, "p2_ev", ("ob", h, i - 1))
                        if phase == 1:
                            ctx.wait_mark(t, "p2_ev", ("yk", h, i, 6))
                            ctx.wait_mark(t, "p2_ev", ("y0", h, i))
                        first_mm2 = True
                        lm_here = last_m[(h, i, phase)]
                        for (g, _h, _i, _ph, gl) in by_hi.get((h, i, phase), []):
                            pe = psE[g % 2]
                            if g >= 2:
                                ctx.wait_mark(t, "p2_ex", ("x", g - 2))
                            for a, j in enumerate(gl):
                                inst = t.matmul(pe[:, a * 128:(a + 1) * 128],
                                                kh[:, j * 128:(j + 1) * 128],
                                                rh[:, isl],
                                                start=True, stop=True)
                            ctx.inc(inst, "p2_mm", 1)
                            ctx.mark("p2_mm", ("e", g))
                            ctx.wait_mark(t, "p2_ex", ("x", g))
                            for a, j in enumerate(gl):
                                inst = t.matmul(
                                    psM[:, 0:129],
                                    esb[:, g % 2, a * 128:(a + 1) * 128],
                                    usb[:, hp, j, 0:129],
                                    start=first_mm2, stop=False)
                                first_mm2 = False
                            ctx.inc(inst, "p2_mm", 1)
                            ctx.mark("p2_mm", ("m", g))
                        if phase == 0:
                            if i > 0:
                                ctx.wait_mark(t, "p2_ev", ("zd2", h, i - 1))
                            inst = t.matmul(psA[:], kh[:, isl], kh[:, isl],
                                            start=True, stop=True)
                            ctx.inc(inst, "p2_mm", 1)
                            ctx.mark("p2_mm", ("ed", h, i))
                            ctx.wait_mark(t, "p2_ex", ("xd", h, i))
                            inst = t.matmul(psM[:, 0:129], edsb[:],
                                            zosb[:, 0:129],
                                            start=first_mm2, stop=True)
                            ctx.inc(inst, "p2_mm", 1)
                            ctx.mark("p2_mm", ("md", h, i))
                            if i > 0:
                                ctx.wait_mark(t, "p2_ev", ("oe", h, i - 1))
                            inst = t.transpose(psT[:], edsb[:], idbsb[:])
                            ctx.inc(inst, "p2_mm", 1)
                            ctx.mark("p2_mm", ("edt", h, i))
                            ctx.wait_mark(t, "p2_ev", ("pb", h, i))
                            inst = t.transpose(psT[:], Psb[:, 1, :], idbsb[:])
                            ctx.inc(inst, "p2_mm", 1)
                            ctx.mark("p2_mm", ("ntr", h, i))
                            ctx.wait_mark(t, "p2_ev", ("y0", h, i))
                            ctx.wait_mark(t, "p2_ev", ("nb", h, i))
                            inst = t.matmul(psA[:], nsb[:], ysb[:, 0, :],
                                            start=True, stop=True)
                            ctx.inc(inst, "p2_mm", 1)
                            ctx.mark("p2_mm", ("ap", h, i, 0))
                            for k in range(1, 7):
                                Sold = nsb[:] if k == 1 else Ssb[:, (k - 1) % 2, :]
                                Pold = (Psb[:, 1, :] if k == 1
                                        else Psb[:, (k - 1) % 2, :])
                                ctx.wait_mark(t, "p2_ev", ("yk", h, i, k - 1))
                                inst = t.matmul(psA[:], Pold, Sold,
                                                start=True, stop=True)
                                ctx.inc(inst, "p2_mm", 1)
                                ctx.mark("p2_mm", ("sq", h, i, k))
                                ctx.wait_mark(t, "p2_ev", ("sqe", h, i, k))
                                if k < 6:
                                    inst = t.matmul(psA[:], Sold, Pold,
                                                    start=True, stop=True)
                                    ctx.inc(inst, "p2_mm", 1)
                                    ctx.mark("p2_mm", ("sqp", h, i, k))
                                    ctx.wait_mark(t, "p2_ev", ("spe", h, i, k))
                                inst = t.matmul(psA[:], Ssb[:, k % 2, :],
                                                ysb[:, k % 2, :],
                                                start=True, stop=True)
                                ctx.inc(inst, "p2_mm", 1)
                                ctx.mark("p2_mm", ("ap", h, i, k))
                        else:
                            inst = t.matmul(psA[:], kh[:, isl], qh[:, isl],
                                            start=True, stop=True)
                            ctx.inc(inst, "p2_mm", 1)
                            ctx.mark("p2_mm", ("ed2", h, i))
                            ctx.wait_mark(t, "p2_ex", ("xd2", h, i))
                            inst = t.matmul(psM[:, 0:129], ed2sb[:],
                                            usb[:, hp, i, 0:129],
                                            start=first_mm2, stop=True)
                            ctx.inc(inst, "p2_mm", 1)
                            ctx.mark("p2_mm", ("m2", h, i))
                            ctx.wait_mark(t, "p2_ev", ("obn", h, i))
                            inst = t.transpose(psT[:], osb[:], idbsb[:])
                            ctx.inc(inst, "p2_mm", 1)
                            ctx.mark("p2_mm", ("ot", h, i))


        @blk.scalar
        def _(s):
            ctx.wait_mark(s, "p2_ev", "init")
            def group_exps(s, h, i, phase):
                for (g, _h, _i, _ph, gl) in by_hi.get((h, i, phase), []):
                    pe = psE[g % 2]
                    ctx.wait_mark(s, "p2_mm", ("e", g))
                    if g >= 2:
                        ctx.wait_mark(s, "p2_mm", ("m", g - 2))
                    inst = s.activation(
                        out=esb[:, g % 2, 0:len(gl) * 128],
                        in_=pe[:, 0:len(gl) * 128],
                        func=AF.Exp, scale=SC0)
                    ctx.inc(inst, "p2_ex", 1)
                    ctx.mark("p2_ex", ("x", g))

            def diag_exp(s, h, i, tag, ztag):
                ctx.wait_mark(s, "p2_ev", (ztag, h, i))
                if tag == "xd" and i > 0:
                    ctx.wait_mark(s, "p2_mm", ("edt", h, i - 1))
                if tag == "xd2" and i > 0:
                    ctx.wait_mark(s, "p2_mm", ("m2", h, i - 1))
                dst = edsb if tag == "xd" else ed2sb
                inst = s.activation(out=dst[:], in_=zsb[:],
                                    func=AF.Exp, scale=SC0)
                ctx.inc(inst, "p2_ex", 1)
                ctx.mark("p2_ex", (tag, h, i))

            for h in range(P2_NH):
                for i in range(P2_NS):
                    group_exps(s, h, i, 0)
                    diag_exp(s, h, i, "xd", "zd")
                    group_exps(s, h, i, 1)
                    diag_exp(s, h, i, "xd2", "zd2")

        @blk.vector
        def _(v):
            inst = v.memset(zosb[:, :], 0.0)
            ctx.inc(inst, "p2_vs", 1)
            v.wait_ge(ctx.sem("p2_vs"), ctx.cnt["p2_vs"])
            inst = v.memset(zosb[:, 128:132], 1.0)
            ctx.inc(inst, "p2_ev", 1)
            ctx.mark("p2_ev", "init")
            v.wait_ge(ctx.sem("p2_ld"), 64)
            ot_ctr = 0
            for h in range(P2_NH):
                hp = h % 2
                if h >= 2:
                    ctx.wait_mark(v, "p2_mm", ("ot", h - 2, P2_NS - 1))
                inst = v.memset(usb[:, hp, :, 128:132], 1.0)
                ctx.inc(inst, "p2_ev", 1)
                ctx.mark("p2_ev", ("uones", h))
                v.wait_ge(ctx.sem(f"p2_ldv{hp}"), 16 * (h // 2 + 1))
                v.wait_ge(ctx.sem(f"p2_ldb{hp}"), 16 * (h // 2 + 1))
                for i in range(P2_NS):
                    isl = slice(i * 128, (i + 1) * 128)
                    ctx.wait_mark(v, "p2_mm", ("ed", h, i))
                    if i > 0:
                        ctx.wait_mark(v, "p2_ex", ("xd2", h, i - 1))
                    inst = v.tensor_sub(zsb[:], psA[:], dstrsb[:])
                    ctx.inc(inst, "p2_ev", 1)
                    ctx.mark("p2_ev", ("zd", h, i))
                    ctx.wait_mark(v, "p2_mm", ("md", h, i))
                    if P2_DEBUG and h == 0 and i == P2_DBG_I:
                        v.tensor_copy(dbgT[:], psT[:])
                    inst = v.tensor_scalar_max(densb[:, 1:2],
                                               psM[:, 128:129], 1e-30)
                    ctx.inc(inst, "p2_vs", 1)
                    v.wait_ge(ctx.sem("p2_vs"), ctx.cnt["p2_vs"])
                    inst = v.reciprocal(densb[:, 2:3], densb[:, 1:2])
                    ctx.inc(inst, "p2_vs", 1)
                    v.wait_ge(ctx.sem("p2_vs"), ctx.cnt["p2_vs"])
                    v.tensor_scalar(densb[:, 3:4], densb[:, 2:3],
                                    bTsb[:, i, h:h + 1], -1.0,
                                    ALU.mult, ALU.mult)
                    inst = v.tensor_scalar(densb[:, 0:1], densb[:, 2:3],
                                           bTsb[:, i, h:h + 1], None,
                                           ALU.mult)
                    ctx.inc(inst, "p2_ev", 1)
                    ctx.mark("p2_ev", ("sc2", h, i))
                    ctx.wait_mark(v, "p2_mm", ("edt", h, i))
                    ctx.wait_mark(v, "p2_ev", ("sc2", h, i))
                    if P2_DEBUG and h == 0 and i == P2_DBG_I:
                        v.tensor_copy(dbgD[:, 0:4], densb[:])
                    inst = v.tensor_scalar(Psb[:, 1, :], psT[:],
                                           densb[:, 0:1], None, ALU.mult)
                    ctx.inc(inst, "p2_ev", 1)
                    ctx.mark("p2_ev", ("pb", h, i))
                    ctx.wait_mark(v, "p2_mm", ("ntr", h, i))
                    inst = v.tensor_copy(nsb[:], psT[:])
                    ctx.inc(inst, "p2_ev", 1)
                    ctx.mark("p2_ev", ("nb", h, i))
                    ctx.wait_mark(v, "p2_ev", ("uones", h))
                    lm0 = last_m[(h, i, 0)]
                    if lm0 is not None:
                        ctx.wait_mark(v, "p2_ev", ("sc2", h, i))
                        inst = v.scalar_tensor_tensor(
                            ysb[:, 0, :], psM[:, 0:128], densb[:, 3:4],
                            vsb[:, hp, i, 0:128], ALU.mult, ALU.add)
                    else:
                        inst = v.tensor_copy(ysb[:, 0, :],
                                             vsb[:, hp, i, 0:128])
                    ctx.inc(inst, "p2_ev", 1)
                    ctx.mark("p2_ev", ("y0", h, i))
                    ctx.wait_mark(v, "p2_mm", ("ap", h, i, 0))
                    inst = v.tensor_sub(ysb[:, 1, :], ysb[:, 0, :], psA[:])
                    ctx.inc(inst, "p2_ev", 1)
                    ctx.mark("p2_ev", ("yk", h, i, 0))
                    for k in range(1, 7):
                        ctx.wait_mark(v, "p2_mm", ("sq", h, i, k))
                        inst = v.tensor_copy(Ssb[:, k % 2, :], psA[:])
                        ctx.inc(inst, "p2_ev", 1)
                        ctx.mark("p2_ev", ("sqe", h, i, k))
                        if k < 6:
                            ctx.wait_mark(v, "p2_mm", ("sqp", h, i, k))
                            inst = v.tensor_copy(Psb[:, k % 2, :], psA[:])
                            ctx.inc(inst, "p2_ev", 1)
                            ctx.mark("p2_ev", ("spe", h, i, k))
                        ctx.wait_mark(v, "p2_mm", ("ap", h, i, k))
                        dst = (usb[:, hp, i, 0:128] if k == 6
                               else ysb[:, (k + 1) % 2, :])
                        inst = v.tensor_add(dst, ysb[:, k % 2, :], psA[:])
                        ctx.inc(inst, "p2_ev", 1)
                        ctx.mark("p2_ev", ("yk", h, i, k))
                    ctx.wait_mark(v, "p2_mm", ("ed2", h, i))
                    ctx.wait_mark(v, "p2_ex", ("xd", h, i))
                    inst = v.tensor_sub(zsb[:], psA[:], dincsb[:])
                    ctx.inc(inst, "p2_ev", 1)
                    ctx.mark("p2_ev", ("zd2", h, i))
                    ctx.wait_mark(v, "p2_mm", ("m2", h, i))
                    if ot_ctr >= 2:
                        ctx.wait(v, f"p2_st{ot_ctr % 2}",
                                 at=16 * (ot_ctr // 2))
                    inst = v.reciprocal(densb[:, 2:3], psM[:, 128:129])
                    ctx.inc(inst, "p2_vs", 1)
                    v.wait_ge(ctx.sem("p2_vs"), ctx.cnt["p2_vs"])
                    inst = v.tensor_scalar(osb[:], psM[:, 0:128],
                                           densb[:, 2:3], None, ALU.mult)
                    ctx.inc(inst, "p2_ev", 1)
                    ctx.mark("p2_ev", ("ob", h, i))
                    ctx.mark("p2_ev", ("obn", h, i))
                    ctx.wait_mark(v, "p2_mm", ("ot", h, i))
                    inst = v.tensor_copy(otsb[:, ot_ctr % 2, :], psT[:])
                    ctx.inc(inst, "p2_ev", 1)
                    ctx.mark("p2_ev", ("oe", h, i))
                    ot_ctr += 1

        @blk.gpsimd
        def _(g):
            ctx.inc(g.dma_start(out=idbsb[:], in_=ident[:, :]), "p2_gld", 16)
            ctx.mark("p2_gld", "done")
            if P2_DEBUG:
                dbg = io["dbg_s"]
                ctx.wait_mark(g, "p2_ev", ("oe", P2_NH - 1, P2_NS - 1))
                for a, (dsl, srcap) in enumerate([
                        (slice(0, 132), usb[:, 0, 0, :]),
                        (slice(132, 264), usb[:, 0, 1, :]),
                        (slice(264, 268), densb[:]),
                        (slice(268, 396), edsb[:]),
                        (slice(396, 524), nsb[:]),
                        (slice(524, 652), ysb[:, 0, :]),
                        (slice(652, 780), ed2sb[:]),
                        (slice(1036, 1164), Psb[:, 1, :]),
                        (slice(1164, 1292), Ssb[:, 0, :]),
                        (slice(1292, 1420), Ssb[:, 1, :]),
                        (slice(1420, 1548), ysb[:, 1, :]),
                        (slice(1548, 1676), Psb[:, 0, :]),
                        (slice(1676, 1804), dbgT[:]),
                        (slice(1804, 1812), dbgD[:]),
                ]):
                    ctx.inc(g.dma_start(out=dbg[:, dsl], in_=srcap),
                            "p2_gld", 16)
                ctx.mark("p2_gld", "dbg")

        @blk.sync
        def _(s):
            ctx.inc(s.dma_start(out=idsb[:], in_=ident[:, :]), "p2_ld", 16)
            ctx.inc(s.dma_start(out=dstrsb[:], in_=dstrict[:, :]), "p2_ld", 16)
            ctx.inc(s.dma_start(out=dincsb[:], in_=dincl[:, :]), "p2_ld", 16)
            ctx.inc(s.dma_start(
                out=bTsb[:, :, :],
                in_=betaT_s.rearrange("(s p) e -> p s e", p=128)[:, :, :]),
                "p2_ld", 16)
            ot_ctr = 0
            for h in range(P2_NH):
                hp = h % 2
                if h >= 2:
                    ctx.wait_mark(s, "p2_mm", ("ot", h - 2, P2_NS - 1))
                nld = 16 * (h // 2 + 1)
                ctx.inc(s.dma_start(out=ksb[:, hp, :],
                                    in_=kT_s[h * 128:(h + 1) * 128, :]),
                        f"p2_ldk{hp}", 16)
                ctx.inc(s.dma_start(out=qsb[:, hp, :],
                                    in_=qT_s[h * 128:(h + 1) * 128, :]),
                        f"p2_ldq{hp}", 16)
                vr = v_s[:, h, :].rearrange("(s p) w -> p s w", p=128)
                ctx.inc(s.dma_start(out=vsb[:, hp, :, :], in_=vr[:, :, :]),
                        f"p2_ldv{hp}", 16)
                ctx.inc(s.dma_start(out=betarow[0:1, hp, :],
                                    in_=beta_s[h:h + 1, :]),
                        f"p2_ldb{hp}", 16)
                if h >= 1:
                    for i in range(P2_NS):
                        ctx.wait_mark(s, "p2_ev", ("oe", h - 1, i))
                        ctx.inc(s.dma_start(
                            out=oT0_s[(h - 1) * 128:h * 128,
                                      i * 128:(i + 1) * 128],
                            in_=otsb[:, ot_ctr % 2, :]),
                            f"p2_st{ot_ctr % 2}", 16)
                        ot_ctr += 1
            for i in range(P2_NS):
                ctx.wait_mark(s, "p2_ev", ("oe", P2_NH - 1, i))
                ctx.inc(s.dma_start(
                    out=oT0_s[(P2_NH - 1) * 128:P2_NH * 128,
                              i * 128:(i + 1) * 128],
                    in_=otsb[:, ot_ctr % 2, :]),
                    f"p2_st{ot_ctr % 2}", 16)
                ot_ctr += 1
            ctx.wait(s, "p2_st0")
            ctx.wait(s, "p2_st1")
            if P2_DEBUG:
                ctx.wait_mark(s, "p2_gld", "dbg")



def emit_post(nc, ctx, io, cfg):
    P = cfg["pref"]
    oT = io.get(cfg.get("oT"))
    w = io.get(cfg.get("w"))
    bias = io.get(cfg.get("bias"))
    resid = io[cfg["resid"]]
    pre = io.get(cfg.get("pre"))
    out_T = io.get(cfg.get("out_T"))
    ident = io["ident"]
    NTS = NT // 128

    from contextlib import ExitStack
    with ExitStack() as _st:
        if oT is not None:
            wsb = _st.enter_context(nc.sbuf_tensor(P + "w", [128, 16, DM], BF16))
            osb = _st.enter_context(nc.sbuf_tensor(P + "o", [128, 16, NT], BF16))
        xr = _st.enter_context(nc.sbuf_tensor(P + "xr", [128, 2, DM], F32))
        xp = (_st.enter_context(nc.sbuf_tensor(P + "xp", [128, 2, DM], F32))
              if pre is not None else None)
        xsum = _st.enter_context(nc.sbuf_tensor(P + "xs", [128, 2, DM], F32))
        sqalias = cfg.get("sqalias", False)
        sq = (None if sqalias else
              _st.enter_context(nc.sbuf_tensor(P + "sq", [128, DM], F32)))
        stat = _st.enter_context(nc.sbuf_tensor(P + "stt", [128, 8], F32))
        x1f = _st.enter_context(nc.sbuf_tensor(P + "f", [128, 2, DM], F32))
        x1b = _st.enter_context(nc.sbuf_tensor(P + "b", [128, 2, DM], BF16))
        xTb = _st.enter_context(nc.sbuf_tensor(P + "tbb", [128, 2, 16, 128],
                                               BF16))
        idb = _st.enter_context(nc.sbuf_tensor(P + "id", [128, 128], BF16))
        on1 = (_st.enter_context(nc.sbuf_tensor(P + "on", [1, 128], F32))
               if bias is not None else None)
        br = (_st.enter_context(nc.sbuf_tensor(P + "brr", [1, DM], F32))
              if bias is not None else None)
        ps0 = _st.enter_context(nc.psum_tensor(P + "p0", [128, 512], F32))
        ps1 = _st.enter_context(nc.psum_tensor(P + "p1", [128, 512], F32))
        psT0 = _st.enter_context(nc.psum_tensor(P + "t0", [128, 128], BF16))
        psT1 = _st.enter_context(nc.psum_tensor(P + "t1", [128, 128], BF16))
        psB = (_st.enter_context(nc.psum_tensor(P + "pbb", [128, DM], F32))
               if bias is not None else None)
        blk = _st.enter_context(nc.Block(P))
        psAB = (ps0, ps1)
        psTT = (psT0, psT1)

        def store_ts(s, ts):
            par = ts % 2
            ctx.wait_mark(s, P + "ev", ("lnf", ts))
            ctx.inc(s.dma_start(out=io[cfg["out_f32"]][
                ts * 128:(ts + 1) * 128, :], in_=x1f[:, par, :]),
                P + f"s{par}", 16)
            ctx.mark(P + f"s{par}", ("sf", ts))
            if out_T is not None:
                ctx.wait_mark(s, P + "ev", ("tb", ts, 15))
                dstcol = out_T[:, ts * 128:(ts + 1) * 128].rearrange(
                    "(s p) t -> p s t", p=128)
                ctx.inc(s.dma_start(out=dstcol[:, :, :],
                                    in_=xTb[:, par, :, :]), P + f"q{par}", 16)
                ctx.mark(P + f"q{par}", ("qf", ts))

        @blk.gpsimd
        def _(g):
            ctx.inc(g.dma_start(out=idb[:], in_=ident[:, :]), P + "gld", 16)
            if oT is not None:
                for s_ in range(16):
                    ctx.inc(g.dma_start(out=wsb[:, s_, :],
                                        in_=w[s_ * 128:(s_ + 1) * 128, :]),
                            P + "gld", 16)
            ctx.mark(P + "gld", "done")

        @blk.sync
        def _(s):
            if bias is not None:
                ctx.inc(s.dma_start(out=br[:], in_=bias[:, :]), P + "c", 16)
            if oT is not None:
                for ds in range(16):
                    ctx.inc(s.dma_start(out=osb[:, ds, :],
                                        in_=oT[ds * 128:(ds + 1) * 128, :]),
                            P + "c", 16)
            ctx.mark(P + "c", "done")
            for ts in range(NTS):
                par = ts % 2
                if ts >= 2:
                    ctx.wait_mark(s, P + "ev", ("lnf", ts - 2))
                ctx.inc(s.dma_start(out=xr[:, par, :],
                                    in_=resid[ts * 128:(ts + 1) * 128, :]),
                        P + f"x{par}", 16)
                if pre is not None:
                    ctx.inc(s.dma_start(out=xp[:, par, :],
                                        in_=pre[ts * 128:(ts + 1) * 128, :]),
                            P + f"pp{par}", 16)
                if ts >= 1:
                    store_ts(s, ts - 1)
            store_ts(s, NTS - 1)
            for nm in (P + "s0", P + "s1") + (
                    (P + "q0", P + "q1") if out_T is not None else ()):
                ctx.wait(s, nm)

        @blk.tensor
        def _(t):
            ctx.wait_mark(t, P + "gld", "done")
            ctx.wait_mark(t, P + "c", "done")
            if bias is not None:
                ctx.wait_mark(t, P + "ev", "on1")
                for mc in range(4):
                    inst = t.matmul(psB[:, mc * 512:(mc + 1) * 512],
                                    on1[0:1, :],
                                    br[0:1, mc * 512:(mc + 1) * 512],
                                    start=True, stop=True)
                    ctx.inc(inst, P + "mm", 1)
                    ctx.mark(P + "mm", ("ob", mc))
            for ts in range(NTS):
                if oT is not None:
                    for mc in range(4):
                        j = ts * 4 + mc
                        ps = psAB[j % 2]
                        if j >= 2:
                            ctx.wait_mark(t, P + "ev",
                                          ("ev", (j - 2) // 4, (j - 2) % 4))
                        for ds in range(16):
                            inst = t.matmul(
                                ps[:], osb[:, ds, ts * 128:(ts + 1) * 128],
                                wsb[:, ds, mc * 512:(mc + 1) * 512],
                                start=(ds == 0), stop=(ds == 15))
                        ctx.inc(inst, P + "mm", 1)
                        ctx.mark(P + "mm", ("mm", ts, mc))
                ctx.wait_mark(t, P + "ev", ("lnb", ts))
                for cs in range(16):
                    j2 = ts * 16 + cs
                    if j2 >= 2:
                        ctx.wait_mark(t, P + "ev", ("tb", (j2 - 2) // 16,
                                                    (j2 - 2) % 16))
                    inst = t.transpose(psTT[j2 % 2][:],
                                       x1b[:, ts % 2, cs * 128:(cs + 1) * 128],
                                       idb[:])
                    ctx.inc(inst, P + "mm", 1)
                    ctx.mark(P + "mm", ("tp", ts, cs))

        @blk.vector
        def _(v):
            def vsync(inst):
                ctx.inc(inst, P + "vs", 1)
                v.wait_ge(ctx.sem(P + "vs"), ctx.cnt[P + "vs"])
            if bias is not None:
                inst = v.memset(on1[:], 1.0)
                ctx.inc(inst, P + "ev", 1)
                ctx.mark(P + "ev", "on1")
            for ts in range(NTS):
                par = ts % 2
                v.wait_ge(ctx.sem(P + f"x{par}"), 16 * (ts // 2 + 1))
                if pre is not None:
                    v.wait_ge(ctx.sem(P + f"pp{par}"), 16 * (ts // 2 + 1))
                for mc in range(4):
                    csl = slice(mc * 512, (mc + 1) * 512)
                    if ts >= 2 and mc == 0:
                        ctx.wait_mark(v, P + "ex", ("sq", ts - 2))
                    if oT is not None:
                        j = ts * 4 + mc
                        ctx.wait_mark(v, P + "mm", ("mm", ts, mc))
                        if bias is not None:
                            vsync(v.tensor_add(xsum[:, par, csl],
                                               psAB[j % 2][:],
                                               xr[:, par, csl]))
                            inst = v.tensor_add(xsum[:, par, csl],
                                                xsum[:, par, csl],
                                                psB[:, csl])
                        else:
                            inst = v.tensor_add(xsum[:, par, csl],
                                                psAB[j % 2][:],
                                                xr[:, par, csl])
                    else:
                        inst = v.tensor_add(xsum[:, par, csl],
                                            xp[:, par, csl], xr[:, par, csl])
                    ctx.inc(inst, P + "ev", 1)
                    ctx.mark(P + "ev", ("ev", ts, mc))
                ctx.wait(v, P + "ev")
                vsync(v.reduce_sum(stat[:, 0:1], xsum[:, par, :], axis=AX))
                if ts >= 1:
                    ctx.wait_mark(v, P + "ex", ("sq", ts - 1))
                v.tensor_scalar(stat[:, 1:2], stat[:, 0:1], 1.0 / DM, None,
                                ALU.mult)
                inst = v.tensor_scalar(stat[:, 2:3], stat[:, 0:1],
                                       -1.0 / DM, None, ALU.mult)
                ctx.inc(inst, P + "ev", 1)
                ctx.mark(P + "ev", ("mn", ts))
                ctx.wait_mark(v, P + "ex", ("sq", ts))
                inst = v.tensor_scalar(stat[:, 4:5], stat[:, 3:4], 1.0 / DM,
                                       EPS, ALU.mult, ALU.add)
                ctx.inc(inst, P + "ev", 1)
                ctx.mark(P + "ev", ("veps", ts))
                ctx.wait_mark(v, P + "ex", ("sd", ts))
                vsync(v.reciprocal(stat[:, 6:7], stat[:, 5:6]))
                if ts >= 2:
                    ctx.wait_mark(v, P + f"s{par}", ("sf", ts - 2))
                inst = v.tensor_scalar(x1f[:, par, :], xsum[:, par, :],
                                       stat[:, 1:2], stat[:, 6:7],
                                       ALU.subtract, ALU.mult)
                ctx.inc(inst, P + "ev", 1)
                ctx.mark(P + "ev", ("lnf", ts))
                if ts >= 2:
                    ctx.wait_mark(v, P + "mm", ("tp", ts - 2, 15))
                ctx.wait(v, P + "ev")
                inst = v.tensor_copy(x1b[:, par, :], x1f[:, par, :])
                ctx.inc(inst, P + "ev", 1)
                ctx.mark(P + "ev", ("lnb", ts))
                for cs in range(16):
                    ctx.wait_mark(v, P + "mm", ("tp", ts, cs))
                    if out_T is not None and ts >= 2 and cs == 0:
                        ctx.wait_mark(v, P + f"q{par}", ("qf", ts - 2))
                    inst = v.tensor_copy(xTb[:, par, cs, :],
                                         psTT[(ts * 16 + cs) % 2][:])
                    ctx.inc(inst, P + "ev", 1)
                    ctx.mark(P + "ev", ("tb", ts, cs))

        @blk.scalar
        def _(s):
            for ts in range(NTS):
                par = ts % 2
                ctx.wait_mark(s, P + "ev", ("mn", ts))
                if sqalias:
                    if ts >= 2:
                        ctx.wait_mark(s, P + f"s{par}", ("sf", ts - 2))
                    sqout = x1f[:, par, :]
                else:
                    sqout = sq[:]
                inst = s.activation(out=sqout, in_=xsum[:, par, :],
                                    func=AF.Square, bias=stat[:, 2:3],
                                    accum_out=stat[:, 3:4])
                ctx.inc(inst, P + "ex", 1)
                ctx.mark(P + "ex", ("sq", ts))
                ctx.wait_mark(s, P + "ev", ("veps", ts))
                inst = s.activation(out=stat[:, 5:6], in_=stat[:, 4:5],
                                    func=AF.Sqrt)
                ctx.inc(inst, P + "ex", 1)
                ctx.mark(P + "ex", ("sd", ts))



def p4_block_schedule():
    blocks = []
    for j in range(22):
        blocks.append(("gw", HS + j * 128, j * 128, j, None))
    for e in range(NEXP):
        for j in range(11):
            blocks.append(("up", j * 128, HF + j * 128, 22 + e * 11 + j, e))
    return blocks


def stage_wblk(inp):
    gwT = inp["L0_ffn_shared_gw"].T
    upT = inp["L0_ffn_up_w"].transpose(0, 2, 1)
    out = np.empty((128, NHS, 2, 16, 128), np.float32)
    for (wsrc, gcol, ucol, hsidx, e) in p4_block_schedule():
        src = gwT if wsrc == "gw" else upT[e]
        for gi, col in ((0, gcol), (1, ucol)):
            t = src[:, col:col + 128].reshape(16, 128, 128)
            out[:, hsidx, gi, :, :] = t.transpose(1, 0, 2)
    return np.ascontiguousarray(out)


def emit_p4(nc, ctx, io, ten):
    x1T, wblk = io["x1T_s"], io["wblk"]
    gateT, gbias = io["gateT"], io["gbias"]
    iota8, iotaB8, ident = io["iota8"], io["iotaB8"], io["ident"]
    hacts = io["hacts_s"]
    NTS = NT // 128

    blocks = p4_block_schedule()
    if P4_LIMIT is not None:
        blocks = [b for b in blocks if P4_LIMIT(b)]
    presentE = sorted({b[4] for b in blocks if b[4] is not None})
    eidx = {e: i for i, e in enumerate(presentE)}

    from contextlib import ExitStack
    with ExitStack() as _st:
        xsb = _st.enter_context(nc.sbuf_tensor("p4x", [128, 16, NT], BF16))
        gsb = _st.enter_context(nc.sbuf_tensor("p4g", [128, 16, 8], BF16))
        wg = _st.enter_context(nc.sbuf_tensor("p4wg", [128, 2, 16, 128], BF16))
        wu = _st.enter_context(nc.sbuf_tensor("p4wu", [128, 2, 16, 128], BF16))
        idsb = _st.enter_context(nc.sbuf_tensor("p4id", [128, 128], F32))
        on1 = _st.enter_context(nc.sbuf_tensor("p4on", [1, 128], F32))
        gbrow = _st.enter_context(nc.sbuf_tensor("p4gb", [1, 8], F32))
        iorow = _st.enter_context(nc.sbuf_tensor("p4io", [1, 16], F32))
        scoT = _st.enter_context(nc.sbuf_tensor("p4sc", [8, NT], F32))
        weTt = _st.enter_context(nc.sbuf_tensor("p4we", [8, 2, 128], F32))
        werow = _st.enter_context(nc.sbuf_tensor("p4wr", [1, 2, NT], F32))
        we_d = nc.dram_tensor("p4we_d", [8, NT], F32, kind="Internal").ap()
        gbb = _st.enter_context(nc.sbuf_tensor("p4gbb", [128, 8], F32))
        iob = _st.enter_context(nc.sbuf_tensor("p4iob", [128, 16], F32))
        rt = _st.enter_context(nc.sbuf_tensor("p4rt", [128, 12, 8], F32))
        rst = _st.enter_context(nc.sbuf_tensor("p4rst", [128, 8], F32))
        webc = _st.enter_context(nc.sbuf_tensor("p4wb", [128, NT], F32))
        t1 = _st.enter_context(nc.sbuf_tensor("p4t1", [128, 2, 512], F32))
        t2 = _st.enter_context(nc.sbuf_tensor("p4t2", [128, 2, 512], F32))
        hbuf = _st.enter_context(nc.sbuf_tensor("p4h", [128, 2, 512], BF16))
        psG0 = _st.enter_context(nc.psum_tensor("p4pg0", [128, 512], F32))
        psG1 = _st.enter_context(nc.psum_tensor("p4pg1", [128, 512], F32))
        psU0 = _st.enter_context(nc.psum_tensor("p4pu0", [128, 512], F32))
        psU1 = _st.enter_context(nc.psum_tensor("p4pu1", [128, 512], F32))
        psL = _st.enter_context(nc.psum_tensor("p4pl", [8, 512], F32))
        psR = _st.enter_context(nc.psum_tensor("p4pr", [128, 16], F32))
        psR2 = _st.enter_context(nc.psum_tensor("p4pr2", [8, 128], F32))
        psW = _st.enter_context(nc.psum_tensor("p4pw", [128, 512], F32))
        blk = _st.enter_context(nc.Block("p4"))
        psG = (psG0, psG1)
        psU = (psU0, psU1)

        @blk.gpsimd
        def _(g):
            for s_ in range(16):
                ctx.inc(g.dma_start(out=gsb[:, s_, :],
                                    in_=gateT[s_ * 128:(s_ + 1) * 128, :]),
                        "p4gld", 16)
            ctx.mark("p4gld", "gate")
            for bi, (wsrc, gcol, ucol, hsidx, e) in enumerate(blocks):
                par = bi % 2
                if bi >= 2:
                    ctx.wait_mark(g, "p4mm", ("u", bi - 2, 3))
                ctx.inc(g.dma_start(out=wg[:, par, :, :],
                                    in_=wblk[:, hsidx, 0, :, :]),
                        f"p4w{par}", 16)
                ctx.inc(g.dma_start(out=wu[:, par, :, :],
                                    in_=wblk[:, hsidx, 1, :, :]),
                        f"p4w{par}", 16)

        @blk.sync
        def _(s):
            ctx.inc(s.dma_start(out=idsb[:], in_=ident[:, :]), "p4c", 16)
            ctx.inc(s.dma_start(out=gbrow[:], in_=gbias[:, :]), "p4c", 16)
            ctx.inc(s.dma_start(out=iorow[0:1, 0:8], in_=iota8[:, :]),
                    "p4c", 16)
            ctx.inc(s.dma_start(out=iorow[0:1, 8:16], in_=iotaB8[:, :]),
                    "p4c", 16)
            for ds in range(16):
                ctx.inc(s.dma_start(out=xsb[:, ds, :],
                                    in_=x1T[ds * 128:(ds + 1) * 128, :]),
                        "p4c", 16)
            ctx.mark("p4c", "done")
            for ts in range(NTS):
                ctx.wait_mark(s, "p4ev", ("rt1", ts))
                ctx.inc(s.dma_start(out=we_d[:, ts * 128:(ts + 1) * 128],
                                    in_=weTt[:, ts % 2, :]),
                        f"p4wd{ts % 2}", 16)
            ctx.wait(s, "p4wd0")
            ctx.wait(s, "p4wd1")
            for i_, e in enumerate(presentE):
                if i_ >= 2:
                    ctx.wait_mark(s, "p4mm", ("wb", presentE[i_ - 2], 3))
                ctx.inc(s.dma_start(out=werow[0:1, i_ % 2, :],
                                    in_=we_d[e:e + 1, :]),
                        f"p4wr{i_ % 2}", 16)
            for bi in range(len(blocks)):
                hsidx = blocks[bi][3]
                for tc in range(4):
                    j = bi * 4 + tc
                    ctx.wait_mark(s, "p4ev", ("hb", bi, tc))
                    ctx.inc(s.dma_start(
                        out=hacts[hsidx, :, tc * 512:(tc + 1) * 512],
                        in_=hbuf[:, j % 2, :]), f"p4s{j % 2}", 16)
                    ctx.mark(f"p4s{j % 2}", ("st", bi, tc))
            ctx.wait(s, "p4s0")
            ctx.wait(s, "p4s1")

        @blk.tensor
        def _(t):
            ctx.wait_mark(t, "p4c", "done")
            ctx.wait_mark(t, "p4gld", "gate")
            ctx.wait_mark(t, "p4ev", "on1")
            for tc in range(4):
                if tc >= 1:
                    ctx.wait_mark(t, "p4ex", ("sg", tc - 1))
                for s_ in range(16):
                    inst = t.matmul(psL[:], gsb[:, s_, :],
                                    xsb[:, s_, tc * 512:(tc + 1) * 512],
                                    start=(s_ == 0), stop=(s_ == 15))
                ctx.inc(inst, "p4mm", 1)
                ctx.mark("p4mm", ("lg", tc))
            inst = t.matmul(psR[:, 0:8], on1[0:1, :], gbrow[0:1, :],
                            start=True, stop=True)
            ctx.inc(inst, "p4mm", 1)
            ctx.mark("p4mm", ("gbb",))
            ctx.wait_mark(t, "p4ev", ("gbbc",))
            inst = t.matmul(psR[:], on1[0:1, :], iorow[0:1, 0:16],
                            start=True, stop=True)
            ctx.inc(inst, "p4mm", 1)
            ctx.mark("p4mm", ("iob",))
            for ts in range(NTS):
                ctx.wait_mark(t, "p4ex", ("sg", 3))
                if ts >= 1:
                    ctx.wait_mark(t, "p4ev", ("rt1", ts - 1))
                else:
                    ctx.wait_mark(t, "p4ev", ("iobc",))
                inst = t.transpose(psR[:, 0:8],
                                   scoT[0:8, ts * 128:(ts + 1) * 128],
                                   idsb[0:8, 0:8])
                ctx.inc(inst, "p4mm", 1)
                ctx.mark("p4mm", ("rtp", ts))
                ctx.wait_mark(t, "p4ev", ("wet", ts))
                inst = t.transpose(psR2[:], rt[:, 11, :], idsb[:])
                ctx.inc(inst, "p4mm", 1)
                ctx.mark("p4mm", ("rtp2", ts))
            lastex = None
            for bi, (wsrc, gcol, ucol, hsidx, e) in enumerate(blocks):
                par = bi % 2
                if e is not None and (bi == 0 or blocks[bi - 1][4] != e):
                    i_ = eidx[e]
                    t.wait_ge(ctx.sem(f"p4wr{i_ % 2}"), 16 * (i_ // 2 + 1))
                    for tc in range(4):
                        if not (eidx[e] == 0 and tc == 0):
                            ctx.wait_mark(t, "p4ev", ("wbc", e, tc - 1)
                                          if tc >= 1 else
                                          ("wbc", presentE[eidx[e] - 1], 3))
                        inst = t.matmul(
                            psW[:], on1[0:1, :],
                            werow[0:1, eidx[e] % 2, tc * 512:(tc + 1) * 512],
                            start=True, stop=True)
                        ctx.inc(inst, "p4mm", 1)
                        ctx.mark("p4mm", ("wb", e, tc))
                t.wait_ge(ctx.sem(f"p4w{par}"), 32 * (bi // 2 + 1))
                for tc in range(4):
                    j = bi * 4 + tc
                    if j >= 2:
                        ctx.wait_mark(t, "p4ev", ("hb", (j - 2) // 4,
                                                  (j - 2) % 4))
                    for s_ in range(16):
                        inst = t.matmul(psG[j % 2][:], wg[:, par, s_, :],
                                        xsb[:, s_, tc * 512:(tc + 1) * 512],
                                        start=(s_ == 0), stop=(s_ == 15))
                    ctx.inc(inst, "p4mm", 1)
                    ctx.mark("p4mm", ("g", bi, tc))
                    for s_ in range(16):
                        inst = t.matmul(psU[j % 2][:], wu[:, par, s_, :],
                                        xsb[:, s_, tc * 512:(tc + 1) * 512],
                                        start=(s_ == 0), stop=(s_ == 15))
                    ctx.inc(inst, "p4mm", 1)
                    ctx.mark("p4mm", ("u", bi, tc))

        @blk.scalar
        def _(s):
            for tc in range(4):
                ctx.wait_mark(s, "p4mm", ("lg", tc))
                inst = s.activation(out=scoT[0:8, tc * 512:(tc + 1) * 512],
                                    in_=psL[:], func=AF.Sigmoid)
                ctx.inc(inst, "p4ex", 1)
                ctx.mark("p4ex", ("sg", tc))
            for bi in range(len(blocks)):
                for tc in range(4):
                    j = bi * 4 + tc
                    ctx.wait_mark(s, "p4mm", ("g", bi, tc))
                    if j >= 2:
                        ctx.wait_mark(s, "p4ev", ("t2", (j - 2) // 4,
                                                  (j - 2) % 4))
                    inst = s.activation(out=t1[:, j % 2, :], in_=psG[j % 2][:],
                                        func=AF.Silu)
                    ctx.inc(inst, "p4ex", 1)
                    ctx.mark("p4ex", ("sl", bi, tc))

        @blk.vector
        def _(v):
            def vsync(inst):
                ctx.inc(inst, "p4vs", 1)
                v.wait_ge(ctx.sem("p4vs"), ctx.cnt["p4vs"])
            inst = v.memset(on1[:], 1.0)
            ctx.inc(inst, "p4ev", 1)
            ctx.mark("p4ev", "on1")
            ctx.wait_mark(v, "p4mm", ("gbb",))
            inst = v.tensor_copy(gbb[:], psR[:, 0:8])
            ctx.inc(inst, "p4ev", 1)
            ctx.mark("p4ev", ("gbbc",))
            ctx.wait_mark(v, "p4mm", ("iob",))
            inst = v.tensor_copy(iob[:], psR[:, 0:16])
            ctx.inc(inst, "p4ev", 1)
            ctx.mark("p4ev", ("iobc",))
            for ts in range(NTS):
                ctx.wait_mark(v, "p4mm", ("rtp", ts))
                vsync(v.tensor_copy(rt[:, 0, :], psR[:, 0:8]))
                vsync(v.tensor_add(rt[:, 1, :], rt[:, 0, :], gbb[:]))
                vsync(v.reduce_max(rst[:, 0:1], rt[:, 1, :], axis=AX))
                vsync(v.tensor_scalar(rt[:, 2, :], rt[:, 1, :],
                                      rst[:, 0:1], None, ALU.is_equal))
                vsync(v.scalar_tensor_tensor(rt[:, 3, :], rt[:, 2, :],
                                             -100.0, iob[:, 8:16],
                                             ALU.mult, ALU.add))
                vsync(v.tensor_reduce(rst[:, 1:2], rt[:, 3, :], AX,
                                      ALU.min))
                vsync(v.tensor_scalar(rt[:, 4, :], iob[:, 0:8],
                                      rst[:, 1:2], None, ALU.is_equal))
                vsync(v.scalar_tensor_tensor(rt[:, 5, :], rt[:, 4, :],
                                             -100.0, rt[:, 1, :],
                                             ALU.mult, ALU.add))
                vsync(v.reduce_max(rst[:, 2:3], rt[:, 5, :], axis=AX))
                vsync(v.tensor_scalar(rt[:, 6, :], rt[:, 5, :],
                                      rst[:, 2:3], None, ALU.is_equal))
                vsync(v.scalar_tensor_tensor(rt[:, 7, :], rt[:, 6, :],
                                             -100.0, iob[:, 8:16],
                                             ALU.mult, ALU.add))
                vsync(v.tensor_reduce(rst[:, 3:4], rt[:, 7, :], AX,
                                      ALU.min))
                vsync(v.tensor_scalar(rt[:, 8, :], iob[:, 0:8],
                                      rst[:, 3:4], None, ALU.is_equal))
                vsync(v.tensor_add(rt[:, 9, :], rt[:, 4, :], rt[:, 8, :]))
                vsync(v.tensor_mul(rt[:, 10, :], rt[:, 9, :], rt[:, 0, :]))
                vsync(v.reduce_sum(rst[:, 4:5], rt[:, 10, :], axis=AX))
                vsync(v.reciprocal(rst[:, 5:6], rst[:, 4:5]))
                inst = v.tensor_scalar(rt[:, 11, :], rt[:, 10, :],
                                       rst[:, 5:6], None, ALU.mult)
                ctx.inc(inst, "p4ev", 1)
                ctx.mark("p4ev", ("wet", ts))
                ctx.wait_mark(v, "p4mm", ("rtp2", ts))
                if ts >= 2:
                    ctx.wait(v, f"p4wd{ts % 2}", at=16 * (ts // 2))
                inst = v.tensor_copy(weTt[:, ts % 2, :], psR2[:])
                ctx.inc(inst, "p4ev", 1)
                ctx.mark("p4ev", ("rt1", ts))
            ctx.mark("p4ev", ("rte", NTS - 1))
            for bi, (wsrc, gcol, ucol, hsidx, e) in enumerate(blocks):
                if e is not None and (bi == 0 or blocks[bi - 1][4] != e):
                    for tc in range(4):
                        ctx.wait_mark(v, "p4mm", ("wb", e, tc))
                        inst = v.tensor_copy(
                            webc[:, tc * 512:(tc + 1) * 512], psW[:])
                        ctx.inc(inst, "p4ev", 1)
                        ctx.mark("p4ev", ("wbc", e, tc))
                for tc in range(4):
                    j = bi * 4 + tc
                    csl = slice(tc * 512, (tc + 1) * 512)
                    ctx.wait_mark(v, "p4ex", ("sl", bi, tc))
                    ctx.wait_mark(v, "p4mm", ("u", bi, tc))
                    if j >= 2:
                        ctx.wait_mark(v, f"p4s{j % 2}",
                                      ("st", (j - 2) // 4, (j - 2) % 4))
                    if e is None:
                        inst = v.tensor_mul(hbuf[:, j % 2, :], t1[:, j % 2, :],
                                            psU[j % 2][:])
                    else:
                        vsync(v.tensor_mul(t2[:, j % 2, :], t1[:, j % 2, :],
                                           psU[j % 2][:]))
                        inst = v.tensor_mul(hbuf[:, j % 2, :],
                                            t2[:, j % 2, :], webc[:, csl])
                    ctx.inc(inst, "p4ev", 1)
                    ctx.mark("p4ev", ("t2", bi, tc))
                    ctx.mark("p4ev", ("hb", bi, tc))



def emit_p5(nc, ctx, io, ten):
    hacts, sdwT, dnT = io["hacts_s"], io["sdwT"], io["dnT"]
    x2pre = io["x2pre_s"]
    NTS = NT // 128
    CC = 256
    NCC = DM // CC

    def wsrc(hs, cc):
        csl = slice(cc * CC, (cc + 1) * CC)
        if hs < 22:
            return sdwT[hs * 128:(hs + 1) * 128, csl]
        e, j = divmod(hs - 22, 11)
        return dnT[e, j * 128:(j + 1) * 128, csl]

    from contextlib import ExitStack
    with ExitStack() as _st:
        wdn = _st.enter_context(nc.sbuf_tensor("p5w", [128, 2, NHS, CC], BF16))
        hsb = _st.enter_context(nc.sbuf_tensor("p5h", [128, 2, NHS, 128], BF16))
        xb = _st.enter_context(nc.sbuf_tensor("p5x", [128, 2, CC], F32))
        ps0 = _st.enter_context(nc.psum_tensor("p5p0", [128, CC], F32))
        ps1 = _st.enter_context(nc.psum_tensor("p5p1", [128, CC], F32))
        blk = _st.enter_context(nc.Block("p5"))
        psAB = (ps0, ps1)

        @blk.gpsimd
        def _(g):
            for cc in range(NCC):
                par = cc % 2
                if cc >= 2:
                    ctx.wait_mark(g, "p5mm", ("ccdone", cc - 2))
                for hs in range(NHS):
                    ctx.inc(g.dma_start(out=wdn[:, par, hs, :],
                                        in_=wsrc(hs, cc)), f"p5w{par}", 16)

        @blk.sync
        def _(s):
            for cc in range(NCC):
                for ts in range(NTS):
                    gidx = cc * NTS + ts
                    par = gidx % 2
                    if gidx >= 2:
                        ctx.wait_mark(s, "p5mm", ("tsdone", gidx - 2))
                    hsl = hacts[:, :, ts * 128:(ts + 1) * 128]
                    ctx.inc(s.dma_start(
                        out=hsb[:, par, :, :],
                        in_=hsl.rearrange("h p t -> p h t")),
                        f"p5h{par}", 16)
                    if gidx >= 1:
                        g2 = gidx - 1
                        ctx.wait_mark(s, "p5ev", ("ev", g2))
                        cc2, ts2 = divmod(g2, NTS)
                        ctx.inc(s.dma_start(
                            out=x2pre[ts2 * 128:(ts2 + 1) * 128,
                                      cc2 * CC:(cc2 + 1) * CC],
                            in_=xb[:, g2 % 2, :]), f"p5s{g2 % 2}", 16)
                        ctx.mark(f"p5s{g2 % 2}", ("st", g2))
            g2 = NCC * NTS - 1
            ctx.wait_mark(s, "p5ev", ("ev", g2))
            cc2, ts2 = divmod(g2, NTS)
            ctx.inc(s.dma_start(
                out=x2pre[ts2 * 128:(ts2 + 1) * 128,
                          cc2 * CC:(cc2 + 1) * CC],
                in_=xb[:, g2 % 2, :]), f"p5s{g2 % 2}", 16)
            ctx.mark(f"p5s{g2 % 2}", ("st", g2))
            ctx.wait(s, "p5s0")
            ctx.wait(s, "p5s1")

        @blk.tensor
        def _(t):
            for cc in range(NCC):
                par = cc % 2
                t.wait_ge(ctx.sem(f"p5w{par}"), 16 * NHS * (cc // 2 + 1))
                for ts in range(NTS):
                    gidx = cc * NTS + ts
                    hp = gidx % 2
                    t.wait_ge(ctx.sem(f"p5h{hp}"), 16 * (gidx // 2 + 1))
                    if gidx >= 2:
                        ctx.wait_mark(t, "p5ev", ("ev", gidx - 2))
                    for hs in range(NHS):
                        inst = t.matmul(psAB[gidx % 2][:],
                                        hsb[:, hp, hs, :],
                                        wdn[:, par, hs, :],
                                        start=(hs == 0), stop=(hs == NHS - 1))
                    ctx.inc(inst, "p5mm", 1)
                    ctx.mark("p5mm", ("mm", gidx))
                    ctx.mark("p5mm", ("tsdone", gidx))
                ctx.mark("p5mm", ("ccdone", cc))

        @blk.vector
        def _(v):
            for cc in range(NCC):
                for ts in range(NTS):
                    gidx = cc * NTS + ts
                    ctx.wait_mark(v, "p5mm", ("mm", gidx))
                    if gidx >= 2:
                        ctx.wait_mark(v, f"p5s{gidx % 2}", ("st", gidx - 2))
                    inst = v.tensor_copy(xb[:, gidx % 2, :],
                                         psAB[gidx % 2][:])
                    ctx.inc(inst, "p5ev", 1)
                    ctx.mark("p5ev", ("ev", gidx))




def emit_p7(nc, ctx, io, ten):
    x2T, qwT, kwT, vwT = io["x2T_s"], io["qwT"], io["kwT"], io["vwT"]
    qb, kb, vbrow = io["qb"], io["kb"], io["vbrow"]
    QT1, KT1, V1 = io["QT1_s"], io["KT1_s"], io["V1_s"]
    NCH = T // 512

    from contextlib import ExitStack
    with ExitStack() as _st:
        wq = _st.enter_context(nc.sbuf_tensor("p7wq", [128, 16, 1024], BF16))
        wk = _st.enter_context(nc.sbuf_tensor("p7wk", [128, 16, 1024], BF16))
        wv = _st.enter_context(nc.sbuf_tensor("p7wv", [128, 16, 1024], BF16))
        xsb = _st.enter_context(nc.sbuf_tensor("p7x", [128, 2, 16, 512], BF16))
        qbs = _st.enter_context(nc.sbuf_tensor("p7qb", [128, 8], F32))
        kbs = _st.enter_context(nc.sbuf_tensor("p7kb", [128, 8], F32))
        vbr = _st.enter_context(nc.sbuf_tensor("p7vr", [1, 1024], F32))
        on1 = _st.enter_context(nc.sbuf_tensor("p7on", [1, 128], F32))
        vbb = _st.enter_context(nc.sbuf_tensor("p7vb", [128, 1024], F32))
        qko = _st.enter_context(nc.sbuf_tensor("p7o", [128, 2, 512], BF16))
        v1b = _st.enter_context(nc.sbuf_tensor("p7v", [128, 2, 8, 132], BF16))
        ps0 = _st.enter_context(nc.psum_tensor("p7p0", [128, 512], F32))
        ps1 = _st.enter_context(nc.psum_tensor("p7p1", [128, 512], F32))
        psv = _st.enter_context(nc.psum_tensor("p7pv", [128, 512], F32))
        psb = _st.enter_context(nc.psum_tensor("p7pb", [128, 512], F32))
        blk = _st.enter_context(nc.Block("p7"))
        psAB = (ps0, ps1)

        @blk.gpsimd
        def _(g):
            for s_ in range(16):
                r = slice(s_ * 128, (s_ + 1) * 128)
                ctx.inc(g.dma_start(out=wq[:, s_, :], in_=qwT[r, :]),
                        "p7gw", 16)
                ctx.inc(g.dma_start(out=wk[:, s_, :], in_=kwT[r, :]),
                        "p7gw", 16)
                ctx.inc(g.dma_start(out=wv[:, s_, :], in_=vwT[r, :]),
                        "p7gw", 16)
            ctx.mark("p7gw", "done")

        @blk.sync
        def _(s):
            ctx.inc(s.dma_start(out=qbs[:], in_=qb[:, :]), "p7c", 16)
            ctx.inc(s.dma_start(out=kbs[:], in_=kb[:, :]), "p7c", 16)
            ctx.inc(s.dma_start(out=vbr[:], in_=vbrow[:, :]), "p7c", 16)
            ctx.mark("p7c", "done")
            for ch in range(NCH):
                par = ch % 2
                if ch >= 2:
                    ctx.wait_mark(s, "p7mm", ("chdone", ch - 2))
                xc = x2T[:, ch * 512:(ch + 1) * 512].rearrange(
                    "(s p) t -> p s t", p=128)
                ctx.inc(s.dma_start(out=xsb[:, par, :, :], in_=xc[:, :, :]),
                        f"p7x{par}", 16)
                if ch >= 1:
                    p7_stores(nc, ctx, io, s, ch - 1, qko, v1b)
            p7_stores(nc, ctx, io, s, NCH - 1, qko, v1b)
            for nm in ("p7sq0", "p7sq1", "p7sv0", "p7sv1"):
                ctx.wait(s, nm)

        @blk.vector
        def _(v):
            inst = v.memset(on1[:], 1.0)
            ctx.inc(inst, "p7ev", 1)
            ctx.mark("p7ev", "on1")
            inst = v.memset(v1b[:, 0, :, 128:132], 1.0)
            v.wait_ge(ctx.sem("p7ev"), ctx.cnt["p7ev"])
            inst = v.memset(v1b[:, 1, :, 128:132], 1.0)
            ctx.inc(inst, "p7ev", 1)
            ctx.mark("p7ev", "vones")
            for c2 in range(2):
                ctx.wait_mark(v, "p7mm", ("vbb", c2))
                inst = v.tensor_copy(vbb[:, c2 * 512:(c2 + 1) * 512], psb[:])
                ctx.inc(inst, "p7ev", 1)
                ctx.mark("p7ev", ("vbc", c2))
            for ch in range(NCH):
                par = ch % 2
                for qk in range(16):
                    j = ch * 16 + qk
                    dbs = (qbs if qk < 8 else kbs)[:, qk % 8:qk % 8 + 1]
                    ctx.wait_mark(v, "p7mm", ("qk", ch, qk))
                    if j >= 2:
                        ctx.wait_mark(v, f"p7sq{j % 2}", ("sq", j - 2))
                    inst = v.tensor_scalar(qko[:, j % 2, :], psAB[j % 2][:],
                                           dbs, None, ALU.add)
                    ctx.inc(inst, "p7ev", 1)
                    ctx.mark("p7ev", ("qke", ch, qk))
                for ts in range(4):
                    gt = ch * 4 + ts
                    if gt >= 2:
                        ctx.wait_mark(v, f"p7sv{gt % 2}", ("sv", gt - 2))
                    for vc in range(2):
                        idx = ch * 8 + ts * 2 + vc
                        ctx.wait_mark(v, "p7mm", ("v", ch, ts, vc))
                        for hh4 in range(4):
                            hidx = vc * 4 + hh4
                            inst = v.scalar_tensor_tensor(
                                v1b[:, gt % 2, hidx, 0:128],
                                psv[:, hh4 * 128:(hh4 + 1) * 128], 1.0,
                                vbb[:, hidx * 128:(hidx + 1) * 128],
                                ALU.mult, ALU.add)
                        ctx.inc(inst, "p7ev", 1)
                        ctx.mark("p7ev", ("ve", ch, ts, vc))

        @blk.tensor
        def _(t):
            ctx.wait_mark(t, "p7gw", "done")
            ctx.wait_mark(t, "p7c", "done")
            ctx.wait_mark(t, "p7ev", "on1")
            for c2 in range(2):
                if c2 >= 1:
                    ctx.wait_mark(t, "p7ev", ("vbc", c2 - 1))
                inst = t.matmul(psb[:], on1[0:1, :],
                                vbr[0:1, c2 * 512:(c2 + 1) * 512],
                                start=True, stop=True)
                ctx.inc(inst, "p7mm", 1)
                ctx.mark("p7mm", ("vbb", c2))
            for ch in range(NCH):
                par = ch % 2
                t.wait_ge(ctx.sem(f"p7x{par}"), 16 * (ch // 2 + 1))
                for qk in range(16):
                    j = ch * 16 + qk
                    w = wq if qk < 8 else wk
                    db = qk % 8
                    if j >= 2:
                        ctx.wait_mark(t, "p7ev", ("qke", (j - 2) // 16,
                                                  (j - 2) % 16))
                    for s_ in range(16):
                        inst = t.matmul(
                            psAB[j % 2][:],
                            w[:, s_, db * 128:(db + 1) * 128],
                            xsb[:, par, s_, :],
                            start=(s_ == 0), stop=(s_ == 15))
                    ctx.inc(inst, "p7mm", 1)
                    ctx.mark("p7mm", ("qk", ch, qk))
                for ts in range(4):
                    for vc in range(2):
                        idx = ch * 8 + ts * 2 + vc
                        if idx >= 1:
                            pr = divmod(idx - 1, 8)
                            ctx.wait_mark(t, "p7ev", ("ve", pr[0],
                                                      pr[1] // 2, pr[1] % 2))
                        for s_ in range(16):
                            inst = t.matmul(
                                psv[:],
                                xsb[:, par, s_, ts * 128:(ts + 1) * 128],
                                wv[:, s_, vc * 512:(vc + 1) * 512],
                                start=(s_ == 0), stop=(s_ == 15))
                        ctx.inc(inst, "p7mm", 1)
                        ctx.mark("p7mm", ("v", ch, ts, vc))
                ctx.mark("p7mm", ("chdone", ch))


def p7_stores(nc, ctx, io, s, ch, qko, v1b):
    QT1, KT1, V1 = io["QT1_s"], io["KT1_s"], io["V1_s"]
    csl = slice(ch * 512, (ch + 1) * 512)
    for qk in range(16):
        j = ch * 16 + qk
        dst = QT1 if qk < 8 else KT1
        row = (qk % 8) * 128
        ctx.wait_mark(s, "p7ev", ("qke", ch, qk))
        ctx.inc(s.dma_start(out=dst[row:row + 128, csl],
                            in_=qko[:, j % 2, :]), f"p7sq{j % 2}", 16)
        ctx.mark(f"p7sq{j % 2}", ("sq", j))
    for ts in range(4):
        gt = ch * 4 + ts
        ctx.wait_mark(s, "p7ev", ("ve", ch, ts, 1))
        ctx.wait_mark(s, "p7ev", "vones")
        ctx.inc(s.dma_start(out=V1[gt * 128:(gt + 1) * 128, :, :],
                            in_=v1b[:, gt % 2, :, :]), f"p7sv{gt % 2}", 16)
        ctx.mark(f"p7sv{gt % 2}", ("sv", gt))



def emit_p8(nc, ctx, io, ten):
    QT1, KT1, V1 = io["QT1_s"], io["KT1_s"], io["V1_s"]
    l1delta, ident = io["l1delta"], io["ident"]
    oT1 = io["oT1_s"]

    from contextlib import ExitStack
    with ExitStack() as _st:
        qsb = _st.enter_context(nc.sbuf_tensor("p8q", [128, 2, T], BF16))
        ksb = _st.enter_context(nc.sbuf_tensor("p8k", [128, 2, T], BF16))
        vsb = _st.enter_context(nc.sbuf_tensor("p8v", [128, 2, 32, 132], BF16))
        dts = _st.enter_context(nc.sbuf_tensor("p8d", [128, 6, 512], F32))
        idb = _st.enter_context(nc.sbuf_tensor("p8id", [128, 128], BF16))
        zb = _st.enter_context(nc.sbuf_tensor("p8z", [128, 2, 512], F32))
        eb = _st.enter_context(nc.sbuf_tensor("p8e", [128, 2, 512], BF16))
        ob = _st.enter_context(nc.sbuf_tensor("p8o", [128, 128], BF16))
        otb = _st.enter_context(nc.sbuf_tensor("p8ot", [128, 2, 128], BF16))
        den = _st.enter_context(nc.sbuf_tensor("p8dn", [128, 2], F32))
        psS = _st.enter_context(nc.psum_tensor("p8ps", [128, 512], F32))
        psO0 = _st.enter_context(nc.psum_tensor("p8o0", [128, 132], F32))
        psO1 = _st.enter_context(nc.psum_tensor("p8o1", [128, 132], F32))
        psO2 = _st.enter_context(nc.psum_tensor("p8o2", [128, 132], F32))
        psO3 = _st.enter_context(nc.psum_tensor("p8o3", [128, 132], F32))
        psT = _st.enter_context(nc.psum_tensor("p8t", [128, 128], BF16))
        blk = _st.enter_context(nc.Block("p8"))
        psO = (psO0, psO1, psO2, psO3)

        def jbs_of(ci):
            c0 = ci * 4
            return [jb for jb in range(max(0, c0 - 2), min(32, c0 + 4))]

        @blk.gpsimd
        def _(g):
            ctx.inc(g.dma_start(out=idb[:], in_=ident[:, :]), "p8g", 16)
            ctx.mark("p8g", "done")

        @blk.sync
        def _(s):
            for r in range(6):
                ctx.inc(s.dma_start(out=dts[:, r, :], in_=l1delta[r, :, :]),
                        "p8c", 16)
            ctx.mark("p8c", "done")
            ot_ctr = 0
            for h in range(8):
                hp = h % 2
                if h >= 2:
                    ctx.wait_mark(s, "p8mm", ("hdone", h - 2))
                ctx.inc(s.dma_start(out=qsb[:, hp, :],
                                    in_=QT1[h * 128:(h + 1) * 128, :]),
                        f"p8lq{hp}", 16)
                ctx.inc(s.dma_start(out=ksb[:, hp, :],
                                    in_=KT1[h * 128:(h + 1) * 128, :]),
                        f"p8lk{hp}", 16)
                vr = V1[:, h, :].rearrange("(s p) w -> p s w", p=128)
                ctx.inc(s.dma_start(out=vsb[:, hp, :, :], in_=vr[:, :, :]),
                        f"p8lv{hp}", 16)
                if h >= 1:
                    for ib in range(32):
                        ctx.wait_mark(s, "p8ev", ("oe", h - 1, ib))
                        ctx.inc(s.dma_start(
                            out=oT1[(h - 1) * 128:h * 128,
                                    ib * 128:(ib + 1) * 128],
                            in_=otb[:, ot_ctr % 2, :]), f"p8s{ot_ctr % 2}", 16)
                        ctx.mark(f"p8s{ot_ctr % 2}", ("st", ot_ctr))
                        ot_ctr += 1
            for ib in range(32):
                ctx.wait_mark(s, "p8ev", ("oe", 7, ib))
                ctx.inc(s.dma_start(
                    out=oT1[7 * 128:8 * 128, ib * 128:(ib + 1) * 128],
                    in_=otb[:, ot_ctr % 2, :]), f"p8s{ot_ctr % 2}", 16)
                ctx.mark(f"p8s{ot_ctr % 2}", ("st", ot_ctr))
                ot_ctr += 1
            ctx.wait(s, "p8s0")
            ctx.wait(s, "p8s1")

        @blk.tensor
        def _(t):
            ctx.wait_mark(t, "p8g", "done")
            ctx.wait_mark(t, "p8c", "done")
            gz = 0
            for h in range(8):
                hp = h % 2
                nld = 16 * (h // 2 + 1)
                t.wait_ge(ctx.sem(f"p8lq{hp}"), nld)
                t.wait_ge(ctx.sem(f"p8lk{hp}"), nld)
                t.wait_ge(ctx.sem(f"p8lv{hp}"), nld)
                for ci in range(8):
                    c0 = ci * 4
                    jl = jbs_of(ci)
                    started = [False] * 4
                    for jn, jb in enumerate(jl):
                        if gz >= 1:
                            ctx.wait_mark(t, "p8ev", ("z", gz - 1))
                        inst = t.matmul(psS[:],
                                        qsb[:, hp, jb * 128:(jb + 1) * 128],
                                        ksb[:, hp, c0 * 128:(c0 + 4) * 128],
                                        start=True, stop=True)
                        ctx.inc(inst, "p8mm", 1)
                        ctx.mark("p8mm", ("s", gz))
                        ctx.wait_mark(t, "p8ex", ("x", gz))
                        for sst in range(4):
                            ib = c0 + sst
                            if jb > ib or jb < ib - 2:
                                continue
                            first = (jb == max(0, ib - 2))
                            last = (jb == ib)
                            if first and ci >= 1 and False:
                                pass
                            if first:
                                ctx.wait_mark(t, "p8ev", ("ob", h, ib - 4)
                                              if ib >= 4 else ("init",))
                            inst = t.matmul(
                                psO[sst][:, 0:129],
                                eb[:, gz % 2, sst * 128:(sst + 1) * 128],
                                vsb[:, hp, jb, 0:129],
                                start=first, stop=last)
                            ctx.inc(inst, "p8mm", 1)
                            if last:
                                ctx.mark("p8mm", ("m", h, ib))
                        gz += 1
                    for sst in range(4):
                        ib = c0 + sst
                        ctx.wait_mark(t, "p8ev", ("obn", h, ib))
                        inst = t.transpose(psT[:], ob[:], idb[:])
                        ctx.inc(inst, "p8mm", 1)
                        ctx.mark("p8mm", ("ot", h, ib))
                ctx.mark("p8mm", ("hdone", h))

        @blk.scalar
        def _(s):
            gz = 0
            for h in range(8):
                for ci in range(8):
                    for jn, jb in enumerate(jbs_of(ci)):
                        ctx.wait_mark(s, "p8ev", ("z", gz))
                        inst = s.activation(out=eb[:, gz % 2, :],
                                            in_=zb[:, gz % 2, :],
                                            func=AF.Exp, scale=SC1)
                        ctx.inc(inst, "p8ex", 1)
                        ctx.mark("p8ex", ("x", gz))
                        gz += 1

        @blk.vector
        def _(v):
            def vsync(inst):
                ctx.inc(inst, "p8vs", 1)
                v.wait_ge(ctx.sem("p8vs"), ctx.cnt["p8vs"])
            inst = v.nop()
            ctx.inc(inst, "p8ev", 1)
            ctx.mark("p8ev", ("init",))
            gz = 0
            ot_ctr = 0
            for h in range(8):
                for ci in range(8):
                    c0 = ci * 4
                    for jn, jb in enumerate(jbs_of(ci)):
                        R = c0 - jb
                        ctx.wait_mark(v, "p8mm", ("s", gz))
                        if gz >= 2:
                            ctx.wait_mark(v, "p8ex", ("x", gz - 2))
                        inst = v.tensor_sub(zb[:, gz % 2, :], psS[:],
                                            dts[:, R + 3, :])
                        ctx.inc(inst, "p8ev", 1)
                        ctx.mark("p8ev", ("z", gz))
                        gz += 1
                    for sst in range(4):
                        ib = c0 + sst
                        ctx.wait_mark(v, "p8mm", ("m", h, ib))
                        vsync(v.reciprocal(den[:, 0:1],
                                           psO[sst][:, 128:129]))
                        if ot_ctr >= 2:
                            ctx.wait(v, f"p8s{ot_ctr % 2}",
                                     at=16 * (ot_ctr // 2))
                        inst = v.tensor_scalar(ob[:], psO[sst][:, 0:128],
                                               den[:, 0:1], None, ALU.mult)
                        ctx.inc(inst, "p8ev", 1)
                        ctx.mark("p8ev", ("obn", h, ib))
                        ctx.mark("p8ev", ("ob", h, ib))
                        ctx.wait_mark(v, "p8mm", ("ot", h, ib))
                        inst = v.tensor_copy(otb[:, ot_ctr % 2, :], psT[:])
                        ctx.inc(inst, "p8ev", 1)
                        ctx.mark("p8ev", ("oe", h, ib))
                        ot_ctr += 1



def emit_p10(nc, ctx, io, ten):
    x3T, x3, AT, BT = io["x3T_s"], io["x3_s"], io["AT"], io["BT"]
    wrT, br8c = io["wrT"], io["br8c"]
    iota8, iotaB8, ident = io["iota8"], io["iotaB8"], io["ident"]
    out_s = io["out_s"]
    NTS = NT // 128

    from contextlib import ExitStack
    with ExitStack() as _st:
        xsb = _st.enter_context(nc.sbuf_tensor("pAx", [128, 2, 16, 512], BF16))
        asb = _st.enter_context(nc.sbuf_tensor("pAa", [128, 16, 8, 64], BF16))
        bsb = _st.enter_context(nc.sbuf_tensor("pAb", [64, 8, DM], BF16))
        wrs = _st.enter_context(nc.sbuf_tensor("pAw", [128, 16, 8], BF16))
        brs = _st.enter_context(nc.sbuf_tensor("pAbr", [8, 1], F32))
        idsb = _st.enter_context(nc.sbuf_tensor("pAid", [128, 128], F32))
        on1 = _st.enter_context(nc.sbuf_tensor("pAon", [1, 128], F32))
        iorow = _st.enter_context(nc.sbuf_tensor("pAio", [1, 16], F32))
        iob = _st.enter_context(nc.sbuf_tensor("pAiob", [128, 16], F32))
        logT = _st.enter_context(nc.sbuf_tensor("pAlg", [8, NT], F32))
        rt = _st.enter_context(nc.sbuf_tensor("pArt", [128, 6, 8], F32))
        rst = _st.enter_context(nc.sbuf_tensor("pArst", [128, 8], F32))
        scTt = _st.enter_context(nc.sbuf_tensor("pAsct", [8, 2, 128], F32))
        scrow = _st.enter_context(nc.sbuf_tensor("pAscr", [1, 2, NT], F32))
        scb = _st.enter_context(nc.sbuf_tensor("pAscb", [64, 512], F32))
        hT = _st.enter_context(nc.sbuf_tensor("pAh", [64, 8, NT], BF16))
        xr = _st.enter_context(nc.sbuf_tensor("pAxr", [128, 2, DM], F32))
        xsum = _st.enter_context(nc.sbuf_tensor("pAxs", [128, 2, DM], F32))
        sq = _st.enter_context(nc.sbuf_tensor("pAsq", [128, DM], F32))
        stat = _st.enter_context(nc.sbuf_tensor("pAst", [128, 8], F32))
        of = _st.enter_context(nc.sbuf_tensor("pAof", [128, 2, DM], F32))
        sc_d = nc.dram_tensor("pAsc_d", [8, NT], F32, kind="Internal").ap()
        psL = _st.enter_context(nc.psum_tensor("pApl", [8, 512], F32))
        psR = _st.enter_context(nc.psum_tensor("pApr", [128, 16], F32))
        psR2 = _st.enter_context(nc.psum_tensor("pApr2", [8, 128], F32))
        psH = _st.enter_context(nc.psum_tensor("pAph", [64, 512], F32))
        psB = _st.enter_context(nc.psum_tensor("pApb", [64, 512], F32))
        psY0 = _st.enter_context(nc.psum_tensor("pAy0", [128, 512], F32))
        psY1 = _st.enter_context(nc.psum_tensor("pAy1", [128, 512], F32))
        blk = _st.enter_context(nc.Block("p10"))
        psY = (psY0, psY1)

        @blk.gpsimd
        def _(g):
            for s_ in range(16):
                r = slice(s_ * 128, (s_ + 1) * 128)
                ctx.inc(g.dma_start(out=wrs[:, s_, :], in_=wrT[r, :]),
                        "pAg", 16)
                for e in range(8):
                    ctx.inc(g.dma_start(out=asb[:, s_, e, :],
                                        in_=AT[e, r, :]), "pAg", 16)
            for e in range(8):
                ctx.inc(g.dma_start(out=bsb[:, e, :], in_=BT[e, :, :]),
                        "pAg", 16)
            ctx.mark("pAg", "done")

        @blk.sync
        def _(s):
            ctx.inc(s.dma_start(out=brs[:], in_=br8c[:, :]), "pAc", 16)
            ctx.inc(s.dma_start(out=idsb[:], in_=ident[:, :]), "pAc", 16)
            ctx.inc(s.dma_start(out=iorow[0:1, 0:8], in_=iota8[:, :]),
                    "pAc", 16)
            ctx.inc(s.dma_start(out=iorow[0:1, 8:16], in_=iotaB8[:, :]),
                    "pAc", 16)
            ctx.mark("pAc", "done")

            def xchunk_load(s, cidx):
                par = cidx % 2
                tc = cidx % 4 if cidx < 4 else (cidx - 4) % 4
                if cidx >= 2:
                    ctx.wait_mark(s, "pAmm", ("xfree", cidx - 2))
                xc = x3T[:, tc * 512:(tc + 1) * 512].rearrange(
                    "(s p) t -> p s t", p=128)
                ctx.inc(s.dma_start(out=xsb[:, par, :, :], in_=xc[:, :, :]),
                        f"pAx{par}", 16)

            for cidx in range(6):
                xchunk_load(s, cidx)
            for ts in range(NTS):
                ctx.wait_mark(s, "pAev", ("rt1", ts))
                ctx.inc(s.dma_start(out=sc_d[:, ts * 128:(ts + 1) * 128],
                                    in_=scTt[:, ts % 2, :]),
                        f"pAwd{ts % 2}", 16)
            ctx.wait(s, "pAwd0")
            ctx.wait(s, "pAwd1")
            def wrow_load(s, e):
                if e >= 2:
                    ctx.wait_mark(s, "pAmm", ("hh", e - 2, 3))
                ctx.inc(s.dma_start(out=scrow[0:1, e % 2, :],
                                    in_=sc_d[e:e + 1, :]), f"pAwr{e % 2}", 16)
            wrow_load(s, 0)
            wrow_load(s, 1)
            for e in range(8):
                for cidx in range(max(6, 4 * e + 4), 4 * e + 8):
                    xchunk_load(s, cidx)
                if e + 2 < 8:
                    wrow_load(s, e + 2)
            for ts in range(NTS):
                par = ts % 2
                if ts >= 2:
                    ctx.wait_mark(s, "pAev", ("lnf", ts - 2))
                ctx.inc(s.dma_start(out=xr[:, par, :],
                                    in_=x3[ts * 128:(ts + 1) * 128, :]),
                        f"pAr{par}", 16)
                if ts >= 1:
                    t2 = ts - 1
                    ctx.wait_mark(s, "pAev", ("lnf", t2))
                    ctx.inc(s.dma_start(
                        out=out_s[t2 * 128:(t2 + 1) * 128, :],
                        in_=of[:, t2 % 2, :]), f"pAs{t2 % 2}", 16)
                    ctx.mark(f"pAs{t2 % 2}", ("sf", t2))
            t2 = NTS - 1
            ctx.wait_mark(s, "pAev", ("lnf", t2))
            ctx.inc(s.dma_start(out=out_s[t2 * 128:(t2 + 1) * 128, :],
                                in_=of[:, t2 % 2, :]), f"pAs{t2 % 2}", 16)
            ctx.mark(f"pAs{t2 % 2}", ("sf", t2))
            ctx.wait(s, "pAs0")
            ctx.wait(s, "pAs1")

        @blk.tensor
        def _(t):
            ctx.wait_mark(t, "pAg", "done")
            ctx.wait_mark(t, "pAc", "done")
            ctx.wait_mark(t, "pAev", "on1")
            for tc in range(4):
                cidx = tc
                t.wait_ge(ctx.sem(f"pAx{cidx % 2}"), 16 * (cidx // 2 + 1))
                if tc >= 1:
                    ctx.wait_mark(t, "pAev", ("lge", tc - 1))
                for s_ in range(16):
                    inst = t.matmul(psL[:], wrs[:, s_, :],
                                    xsb[:, cidx % 2, s_, :],
                                    start=(s_ == 0), stop=(s_ == 15))
                ctx.inc(inst, "pAmm", 1)
                ctx.mark("pAmm", ("lg", tc))
                ctx.mark("pAmm", ("xfree", cidx))
            inst = t.matmul(psR[:], on1[0:1, :], iorow[0:1, 0:16],
                            start=True, stop=True)
            ctx.inc(inst, "pAmm", 1)
            ctx.mark("pAmm", ("iob",))
            for ts in range(NTS):
                ctx.wait_mark(t, "pAev", ("lge", 3))
                if ts >= 1:
                    ctx.wait_mark(t, "pAev", ("rt0", ts - 1))
                else:
                    ctx.wait_mark(t, "pAev", ("iobc",))
                inst = t.transpose(psR[:, 0:8],
                                   logT[0:8, ts * 128:(ts + 1) * 128],
                                   idsb[0:8, 0:8])
                ctx.inc(inst, "pAmm", 1)
                ctx.mark("pAmm", ("rtp", ts))
                ctx.wait_mark(t, "pAev", ("sct", ts))
                inst = t.transpose(psR2[:], rt[:, 5, :], idsb[:])
                ctx.inc(inst, "pAmm", 1)
                ctx.mark("pAmm", ("rtp2", ts))
            for e in range(8):
                t.wait_ge(ctx.sem(f"pAwr{e % 2}"), 16 * (e // 2 + 1))
                for tc in range(4):
                    if not (e == 0 and tc == 0):
                        pe, ptc = (e, tc - 1) if tc >= 1 else (e - 1, 3)
                        ctx.wait_mark(t, "pAev", ("scbc", pe, ptc))
                    inst = t.matmul(psB[:], on1[0:1, 0:64],
                                    scrow[0:1, e % 2,
                                          tc * 512:(tc + 1) * 512],
                                    start=True, stop=True)
                    ctx.inc(inst, "pAmm", 1)
                    ctx.mark("pAmm", ("scb", e, tc))
                    cidx = 4 + e * 4 + tc
                    t.wait_ge(ctx.sem(f"pAx{cidx % 2}"),
                              16 * (cidx // 2 + 1))
                    if not (e == 0 and tc == 0):
                        pe, ptc = (e, tc - 1) if tc >= 1 else (e - 1, 3)
                        ctx.wait_mark(t, "pAev", ("he", pe, ptc))
                    for s_ in range(16):
                        inst = t.matmul(psH[:], asb[:, s_, e, :],
                                        xsb[:, cidx % 2, s_, :],
                                        start=(s_ == 0), stop=(s_ == 15))
                    ctx.inc(inst, "pAmm", 1)
                    ctx.mark("pAmm", ("hh", e, tc))
                    ctx.mark("pAmm", ("xfree", cidx))
            for ts in range(NTS):
                for oc in range(4):
                    j = ts * 4 + oc
                    if j >= 2:
                        ctx.wait_mark(t, "pAev", ("ye", (j - 2) // 4,
                                                  (j - 2) % 4))
                    for e in range(8):
                        if e == 0:
                            ctx.wait_mark(t, "pAev", ("he", 7, 3))
                        inst = t.matmul(
                            psY[j % 2][:],
                            hT[:, e, ts * 128:(ts + 1) * 128],
                            bsb[:, e, oc * 512:(oc + 1) * 512],
                            start=(e == 0), stop=(e == 7))
                    ctx.inc(inst, "pAmm", 1)
                    ctx.mark("pAmm", ("yy", ts, oc))

        @blk.scalar
        def _(s):
            for ts in range(NTS):
                ctx.wait_mark(s, "pAev", ("negm", ts))
                inst = s.activation(out=rt[:, 3, :], in_=rt[:, 0, :],
                                    func=AF.Exp, bias=rst[:, 1:2],
                                    accum_out=rst[:, 2:3])
                ctx.inc(inst, "pAex", 1)
                ctx.mark("pAex", ("exps", ts))

        @blk.vector
        def _(v):
            def vsync(inst):
                ctx.inc(inst, "pAvs", 1)
                v.wait_ge(ctx.sem("pAvs"), ctx.cnt["pAvs"])
            inst = v.memset(on1[:], 1.0)
            ctx.inc(inst, "pAev", 1)
            ctx.mark("pAev", "on1")
            for tc in range(4):
                ctx.wait_mark(v, "pAmm", ("lg", tc))
                inst = v.tensor_scalar(logT[0:8, tc * 512:(tc + 1) * 512],
                                       psL[:], brs[:, 0:1], None, ALU.add)
                ctx.inc(inst, "pAev", 1)
                ctx.mark("pAev", ("lge", tc))
            ctx.wait_mark(v, "pAmm", ("iob",))
            inst = v.tensor_copy(iob[:], psR[:, 0:16])
            ctx.inc(inst, "pAev", 1)
            ctx.mark("pAev", ("iobc",))
            for ts in range(NTS):
                ctx.wait_mark(v, "pAmm", ("rtp", ts))
                inst = v.tensor_copy(rt[:, 0, :], psR[:, 0:8])
                ctx.inc(inst, "pAev", 1)
                ctx.mark("pAev", ("rt0", ts))
                ctx.wait(v, "pAev")
                vsync(v.reduce_max(rst[:, 0:1], rt[:, 0, :], axis=AX))
                inst = v.tensor_scalar(rst[:, 1:2], rst[:, 0:1], -1.0, None,
                                       ALU.mult)
                ctx.inc(inst, "pAev", 1)
                ctx.mark("pAev", ("negm", ts))
                ctx.wait(v, "pAev")
                ctx.wait_mark(v, "pAex", ("exps", ts))
                vsync(v.reciprocal(rst[:, 3:4], rst[:, 2:3]))
                vsync(v.tensor_scalar(rt[:, 1, :], rt[:, 0, :], rst[:, 0:1],
                                      None, ALU.is_equal))
                vsync(v.scalar_tensor_tensor(rt[:, 2, :], rt[:, 1, :], -100.0,
                                             iob[:, 8:16], ALU.mult, ALU.add))
                vsync(v.tensor_reduce(rst[:, 4:5], rt[:, 2, :], AX, ALU.min))
                vsync(v.tensor_scalar(rt[:, 4, :], iob[:, 0:8], rst[:, 4:5],
                                      None, ALU.is_equal))
                inst = v.tensor_scalar(rt[:, 5, :], rt[:, 4, :], rst[:, 3:4],
                                       1.0 / RANK, ALU.mult, ALU.mult)
                ctx.inc(inst, "pAev", 1)
                ctx.mark("pAev", ("sct", ts))
                ctx.wait(v, "pAev")
                ctx.wait_mark(v, "pAmm", ("rtp2", ts))
                if ts >= 2:
                    ctx.wait(v, f"pAwd{ts % 2}", at=16 * (ts // 2))
                inst = v.tensor_copy(scTt[:, ts % 2, :], psR2[:])
                ctx.inc(inst, "pAev", 1)
                ctx.mark("pAev", ("rt1", ts))
            for e in range(8):
                for tc in range(4):
                    csl = slice(tc * 512, (tc + 1) * 512)
                    ctx.wait_mark(v, "pAmm", ("scb", e, tc))
                    ctx.wait(v, "pAev")
                    inst = v.tensor_copy(scb[:], psB[:])
                    ctx.inc(inst, "pAev", 1)
                    ctx.mark("pAev", ("scbc", e, tc))
                    ctx.wait(v, "pAev")
                    ctx.wait_mark(v, "pAmm", ("hh", e, tc))
                    inst = v.tensor_mul(hT[:, e, csl], psH[:], scb[:])
                    ctx.inc(inst, "pAev", 1)
                    ctx.mark("pAev", ("he", e, tc))
            for ts in range(NTS):
                par = ts % 2
                v.wait_ge(ctx.sem(f"pAr{par}"), 16 * (ts // 2 + 1))
                for oc in range(4):
                    csl = slice(oc * 512, (oc + 1) * 512)
                    j = ts * 4 + oc
                    ctx.wait_mark(v, "pAmm", ("yy", ts, oc))
                    if ts >= 2 and oc == 0:
                        ctx.wait_mark(v, "pAex", ("sq", ts - 2))
                    inst = v.tensor_add(xsum[:, par, csl], psY[j % 2][:],
                                        xr[:, par, csl])
                    ctx.inc(inst, "pAev", 1)
                    ctx.mark("pAev", ("ye", ts, oc))
                ctx.wait(v, "pAev")
                vsync(v.reduce_sum(stat[:, 0:1], xsum[:, par, :], axis=AX))
                if ts >= 1:
                    ctx.wait_mark(v, "pAex", ("sq", ts - 1))
                v.tensor_scalar(stat[:, 1:2], stat[:, 0:1], 1.0 / DM, None,
                                ALU.mult)
                inst = v.tensor_scalar(stat[:, 2:3], stat[:, 0:1],
                                       -1.0 / DM, None, ALU.mult)
                ctx.inc(inst, "pAev", 1)
                ctx.mark("pAev", ("mn", ts))
                ctx.wait_mark(v, "pAex", ("sq", ts))
                inst = v.tensor_scalar(stat[:, 4:5], stat[:, 3:4], 1.0 / DM,
                                       EPS, ALU.mult, ALU.add)
                ctx.inc(inst, "pAev", 1)
                ctx.mark("pAev", ("veps", ts))
                ctx.wait_mark(v, "pAex", ("sd", ts))
                vsync(v.reciprocal(stat[:, 6:7], stat[:, 5:6]))
                if ts >= 2:
                    ctx.wait_mark(v, f"pAs{par}", ("sf", ts - 2))
                inst = v.tensor_scalar(of[:, par, :], xsum[:, par, :],
                                       stat[:, 1:2], stat[:, 6:7],
                                       ALU.subtract, ALU.mult)
                ctx.inc(inst, "pAev", 1)
                ctx.mark("pAev", ("lnf", ts))

        @blk.scalar
        def _(s):
            for ts in range(NTS):
                par = ts % 2
                ctx.wait_mark(s, "pAev", ("mn", ts))
                inst = s.activation(out=sq[:], in_=xsum[:, par, :],
                                    func=AF.Square, bias=stat[:, 2:3],
                                    accum_out=stat[:, 3:4])
                ctx.inc(inst, "pAex", 1)
                ctx.mark("pAex", ("sq", ts))
                ctx.wait_mark(s, "pAev", ("veps", ts))
                inst = s.activation(out=stat[:, 5:6], in_=stat[:, 4:5],
                                    func=AF.Sqrt)
                ctx.inc(inst, "pAex", 1)
                ctx.mark("pAex", ("sd", ts))



SCRATCH_SPECS = {
    "dbg_s": ([128, 4096], F32),
    "oT0_s": ([1024, T], BF16),
    "oT0full_s": ([DM, NT], BF16),
    "x1_s": ([NT, DM], F32),
    "x1T_s": ([DM, NT], BF16),
    "hacts_s": ([NHS, 128, NT], BF16),
    "x2pre_s": ([NT, DM], F32),
    "x2_s": ([NT, DM], F32),
    "x2bT_s": ([DM, NT], BF16),
    "x2T_s": ([DM, T], BF16),
    "KT1_s": ([1024, T], BF16),
    "QT1_s": ([1024, T], BF16),
    "V1_s": ([T, 8, 132], BF16),
    "oT1_s": ([1024, T], BF16),
    "oT1full_s": ([DM, NT], BF16),
    "x3_s": ([NT, DM], F32),
    "x3T_s": ([DM, NT], BF16),
    "out_s": ([NT, DM], F32),
    "kT_s": ([1024, T], BF16),
    "qT_s": ([1024, T], BF16),
    "v_s": ([T, 8, 132], BF16),
    "beta_s": ([8, T], F32),
    "betaT_s": ([T, 8], F32),
}

INPUT_SPECS = {
    "xT": ([DM, T], F32),
    "x0h": ([NT, DM], F32),
    "wqkvb": ([DM, 3080], F32),
    "bb8c": ([8, 1], F32),
    "wo": ([DM, DM], F32),
    "gwT": ([DM, 2 * HS], F32),
    "sdwT": ([HS, DM], F32),
    "gateT": ([DM, 8], F32),
    "gbias": ([1, 8], F32),
    "upT": ([8, DM, 2 * HF], F32),
    "wblk": ([128, NHS, 2, 16, 128], F32),
    "dnT": ([8, HF, DM], F32),
    "qwT": ([DM, 1024], F32),
    "kwT": ([DM, 1024], F32),
    "vwT": ([DM, 1024], F32),
    "qb": ([128, 8], F32),
    "kb": ([128, 8], F32),
    "vbrow": ([1, 1024], F32),
    "owT": ([DM, DM], F32),
    "obrow": ([1, DM], F32),
    "AT": ([8, DM, RANK], F32),
    "BT": ([8, RANK, DM], F32),
    "wrT": ([DM, 8], F32),
    "br": ([1, 8], F32),
    "br8c": ([8, 1], F32),
    "cos2": ([128, T], F32),
    "sins": ([128, T], F32),
    "ident": ([128, 128], F32),
    "dstrict": ([128, 128], F32),
    "dincl": ([128, 128], F32),
    "l1delta": ([6, 128, 512], F32),
    "iota8": ([1, 8], F32),
    "iotaB8": ([1, 8], F32),
    "meta": ([1, 8], I32),
}

PHASES = {
    "p1": dict(emit=emit_p1,
               reads=["xT", "wqkvb", "bb8c", "cos2", "sins", "ident"],
               writes=["kT_s", "qT_s", "v_s", "beta_s", "betaT_s"]),
    "p2": dict(emit=emit_p2, twopass=True,
               reads=["kT_s", "qT_s", "v_s", "beta_s", "betaT_s",
                      "dstrict", "dincl", "ident"],
               writes=["oT0_s", "dbg_s"]),
    "p8": dict(emit=emit_p8, twopass=True,
               reads=["QT1_s", "KT1_s", "V1_s", "l1delta", "ident"],
               writes=["oT1_s"]),
    "p10": dict(emit=emit_p10, twopass=True,
                reads=["x3T_s", "x3_s", "AT", "BT", "wrT", "br8c",
                       "iota8", "iotaB8", "ident"],
                writes=["out_s"]),
    "p7": dict(emit=emit_p7, twopass=True,
               reads=["x2T_s", "qwT", "kwT", "vwT", "qb", "kb", "vbrow"],
               writes=["QT1_s", "KT1_s", "V1_s"]),
    "p5": dict(emit=emit_p5, twopass=True,
               reads=["hacts_s", "sdwT", "dnT"],
               writes=["x2pre_s"]),
    "p4": dict(emit=emit_p4, twopass=True,
               reads=["x1T_s", "wblk", "gateT", "gbias",
                      "iota8", "iotaB8", "ident"],
               writes=["hacts_s"]),
    "p3": dict(emit=lambda nc, ctx, io, ten: emit_post(nc, ctx, io, dict(
                   pref="p3", oT="oT0full_s", w="wo", resid="x0h",
                   out_f32="x1_s", out_T="x1T_s")),
               twopass=True,
               reads=["oT0full_s", "wo", "x0h", "ident"],
               writes=["x1_s", "x1T_s"]),
    "p6": dict(emit=lambda nc, ctx, io, ten: emit_post(nc, ctx, io, dict(
                   pref="p6", pre="x2pre_s", resid="x1_s",
                   out_f32="x2_s", out_T="x2bT_s")),
               twopass=True,
               reads=["x2pre_s", "x1_s", "ident"],
               writes=["x2_s", "x2bT_s"]),
    "p9": dict(emit=lambda nc, ctx, io, ten: emit_post(nc, ctx, io, dict(
                   pref="p9", oT="oT1full_s", w="owT", bias="obrow",
                   resid="x2_s", out_f32="x3_s", out_T="x3T_s",
                   sqalias=True)),
               twopass=True,
               reads=["oT1full_s", "owT", "obrow", "x2_s", "ident"],
               writes=["x3_s", "x3T_s"]),
}


def host_pair_merge(envs, srckey, dstkey, mode):
    for c in range(NCORE):
        p, h = c // 2, c % 2
        a, b = envs[2 * p], envs[2 * p + 1]
        tk = slice(h * NT, (h + 1) * NT)
        if mode == "heads":
            envs[c][dstkey] = np.ascontiguousarray(np.concatenate(
                [a[srckey][:, tk], b[srckey][:, tk]], axis=0))
        elif mode == "cols":
            envs[c][dstkey] = np.ascontiguousarray(np.concatenate(
                [a[srckey], b[srckey]], axis=1))
    return envs


def build_single_phase(name):
    ph = PHASES[name]
    nc = bass.Bass("TRN2", target_bir_lowering=False)
    io = {}
    for t in ph["reads"]:
        shape, dt = INPUT_SPECS.get(t) or SCRATCH_SPECS[t]
        io[t] = nc.dram_tensor(t, shape, dt, kind="ExternalInput").ap()
    for t in ph["writes"]:
        shape, dt = SCRATCH_SPECS[t]
        io[t] = nc.dram_tensor(t, shape, dt, kind="ExternalOutput").ap()
    ctx = Ctx(nc)
    ph["emit"](nc, ctx, io, None)
    return nc, ph


def build_single_phase2(name):
    ph = PHASES[name]
    marks = None
    for pss in (1, 2):
        nc = bass.Bass("TRN2", target_bir_lowering=False)
        io = {}
        for t in ph["reads"]:
            shape, dt = INPUT_SPECS.get(t) or SCRATCH_SPECS[t]
            io[t] = nc.dram_tensor(t, shape, dt, kind="ExternalInput").ap()
        for t in ph["writes"]:
            shape, dt = SCRATCH_SPECS[t]
            io[t] = nc.dram_tensor(t, shape, dt, kind="ExternalOutput").ap()
        ctx = Ctx(nc, marks=marks)
        ph["emit"](nc, ctx, io, None)
        ctx.check_pass2()
        marks = ctx.marks
    return nc, ph


def run_phase_unfused(name, envs):
    if PHASES[name].get("twopass"):
        nc, ph = build_single_phase2(name)
    else:
        nc, ph = build_single_phase(name)
    in_maps = [{t: envs[c][t] for t in ph["reads"]} for c in range(NCORE)]
    res = run_bass_kernel_spmd(nc, in_maps, core_ids=list(range(NCORE)))
    for c in range(NCORE):
        for t in ph["writes"]:
            envs[c][t] = res.results[c][t]
    return envs


def run_p4_split(envs, bounds=(0, 33, 66, 99, 110)):
    global P4_LIMIT
    acc = [np.empty((NHS, 128, NT), bf16) for _ in range(NCORE)]
    try:
        for lo, hi in zip(bounds[:-1], bounds[1:]):
            P4_LIMIT = lambda b, lo=lo, hi=hi: lo <= b[3] < hi
            envs = run_phase_unfused("p4", envs)
            for c in range(NCORE):
                acc[c][lo:hi] = envs[c]["hacts_s"][lo:hi]
    finally:
        P4_LIMIT = None
    for c in range(NCORE):
        envs[c]["hacts_s"] = acc[c]
    return envs


def kernel(**inputs):
    rng = np.random.default_rng()
    nonce = int(rng.integers(1, 2 ** 30))
    envs = [make_core_inputs(c, inputs, nonce) for c in range(NCORE)]
    envs = run_phase_unfused("p1", envs)
    envs = run_phase_unfused("p2", envs)
    envs = host_pair_merge(envs, "oT0_s", "oT0full_s", "heads")
    envs = run_phase_unfused("p3", envs)
    envs = run_p4_split(envs)
    envs = run_phase_unfused("p5", envs)
    envs = run_phase_unfused("p6", envs)
    envs = host_pair_merge(envs, "x2bT_s", "x2T_s", "cols")
    envs = run_phase_unfused("p7", envs)
    envs = run_phase_unfused("p8", envs)
    envs = host_pair_merge(envs, "oT1_s", "oT1full_s", "heads")
    envs = run_phase_unfused("p9", envs)
    envs = run_phase_unfused("p10", envs)
    out = np.empty((B, T, DM), np.float32)
    for c in range(NCORE):
        p, h = c // 2, c % 2
        out[p, h * NT:(h + 1) * NT, :] = envs[c]["out_s"]
    return out


# version 03 (final)
# speedup vs baseline: 1.1420x; 1.1420x over previous
import math
import numpy as np
import ml_dtypes
import concourse.bass as bass
import concourse.mybir as mybir
from concourse.bass_utils import run_bass_kernel_spmd
from concourse.expressions import make_scalar_value

F32 = mybir.dt.float32
BF16 = mybir.dt.bfloat16
I32 = mybir.dt.int32
AF = mybir.ActivationFunctionType
ALU = mybir.AluOpType
AX = mybir.AxisListType.X

NCORE = 8
B, T, DM = 4, 4096, 2048
NT = 2048
H, HD = 16, 128
HPC = 8
NS = T // 128
NSH = NT // 128
SC0 = 1.0 / math.sqrt(HD)
SC1 = HD ** (-0.5)
THETA = 10000.0
HS = 2816
HF = 1408
NEXP = 8
NHS = 110
RANK = 64
EPS = 1e-5
BIGM = 3.0e9

bf16 = ml_dtypes.bfloat16



def _rope_tables():
    half = HD // 2
    inv_freq = 1.0 / THETA ** (np.arange(half, dtype=np.float32) / half)
    ang = np.arange(T, dtype=np.float32)[:, None] * inv_freq[None, :]
    cos = np.cos(ang).astype(np.float32).T
    sin = np.sin(ang).astype(np.float32).T
    cos2 = np.concatenate([cos, cos], axis=0)
    sins = np.concatenate([-sin, sin], axis=0)
    return np.ascontiguousarray(cos2), np.ascontiguousarray(sins)


def _l1_delta_tiles():
    tiles = np.zeros((6, 128, 512), dtype=np.float32)
    ii = np.arange(512)[None, :]
    jj = np.arange(128)[:, None]
    for k, R in enumerate(range(-3, 3)):
        delta = 128 * R + ii - jj
        t = np.where(delta < 0, BIGM, delta / SC1)
        tiles[k] = t.astype(np.float32)
    return tiles


def _diag_masks():
    jj = np.arange(128)[:, None]
    ii = np.arange(128)[None, :]
    strict = np.where(jj < ii, 0.0, BIGM).astype(np.float32)
    incl = np.where(jj <= ii, 0.0, BIGM).astype(np.float32)
    return strict, incl


def make_core_inputs(c, inp, nonce):
    p, h = c // 2, c % 2
    hc = slice(h * 1024, (h + 1) * 1024)
    hh = slice(h * 8, (h + 1) * 8)
    tk = slice(h * NT, (h + 1) * NT)
    x = inp["L0_attn_hidden_states"][p]

    cos2, sins = _rope_tables()
    dstrict, dincl = _diag_masks()

    d = {
        "xT": np.ascontiguousarray(x.T),
        "x0h": np.ascontiguousarray(x[tk]),
        "wqkvb": np.ascontiguousarray(np.concatenate([
            inp["L0_attn_Wq"][:, hc], inp["L0_attn_Wk"][:, hc],
            inp["L0_attn_Wv"][:, hc], inp["L0_attn_Wb"][:, hh]], axis=1)),
        "bb8c": np.ascontiguousarray(inp["L0_attn_bb"][hh][:, None]),
        "wo": np.ascontiguousarray(inp["L0_attn_Wo"]),
        "gwT": np.ascontiguousarray(inp["L0_ffn_shared_gw"].T),
        "sdwT": np.ascontiguousarray(inp["L0_ffn_shared_dw"].T),
        "gateT": np.ascontiguousarray(inp["L0_ffn_gate_w"].T),
        "gbias": np.ascontiguousarray(
            inp["L0_ffn_gate_bias"][None, :]),
        "upT": np.ascontiguousarray(
            inp["L0_ffn_up_w"].transpose(0, 2, 1)),
        "wblk": stage_wblk(inp),
        "dnT": np.ascontiguousarray(
            inp["L0_ffn_down_w"].transpose(0, 2, 1)),
        "qwT": np.ascontiguousarray(inp["L1_attn_q_w"].T[:, hc]),
        "kwT": np.ascontiguousarray(inp["L1_attn_k_w"].T[:, hc]),
        "vwT": np.ascontiguousarray(inp["L1_attn_v_w"].T[:, hc]),
        "qb": np.ascontiguousarray(
            inp["L1_attn_q_b"][hc].reshape(8, 128).T),
        "kb": np.ascontiguousarray(
            inp["L1_attn_k_b"][hc].reshape(8, 128).T),
        "vbrow": np.ascontiguousarray(inp["L1_attn_v_b"][hc][None, :]),
        "owT": np.ascontiguousarray(inp["L1_attn_o_w"].T),
        "obrow": np.ascontiguousarray(inp["L1_attn_o_b"][None, :]),
        "AT": np.ascontiguousarray(inp["L1_ffn_A"].transpose(0, 2, 1)),
        "BT": np.ascontiguousarray(inp["L1_ffn_B"].transpose(0, 2, 1)),
        "wrT": np.ascontiguousarray(inp["L1_ffn_Wr"].T),
        "br": np.ascontiguousarray(inp["L1_ffn_br"][None, :]),
        "br8c": np.ascontiguousarray(inp["L1_ffn_br"][:, None]),
        "cos2": cos2, "sins": sins,
        "ident": np.eye(128, dtype=np.float32),
        "dstrict": dstrict, "dincl": dincl,
        "l1delta": _l1_delta_tiles(),
        "iota8": np.arange(8, dtype=np.float32)[None, :],
        "iotaB8": (np.arange(8, dtype=np.float32) + 100.0)[None, :],
        "meta": np.array([[c, c ^ 1, nonce, 0, 0, 0, 0, 0]], dtype=np.int32),
    }
    return d



class Ctx:

    def __init__(self, nc, marks=None):
        self.nc = nc
        self.sems = {}
        self.cnt = {}
        self.final = marks
        self.marks = {}

    def sem(self, name):
        if name not in self.sems:
            self.sems[name] = self.nc.alloc_semaphore(name)
            self.cnt[name] = 0
        return self.sems[name]

    def inc(self, inst, name, n):
        inst.then_inc(self.sem(name), n)
        self.cnt[name] += n
        return self.cnt[name]

    def wait(self, eng, name, at=None):
        eng.wait_ge(self.sem(name), self.cnt[name] if at is None else at)

    def val(self, name):
        self.sem(name)
        return self.cnt[name]

    def mark(self, name, label):
        key = (name, label)
        assert key not in self.marks, f"duplicate mark {key}"
        self.marks[key] = self.cnt.get(name, 0)

    def wait_mark(self, eng, name, label):
        key = (name, label)
        if self.final is None:
            tgt = self.marks.get(key, 0)
        else:
            assert key in self.final, f"missing mark {key} in pass 1"
            tgt = self.final[key]
        eng.wait_ge(self.sem(name), tgt)

    def check_pass2(self):
        if self.final is not None:
            assert self.marks == self.final, "pass-2 marks diverged from pass-1"



def emit_p1(nc, ctx, io, ten):
    xT, wqkvb = io["xT"], io["wqkvb"]
    bb8c, cos2, sins, ident = io["bb8c"], io["cos2"], io["sins"], io["ident"]
    kT_s, qT_s, v_s = io["kT_s"], io["qT_s"], io["v_s"]
    beta_s, betaT_s = io["beta_s"], io["betaT_s"]

    NCHUNK = 8

    with (
        nc.sbuf_tensor("p1_w", [128, 16, 3080], BF16) as wsb,
        nc.sbuf_tensor("p1_x", [128, 2, 16, 512], BF16) as xsb,
        nc.sbuf_tensor("p1_cos", [128, T], F32) as cossb,
        nc.sbuf_tensor("p1_sin", [128, T], F32) as sinsb,
        nc.sbuf_tensor("p1_bb", [8, 1], F32) as bbsb,
        nc.sbuf_tensor("p1_id", [128, 128], F32) as idsb,
        nc.sbuf_tensor("p1_tmpA", [128, 512], F32) as tmpA,
        nc.sbuf_tensor("p1_tmpB", [128, 512], F32) as tmpB,
        nc.sbuf_tensor("p1_out", [128, 2, 512], BF16) as outb,
        nc.sbuf_tensor("p1_beta", [8, T], F32) as betasb,
        nc.sbuf_tensor("p1_bT", [128, 32 * 8], F32) as bTsb,
        nc.sbuf_tensor("p1_v", [128, 2, 8, 132], BF16) as vsb,
        nc.psum_tensor("p1_ps0", [128, 512], F32) as ps0,
        nc.psum_tensor("p1_ps1", [128, 512], F32) as ps1,
        nc.psum_tensor("p1_psv", [128, 512], F32) as psv,
        nc.psum_tensor("p1_psb", [8, 512], F32) as psb,
        nc.psum_tensor("p1_pst", [128, 8], F32) as pst,
        nc.Block("p1") as blk,
    ):
        @blk.gpsimd
        def _(g):
            for s in range(16):
                ctx.inc(g.dma_start(out=wsb[:, s, :],
                                    in_=wqkvb[s * 128:(s + 1) * 128, :]),
                        "p1_w", 16)
            xTr = xT.rearrange("(s p) t -> p s t", p=128)
            for ch in range(NCHUNK):
                if ch >= 2:
                    ctx.wait(g, "p1_mmb", at=ch - 1)
                ctx.inc(g.dma_start(
                    out=xsb[:, ch % 2, :, :],
                    in_=xTr[:, :, ch * 512:(ch + 1) * 512]),
                    f"p1_x{ch % 2}", 16)

        @blk.vector
        def _(v):
            ctx.inc(v.memset(vsb[:, 0, :, 128:132], 1.0), "p1_ones", 1)
            ctx.inc(v.memset(vsb[:, 1, :, 128:132], 1.0), "p1_ones", 1)

        @blk.tensor
        def _(t):
            ctx.wait(t, "p1_w")
            ctx.wait(t, "p1_c")
            for ch in range(NCHUNK):
                ctx.wait(t, f"p1_x{ch % 2}", at=(ch // 2 + 1) * 16)
                xb = xsb[:, ch % 2, :, :]
                for bi in range(16):
                    j = ch * 16 + bi
                    ps = (ps0, ps1)[j % 2]
                    if j >= 2:
                        ctx.wait(t, "p1_rope", at=j - 1)
                    for s in range(16):
                        inst = t.matmul(ps[:],
                                        wsb[:, s, bi * 128:(bi + 1) * 128],
                                        xb[:, s, :],
                                        start=(s == 0), stop=(s == 15))
                    ctx.inc(inst, "p1_mmqk", 1)
                for ts in range(4):
                    for dc in range(2):
                        idx = ch * 8 + ts * 2 + dc
                        if idx >= 1:
                            ctx.wait(t, "p1_evv", at=idx)
                        for s in range(16):
                            inst = t.matmul(
                                psv[:],
                                xb[:, s, ts * 128:(ts + 1) * 128],
                                wsb[:, s, 2048 + dc * 512:2048 + (dc + 1) * 512],
                                start=(s == 0), stop=(s == 15))
                        ctx.inc(inst, "p1_mmv", 1)
                if ch >= 1:
                    ctx.wait(t, "p1_evb", at=ch)
                for s in range(16):
                    inst = t.matmul(psb[:], wsb[:, s, 3072:3080], xb[:, s, :],
                                    start=(s == 0), stop=(s == 15))
                ctx.inc(inst, "p1_mmb", 1)
            ctx.wait(t, "p1_evb", at=NCHUNK)
            for st in range(32):
                if st >= 1:
                    ctx.wait(t, "p1_evbT", at=st)
                inst = t.transpose(pst[:], betasb[:, st * 128:(st + 1) * 128],
                                   idsb[0:8, 0:8])
                ctx.inc(inst, "p1_mmbT", 1)

        @blk.vector
        def _(v):
            v.wait_ge(ctx.sem("p1_c"), 64)
            for ch in range(NCHUNK):
                csl = slice(ch * 512, (ch + 1) * 512)
                for bi in range(16):
                    j = ch * 16 + bi
                    ps = (ps0, ps1)[j % 2]
                    ctx.wait(v, "p1_mmqk", at=j + 1)
                    if j >= 2:
                        ctx.wait(v, f"p1_st{j % 2}", at=16 * (j // 2))
                    if j >= 1:
                        ctx.wait(v, "p1_rope", at=j)
                    v.tensor_mul(tmpA[:], ps[:], cossb[:, csl])
                    v.tensor_mul(tmpB[0:64, :], ps[64:128, :],
                                 sinsb[0:64, csl])
                    inst = v.tensor_mul(tmpB[64:128, :], ps[0:64, :],
                                        sinsb[64:128, csl])
                    ctx.inc(inst, "p1_vsr", 1)
                    ctx.wait(v, "p1_vsr")
                    inst = v.tensor_add(outb[:, j % 2, :], tmpA[:], tmpB[:])
                    ctx.inc(inst, "p1_rope", 1)
                for ts in range(4):
                    gt = ch * 4 + ts
                    ctx.wait(v, "p1_ones")
                    if gt >= 2:
                        ctx.wait(v, f"p1_stv{gt % 2}", at=16 * (gt // 2))
                    for dc in range(2):
                        idx = ch * 8 + ts * 2 + dc
                        ctx.wait(v, "p1_mmv", at=idx + 1)
                        for hh4 in range(4):
                            hidx = dc * 4 + hh4
                            inst = v.tensor_copy(
                                vsb[:, gt % 2, hidx, 0:128],
                                psv[:, hh4 * 128:(hh4 + 1) * 128])
                        ctx.inc(inst, "p1_evv", 1)
            for st in range(32):
                ctx.wait(v, "p1_mmbT", at=st + 1)
                inst = v.tensor_copy(bTsb[:, st * 8:(st + 1) * 8], pst[:])
                ctx.inc(inst, "p1_evbT", 1)

        @blk.scalar
        def _(s):
            for ch in range(NCHUNK):
                ctx.wait(s, "p1_mmb", at=ch + 1)
                ctx.wait(s, "p1_c", at=64)
                inst = s.activation(out=betasb[:, ch * 512:(ch + 1) * 512],
                                    in_=psb[:], func=AF.Identity,
                                    bias=bbsb[:, 0:1])
                ctx.inc(inst, "p1_evb", 1)

        @blk.sync
        def _(s):
            ctx.inc(s.dma_start(out=cossb[:], in_=cos2[:, :]), "p1_c", 16)
            ctx.inc(s.dma_start(out=sinsb[:], in_=sins[:, :]), "p1_c", 16)
            ctx.inc(s.dma_start(out=bbsb[:], in_=bb8c[:, :]), "p1_c", 16)
            ctx.inc(s.dma_start(out=idsb[:], in_=ident[:, :]), "p1_c", 16)
            for ch in range(NCHUNK):
                csl = slice(ch * 512, (ch + 1) * 512)
                for bi in range(16):
                    j = ch * 16 + bi
                    ctx.wait(s, "p1_rope", at=j + 1)
                    dst = qT_s if bi < 8 else kT_s
                    row = (bi % 8) * 128
                    ctx.inc(s.dma_start(out=dst[row:row + 128, csl],
                                        in_=outb[:, j % 2, :]),
                            f"p1_st{j % 2}", 16)
                for ts in range(4):
                    gt = ch * 4 + ts
                    ctx.wait(s, "p1_evv", at=2 * (gt + 1))
                    ctx.inc(s.dma_start(
                        out=v_s[gt * 128:(gt + 1) * 128, :, :],
                        in_=vsb[:, gt % 2, :, :]), f"p1_stv{gt % 2}", 16)
            ctx.wait(s, "p1_evb", at=NCHUNK)
            ctx.inc(s.dma_start(out=beta_s[:, :], in_=betasb[:]), "p1_st", 16)
            ctx.wait(s, "p1_evbT", at=32)
            bT = bTsb.ap().rearrange("p (s e) -> p s e", e=8)
            ctx.inc(s.dma_start(
                out=betaT_s.rearrange("(s p) e -> p s e", p=128)[:, :, :],
                in_=bT[:, :, :]), "p1_st", 16)
            for nm in ("p1_st0", "p1_st1", "p1_stv0", "p1_stv1", "p1_st"):
                ctx.wait(s, nm)



P2_NH = 8
P2_NS = NS
P2_DEBUG = False
P2_DBG_I = 0
P4_LIMIT = None


def emit_p2(nc, ctx, io, ten):
    kT_s, qT_s, v_s = io["kT_s"], io["qT_s"], io["v_s"]
    beta_s, betaT_s = io["beta_s"], io["betaT_s"]
    dstrict, dincl, ident = io["dstrict"], io["dincl"], io["ident"]
    oT0_s = io["oT0_s"]

    sched = []
    last_m = {}
    gid = 0
    for h in range(P2_NH):
        for i in range(P2_NS):
            for phase in (0, 1):
                gls = [list(range(g, min(g + 4, i))) for g in range(0, i, 4)]
                last_m[(h, i, phase)] = (gid + len(gls) - 1) if gls else None
                for gl in gls:
                    sched.append((gid, h, i, phase, gl))
                    gid += 1
    by_hi = {}
    for rec in sched:
        by_hi.setdefault((rec[1], rec[2], rec[3]), []).append(rec)

    from contextlib import ExitStack
    with ExitStack() as _st:
        ksb = _st.enter_context(nc.sbuf_tensor("p2_k", [128, 2, T], BF16))
        qsb = _st.enter_context(nc.sbuf_tensor("p2_q", [128, 2, T], BF16))
        vsb = _st.enter_context(nc.sbuf_tensor("p2_v", [128, 2, 32, 132], BF16))
        usb = _st.enter_context(nc.sbuf_tensor("p2_u", [128, 2, 32, 132], BF16))
        betarow = _st.enter_context(nc.sbuf_tensor("p2_brow", [1, 2, T], F32))
        bTsb = _st.enter_context(nc.sbuf_tensor("p2_bT", [128, 32, 8], F32))
        idsb = _st.enter_context(nc.sbuf_tensor("p2_id", [128, 128], F32))
        idbsb = _st.enter_context(nc.sbuf_tensor("p2_idb", [128, 128], BF16))
        dstrsb = _st.enter_context(nc.sbuf_tensor("p2_dstr", [128, 128], F32))
        dincsb = _st.enter_context(nc.sbuf_tensor("p2_dinc", [128, 128], F32))
        esb = _st.enter_context(nc.sbuf_tensor("p2_e", [128, 2, 512], BF16))
        edsb = _st.enter_context(nc.sbuf_tensor("p2_ed", [128, 128], BF16))
        ed2sb = _st.enter_context(nc.sbuf_tensor("p2_ed2", [128, 128], BF16))
        zsb = _st.enter_context(nc.sbuf_tensor("p2_z", [128, 128], F32))
        densb = _st.enter_context(nc.sbuf_tensor("p2_den", [128, 4], F32))
        nsb = _st.enter_context(nc.sbuf_tensor("p2_n", [128, 128], BF16))
        Ssb = _st.enter_context(nc.sbuf_tensor("p2_S", [128, 2, 128], BF16))
        Psb = _st.enter_context(nc.sbuf_tensor("p2_P", [128, 2, 128], BF16))
        ysb = _st.enter_context(nc.sbuf_tensor("p2_y", [128, 2, 128], BF16))
        osb = _st.enter_context(nc.sbuf_tensor("p2_o", [128, 128], BF16))
        otsb = _st.enter_context(nc.sbuf_tensor("p2_ot", [128, 2, 128], BF16))
        zosb = _st.enter_context(nc.sbuf_tensor("p2_zo", [128, 132], BF16))
        dbgT = _st.enter_context(nc.sbuf_tensor("p2_dbgT", [128, 128], F32))
        dbgD = _st.enter_context(nc.sbuf_tensor("p2_dbgD", [128, 8], F32))
        psE0 = _st.enter_context(nc.psum_tensor("p2_psE0", [128, 512], F32))
        psE1 = _st.enter_context(nc.psum_tensor("p2_psE1", [128, 512], F32))
        psM = _st.enter_context(nc.psum_tensor("p2_psM", [128, 132], F32))
        psA = _st.enter_context(nc.psum_tensor("p2_psA", [128, 128], F32))
        psT = _st.enter_context(nc.psum_tensor("p2_psT", [128, 128], BF16))
        blk = _st.enter_context(nc.Block("p2"))
        psE = (psE0, psE1)

        @blk.tensor
        def _(t):
            ctx.wait_mark(t, "p2_ev", "init")
            ctx.wait_mark(t, "p2_gld", "done")
            for h in range(P2_NH):
                hp = h % 2
                nld = 16 * (h // 2 + 1)
                t.wait_ge(ctx.sem(f"p2_ldk{hp}"), nld)
                t.wait_ge(ctx.sem(f"p2_ldq{hp}"), nld)
                kh = ksb[:, hp, :]
                qh = qsb[:, hp, :]
                for i in range(P2_NS):
                    isl = slice(i * 128, (i + 1) * 128)
                    for phase in (0, 1):
                        rh = kh if phase == 0 else qh
                        if phase == 0 and i > 0:
                            ctx.wait_mark(t, "p2_ev", ("yk", h, i - 1, 6))
                            ctx.wait_mark(t, "p2_ev", ("ob", h, i - 1))
                        if phase == 1:
                            ctx.wait_mark(t, "p2_ev", ("yk", h, i, 6))
                            ctx.wait_mark(t, "p2_ev", ("y0", h, i))
                        first_mm2 = True
                        lm_here = last_m[(h, i, phase)]
                        for (g, _h, _i, _ph, gl) in by_hi.get((h, i, phase), []):
                            pe = psE[g % 2]
                            if g >= 2:
                                ctx.wait_mark(t, "p2_ex", ("x", g - 2))
                            for a, j in enumerate(gl):
                                inst = t.matmul(pe[:, a * 128:(a + 1) * 128],
                                                kh[:, j * 128:(j + 1) * 128],
                                                rh[:, isl],
                                                start=True, stop=True)
                            ctx.inc(inst, "p2_mm", 1)
                            ctx.mark("p2_mm", ("e", g))
                            ctx.wait_mark(t, "p2_ex", ("x", g))
                            for a, j in enumerate(gl):
                                inst = t.matmul(
                                    psM[:, 0:129],
                                    esb[:, g % 2, a * 128:(a + 1) * 128],
                                    usb[:, hp, j, 0:129],
                                    start=first_mm2, stop=False)
                                first_mm2 = False
                            ctx.inc(inst, "p2_mm", 1)
                            ctx.mark("p2_mm", ("m", g))
                        if phase == 0:
                            if i > 0:
                                ctx.wait_mark(t, "p2_ev", ("zd2", h, i - 1))
                            inst = t.matmul(psA[:], kh[:, isl], kh[:, isl],
                                            start=True, stop=True)
                            ctx.inc(inst, "p2_mm", 1)
                            ctx.mark("p2_mm", ("ed", h, i))
                            ctx.wait_mark(t, "p2_ex", ("xd", h, i))
                            inst = t.matmul(psM[:, 0:129], edsb[:],
                                            zosb[:, 0:129],
                                            start=first_mm2, stop=True)
                            ctx.inc(inst, "p2_mm", 1)
                            ctx.mark("p2_mm", ("md", h, i))
                            if i > 0:
                                ctx.wait_mark(t, "p2_ev", ("oe", h, i - 1))
                            inst = t.transpose(psT[:], edsb[:], idbsb[:])
                            ctx.inc(inst, "p2_mm", 1)
                            ctx.mark("p2_mm", ("edt", h, i))
                            ctx.wait_mark(t, "p2_ev", ("pb", h, i))
                            inst = t.transpose(psT[:], Psb[:, 1, :], idbsb[:])
                            ctx.inc(inst, "p2_mm", 1)
                            ctx.mark("p2_mm", ("ntr", h, i))
                            ctx.wait_mark(t, "p2_ev", ("y0", h, i))
                            ctx.wait_mark(t, "p2_ev", ("nb", h, i))
                            inst = t.matmul(psA[:], nsb[:], ysb[:, 0, :],
                                            start=True, stop=True)
                            ctx.inc(inst, "p2_mm", 1)
                            ctx.mark("p2_mm", ("ap", h, i, 0))
                            for k in range(1, 7):
                                Sold = nsb[:] if k == 1 else Ssb[:, (k - 1) % 2, :]
                                Pold = (Psb[:, 1, :] if k == 1
                                        else Psb[:, (k - 1) % 2, :])
                                ctx.wait_mark(t, "p2_ev", ("yk", h, i, k - 1))
                                inst = t.matmul(psA[:], Pold, Sold,
                                                start=True, stop=True)
                                ctx.inc(inst, "p2_mm", 1)
                                ctx.mark("p2_mm", ("sq", h, i, k))
                                ctx.wait_mark(t, "p2_ev", ("sqe", h, i, k))
                                if k < 6:
                                    inst = t.matmul(psA[:], Sold, Pold,
                                                    start=True, stop=True)
                                    ctx.inc(inst, "p2_mm", 1)
                                    ctx.mark("p2_mm", ("sqp", h, i, k))
                                    ctx.wait_mark(t, "p2_ev", ("spe", h, i, k))
                                inst = t.matmul(psA[:], Ssb[:, k % 2, :],
                                                ysb[:, k % 2, :],
                                                start=True, stop=True)
                                ctx.inc(inst, "p2_mm", 1)
                                ctx.mark("p2_mm", ("ap", h, i, k))
                        else:
                            inst = t.matmul(psA[:], kh[:, isl], qh[:, isl],
                                            start=True, stop=True)
                            ctx.inc(inst, "p2_mm", 1)
                            ctx.mark("p2_mm", ("ed2", h, i))
                            ctx.wait_mark(t, "p2_ex", ("xd2", h, i))
                            inst = t.matmul(psM[:, 0:129], ed2sb[:],
                                            usb[:, hp, i, 0:129],
                                            start=first_mm2, stop=True)
                            ctx.inc(inst, "p2_mm", 1)
                            ctx.mark("p2_mm", ("m2", h, i))
                            ctx.wait_mark(t, "p2_ev", ("obn", h, i))
                            inst = t.transpose(psT[:], osb[:], idbsb[:])
                            ctx.inc(inst, "p2_mm", 1)
                            ctx.mark("p2_mm", ("ot", h, i))


        @blk.scalar
        def _(s):
            ctx.wait_mark(s, "p2_ev", "init")
            def group_exps(s, h, i, phase):
                for (g, _h, _i, _ph, gl) in by_hi.get((h, i, phase), []):
                    pe = psE[g % 2]
                    ctx.wait_mark(s, "p2_mm", ("e", g))
                    if g >= 2:
                        ctx.wait_mark(s, "p2_mm", ("m", g - 2))
                    inst = s.activation(
                        out=esb[:, g % 2, 0:len(gl) * 128],
                        in_=pe[:, 0:len(gl) * 128],
                        func=AF.Exp, scale=SC0)
                    ctx.inc(inst, "p2_ex", 1)
                    ctx.mark("p2_ex", ("x", g))

            def diag_exp(s, h, i, tag, ztag):
                ctx.wait_mark(s, "p2_ev", (ztag, h, i))
                if tag == "xd" and i > 0:
                    ctx.wait_mark(s, "p2_mm", ("edt", h, i - 1))
                if tag == "xd2" and i > 0:
                    ctx.wait_mark(s, "p2_mm", ("m2", h, i - 1))
                dst = edsb if tag == "xd" else ed2sb
                inst = s.activation(out=dst[:], in_=zsb[:],
                                    func=AF.Exp, scale=SC0)
                ctx.inc(inst, "p2_ex", 1)
                ctx.mark("p2_ex", (tag, h, i))

            for h in range(P2_NH):
                for i in range(P2_NS):
                    group_exps(s, h, i, 0)
                    diag_exp(s, h, i, "xd", "zd")
                    group_exps(s, h, i, 1)
                    diag_exp(s, h, i, "xd2", "zd2")

        @blk.vector
        def _(v):
            inst = v.memset(zosb[:, :], 0.0)
            ctx.inc(inst, "p2_vs", 1)
            v.wait_ge(ctx.sem("p2_vs"), ctx.cnt["p2_vs"])
            inst = v.memset(zosb[:, 128:132], 1.0)
            ctx.inc(inst, "p2_ev", 1)
            ctx.mark("p2_ev", "init")
            v.wait_ge(ctx.sem("p2_ld"), 64)
            ot_ctr = 0
            for h in range(P2_NH):
                hp = h % 2
                if h >= 2:
                    ctx.wait_mark(v, "p2_mm", ("ot", h - 2, P2_NS - 1))
                inst = v.memset(usb[:, hp, :, 128:132], 1.0)
                ctx.inc(inst, "p2_ev", 1)
                ctx.mark("p2_ev", ("uones", h))
                v.wait_ge(ctx.sem(f"p2_ldv{hp}"), 16 * (h // 2 + 1))
                v.wait_ge(ctx.sem(f"p2_ldb{hp}"), 16 * (h // 2 + 1))
                for i in range(P2_NS):
                    isl = slice(i * 128, (i + 1) * 128)
                    ctx.wait_mark(v, "p2_mm", ("ed", h, i))
                    if i > 0:
                        ctx.wait_mark(v, "p2_ex", ("xd2", h, i - 1))
                    inst = v.tensor_sub(zsb[:], psA[:], dstrsb[:])
                    ctx.inc(inst, "p2_ev", 1)
                    ctx.mark("p2_ev", ("zd", h, i))
                    ctx.wait_mark(v, "p2_mm", ("md", h, i))
                    if P2_DEBUG and h == 0 and i == P2_DBG_I:
                        v.tensor_copy(dbgT[:], psT[:])
                    inst = v.tensor_scalar_max(densb[:, 1:2],
                                               psM[:, 128:129], 1e-30)
                    ctx.inc(inst, "p2_vs", 1)
                    v.wait_ge(ctx.sem("p2_vs"), ctx.cnt["p2_vs"])
                    inst = v.reciprocal(densb[:, 2:3], densb[:, 1:2])
                    ctx.inc(inst, "p2_vs", 1)
                    v.wait_ge(ctx.sem("p2_vs"), ctx.cnt["p2_vs"])
                    v.tensor_scalar(densb[:, 3:4], densb[:, 2:3],
                                    bTsb[:, i, h:h + 1], -1.0,
                                    ALU.mult, ALU.mult)
                    inst = v.tensor_scalar(densb[:, 0:1], densb[:, 2:3],
                                           bTsb[:, i, h:h + 1], None,
                                           ALU.mult)
                    ctx.inc(inst, "p2_ev", 1)
                    ctx.mark("p2_ev", ("sc2", h, i))
                    ctx.wait_mark(v, "p2_mm", ("edt", h, i))
                    ctx.wait_mark(v, "p2_ev", ("sc2", h, i))
                    if P2_DEBUG and h == 0 and i == P2_DBG_I:
                        v.tensor_copy(dbgD[:, 0:4], densb[:])
                    inst = v.tensor_scalar(Psb[:, 1, :], psT[:],
                                           densb[:, 0:1], None, ALU.mult)
                    ctx.inc(inst, "p2_ev", 1)
                    ctx.mark("p2_ev", ("pb", h, i))
                    ctx.wait_mark(v, "p2_mm", ("ntr", h, i))
                    inst = v.tensor_copy(nsb[:], psT[:])
                    ctx.inc(inst, "p2_ev", 1)
                    ctx.mark("p2_ev", ("nb", h, i))
                    ctx.wait_mark(v, "p2_ev", ("uones", h))
                    lm0 = last_m[(h, i, 0)]
                    if lm0 is not None:
                        ctx.wait_mark(v, "p2_ev", ("sc2", h, i))
                        inst = v.scalar_tensor_tensor(
                            ysb[:, 0, :], psM[:, 0:128], densb[:, 3:4],
                            vsb[:, hp, i, 0:128], ALU.mult, ALU.add)
                    else:
                        inst = v.tensor_copy(ysb[:, 0, :],
                                             vsb[:, hp, i, 0:128])
                    ctx.inc(inst, "p2_ev", 1)
                    ctx.mark("p2_ev", ("y0", h, i))
                    ctx.wait_mark(v, "p2_mm", ("ap", h, i, 0))
                    inst = v.tensor_sub(ysb[:, 1, :], ysb[:, 0, :], psA[:])
                    ctx.inc(inst, "p2_ev", 1)
                    ctx.mark("p2_ev", ("yk", h, i, 0))
                    for k in range(1, 7):
                        ctx.wait_mark(v, "p2_mm", ("sq", h, i, k))
                        inst = v.tensor_copy(Ssb[:, k % 2, :], psA[:])
                        ctx.inc(inst, "p2_ev", 1)
                        ctx.mark("p2_ev", ("sqe", h, i, k))
                        if k < 6:
                            ctx.wait_mark(v, "p2_mm", ("sqp", h, i, k))
                            inst = v.tensor_copy(Psb[:, k % 2, :], psA[:])
                            ctx.inc(inst, "p2_ev", 1)
                            ctx.mark("p2_ev", ("spe", h, i, k))
                        ctx.wait_mark(v, "p2_mm", ("ap", h, i, k))
                        dst = (usb[:, hp, i, 0:128] if k == 6
                               else ysb[:, (k + 1) % 2, :])
                        inst = v.tensor_add(dst, ysb[:, k % 2, :], psA[:])
                        ctx.inc(inst, "p2_ev", 1)
                        ctx.mark("p2_ev", ("yk", h, i, k))
                    ctx.wait_mark(v, "p2_mm", ("ed2", h, i))
                    ctx.wait_mark(v, "p2_ex", ("xd", h, i))
                    inst = v.tensor_sub(zsb[:], psA[:], dincsb[:])
                    ctx.inc(inst, "p2_ev", 1)
                    ctx.mark("p2_ev", ("zd2", h, i))
                    ctx.wait_mark(v, "p2_mm", ("m2", h, i))
                    if ot_ctr >= 2:
                        ctx.wait(v, f"p2_st{ot_ctr % 2}",
                                 at=16 * (ot_ctr // 2))
                    inst = v.reciprocal(densb[:, 2:3], psM[:, 128:129])
                    ctx.inc(inst, "p2_vs", 1)
                    v.wait_ge(ctx.sem("p2_vs"), ctx.cnt["p2_vs"])
                    inst = v.tensor_scalar(osb[:], psM[:, 0:128],
                                           densb[:, 2:3], None, ALU.mult)
                    ctx.inc(inst, "p2_ev", 1)
                    ctx.mark("p2_ev", ("ob", h, i))
                    ctx.mark("p2_ev", ("obn", h, i))
                    ctx.wait_mark(v, "p2_mm", ("ot", h, i))
                    inst = v.tensor_copy(otsb[:, ot_ctr % 2, :], psT[:])
                    ctx.inc(inst, "p2_ev", 1)
                    ctx.mark("p2_ev", ("oe", h, i))
                    ot_ctr += 1

        @blk.gpsimd
        def _(g):
            ctx.inc(g.dma_start(out=idbsb[:], in_=ident[:, :]), "p2_gld", 16)
            ctx.mark("p2_gld", "done")
            if P2_DEBUG:
                dbg = io["dbg_s"]
                ctx.wait_mark(g, "p2_ev", ("oe", P2_NH - 1, P2_NS - 1))
                for a, (dsl, srcap) in enumerate([
                        (slice(0, 132), usb[:, 0, 0, :]),
                        (slice(132, 264), usb[:, 0, 1, :]),
                        (slice(264, 268), densb[:]),
                        (slice(268, 396), edsb[:]),
                        (slice(396, 524), nsb[:]),
                        (slice(524, 652), ysb[:, 0, :]),
                        (slice(652, 780), ed2sb[:]),
                        (slice(1036, 1164), Psb[:, 1, :]),
                        (slice(1164, 1292), Ssb[:, 0, :]),
                        (slice(1292, 1420), Ssb[:, 1, :]),
                        (slice(1420, 1548), ysb[:, 1, :]),
                        (slice(1548, 1676), Psb[:, 0, :]),
                        (slice(1676, 1804), dbgT[:]),
                        (slice(1804, 1812), dbgD[:]),
                ]):
                    ctx.inc(g.dma_start(out=dbg[:, dsl], in_=srcap),
                            "p2_gld", 16)
                ctx.mark("p2_gld", "dbg")

        @blk.sync
        def _(s):
            ctx.inc(s.dma_start(out=idsb[:], in_=ident[:, :]), "p2_ld", 16)
            ctx.inc(s.dma_start(out=dstrsb[:], in_=dstrict[:, :]), "p2_ld", 16)
            ctx.inc(s.dma_start(out=dincsb[:], in_=dincl[:, :]), "p2_ld", 16)
            ctx.inc(s.dma_start(
                out=bTsb[:, :, :],
                in_=betaT_s.rearrange("(s p) e -> p s e", p=128)[:, :, :]),
                "p2_ld", 16)
            ot_ctr = 0
            for h in range(P2_NH):
                hp = h % 2
                if h >= 2:
                    ctx.wait_mark(s, "p2_mm", ("ot", h - 2, P2_NS - 1))
                nld = 16 * (h // 2 + 1)
                ctx.inc(s.dma_start(out=ksb[:, hp, :],
                                    in_=kT_s[h * 128:(h + 1) * 128, :]),
                        f"p2_ldk{hp}", 16)
                ctx.inc(s.dma_start(out=qsb[:, hp, :],
                                    in_=qT_s[h * 128:(h + 1) * 128, :]),
                        f"p2_ldq{hp}", 16)
                vr = v_s[:, h, :].rearrange("(s p) w -> p s w", p=128)
                ctx.inc(s.dma_start(out=vsb[:, hp, :, :], in_=vr[:, :, :]),
                        f"p2_ldv{hp}", 16)
                ctx.inc(s.dma_start(out=betarow[0:1, hp, :],
                                    in_=beta_s[h:h + 1, :]),
                        f"p2_ldb{hp}", 16)
                if h >= 1:
                    for i in range(P2_NS):
                        ctx.wait_mark(s, "p2_ev", ("oe", h - 1, i))
                        ctx.inc(s.dma_start(
                            out=oT0_s[(h - 1) * 128:h * 128,
                                      i * 128:(i + 1) * 128],
                            in_=otsb[:, ot_ctr % 2, :]),
                            f"p2_st{ot_ctr % 2}", 16)
                        ot_ctr += 1
            for i in range(P2_NS):
                ctx.wait_mark(s, "p2_ev", ("oe", P2_NH - 1, i))
                ctx.inc(s.dma_start(
                    out=oT0_s[(P2_NH - 1) * 128:P2_NH * 128,
                              i * 128:(i + 1) * 128],
                    in_=otsb[:, ot_ctr % 2, :]),
                    f"p2_st{ot_ctr % 2}", 16)
                ot_ctr += 1
            ctx.wait(s, "p2_st0")
            ctx.wait(s, "p2_st1")
            if P2_DEBUG:
                ctx.wait_mark(s, "p2_gld", "dbg")



def emit_post(nc, ctx, io, cfg):
    P = cfg["pref"]
    oT = io.get(cfg.get("oT"))
    w = io.get(cfg.get("w"))
    bias = io.get(cfg.get("bias"))
    resid = io[cfg["resid"]]
    pre = io.get(cfg.get("pre"))
    out_T = io.get(cfg.get("out_T"))
    ident = io["ident"]
    NTS = NT // 128

    from contextlib import ExitStack
    with ExitStack() as _st:
        if oT is not None:
            wsb = _st.enter_context(nc.sbuf_tensor(P + "w", [128, 16, DM], BF16))
            osb = _st.enter_context(nc.sbuf_tensor(P + "o", [128, 16, NT], BF16))
        xr = _st.enter_context(nc.sbuf_tensor(P + "xr", [128, 2, DM], F32))
        xp = (_st.enter_context(nc.sbuf_tensor(P + "xp", [128, 2, DM], F32))
              if pre is not None else None)
        xsum = _st.enter_context(nc.sbuf_tensor(P + "xs", [128, 2, DM], F32))
        sqalias = cfg.get("sqalias", False)
        sq = (None if sqalias else
              _st.enter_context(nc.sbuf_tensor(P + "sq", [128, DM], F32)))
        stat = _st.enter_context(nc.sbuf_tensor(P + "stt", [128, 8], F32))
        x1f = _st.enter_context(nc.sbuf_tensor(P + "f", [128, 2, DM], F32))
        x1b = _st.enter_context(nc.sbuf_tensor(P + "b", [128, 2, DM], BF16))
        xTb = _st.enter_context(nc.sbuf_tensor(P + "tbb", [128, 2, 16, 128],
                                               BF16))
        idb = _st.enter_context(nc.sbuf_tensor(P + "id", [128, 128], BF16))
        on1 = (_st.enter_context(nc.sbuf_tensor(P + "on", [1, 128], F32))
               if bias is not None else None)
        br = (_st.enter_context(nc.sbuf_tensor(P + "brr", [1, DM], F32))
              if bias is not None else None)
        ps0 = _st.enter_context(nc.psum_tensor(P + "p0", [128, 512], F32))
        ps1 = _st.enter_context(nc.psum_tensor(P + "p1", [128, 512], F32))
        psT0 = _st.enter_context(nc.psum_tensor(P + "t0", [128, 128], BF16))
        psT1 = _st.enter_context(nc.psum_tensor(P + "t1", [128, 128], BF16))
        psB = (_st.enter_context(nc.psum_tensor(P + "pbb", [128, DM], F32))
               if bias is not None else None)
        blk = _st.enter_context(nc.Block(P))
        psAB = (ps0, ps1)
        psTT = (psT0, psT1)

        def store_ts(s, ts):
            par = ts % 2
            ctx.wait_mark(s, P + "ev", ("lnf", ts))
            ctx.inc(s.dma_start(out=io[cfg["out_f32"]][
                ts * 128:(ts + 1) * 128, :], in_=x1f[:, par, :]),
                P + f"s{par}", 16)
            ctx.mark(P + f"s{par}", ("sf", ts))
            if out_T is not None:
                ctx.wait_mark(s, P + "ev", ("tb", ts, 15))
                dstcol = out_T[:, ts * 128:(ts + 1) * 128].rearrange(
                    "(s p) t -> p s t", p=128)
                ctx.inc(s.dma_start(out=dstcol[:, :, :],
                                    in_=xTb[:, par, :, :]), P + f"q{par}", 16)
                ctx.mark(P + f"q{par}", ("qf", ts))

        @blk.gpsimd
        def _(g):
            ctx.inc(g.dma_start(out=idb[:], in_=ident[:, :]), P + "gld", 16)
            if oT is not None:
                for s_ in range(16):
                    ctx.inc(g.dma_start(out=wsb[:, s_, :],
                                        in_=w[s_ * 128:(s_ + 1) * 128, :]),
                            P + "gld", 16)
            ctx.mark(P + "gld", "done")

        @blk.sync
        def _(s):
            if bias is not None:
                ctx.inc(s.dma_start(out=br[:], in_=bias[:, :]), P + "c", 16)
            if oT is not None:
                for ds in range(16):
                    ctx.inc(s.dma_start(out=osb[:, ds, :],
                                        in_=oT[ds * 128:(ds + 1) * 128, :]),
                            P + "c", 16)
            ctx.mark(P + "c", "done")
            for ts in range(NTS):
                par = ts % 2
                if ts >= 2:
                    ctx.wait_mark(s, P + "ev", ("lnf", ts - 2))
                ctx.inc(s.dma_start(out=xr[:, par, :],
                                    in_=resid[ts * 128:(ts + 1) * 128, :]),
                        P + f"x{par}", 16)
                if pre is not None:
                    ctx.inc(s.dma_start(out=xp[:, par, :],
                                        in_=pre[ts * 128:(ts + 1) * 128, :]),
                            P + f"pp{par}", 16)
                if ts >= 1:
                    store_ts(s, ts - 1)
            store_ts(s, NTS - 1)
            for nm in (P + "s0", P + "s1") + (
                    (P + "q0", P + "q1") if out_T is not None else ()):
                ctx.wait(s, nm)

        @blk.tensor
        def _(t):
            ctx.wait_mark(t, P + "gld", "done")
            ctx.wait_mark(t, P + "c", "done")
            if bias is not None:
                ctx.wait_mark(t, P + "ev", "on1")
                for mc in range(4):
                    inst = t.matmul(psB[:, mc * 512:(mc + 1) * 512],
                                    on1[0:1, :],
                                    br[0:1, mc * 512:(mc + 1) * 512],
                                    start=True, stop=True)
                    ctx.inc(inst, P + "mm", 1)
                    ctx.mark(P + "mm", ("ob", mc))
            for ts in range(NTS):
                if oT is not None:
                    for mc in range(4):
                        j = ts * 4 + mc
                        ps = psAB[j % 2]
                        if j >= 2:
                            ctx.wait_mark(t, P + "ev",
                                          ("ev", (j - 2) // 4, (j - 2) % 4))
                        for ds in range(16):
                            inst = t.matmul(
                                ps[:], osb[:, ds, ts * 128:(ts + 1) * 128],
                                wsb[:, ds, mc * 512:(mc + 1) * 512],
                                start=(ds == 0), stop=(ds == 15))
                        ctx.inc(inst, P + "mm", 1)
                        ctx.mark(P + "mm", ("mm", ts, mc))
                ctx.wait_mark(t, P + "ev", ("lnb", ts))
                for cs in range(16):
                    j2 = ts * 16 + cs
                    if j2 >= 2:
                        ctx.wait_mark(t, P + "ev", ("tb", (j2 - 2) // 16,
                                                    (j2 - 2) % 16))
                    inst = t.transpose(psTT[j2 % 2][:],
                                       x1b[:, ts % 2, cs * 128:(cs + 1) * 128],
                                       idb[:])
                    ctx.inc(inst, P + "mm", 1)
                    ctx.mark(P + "mm", ("tp", ts, cs))

        @blk.vector
        def _(v):
            def vsync(inst):
                ctx.inc(inst, P + "vs", 1)
                v.wait_ge(ctx.sem(P + "vs"), ctx.cnt[P + "vs"])
            if bias is not None:
                inst = v.memset(on1[:], 1.0)
                ctx.inc(inst, P + "ev", 1)
                ctx.mark(P + "ev", "on1")
            for ts in range(NTS):
                par = ts % 2
                v.wait_ge(ctx.sem(P + f"x{par}"), 16 * (ts // 2 + 1))
                if pre is not None:
                    v.wait_ge(ctx.sem(P + f"pp{par}"), 16 * (ts // 2 + 1))
                for mc in range(4):
                    csl = slice(mc * 512, (mc + 1) * 512)
                    if ts >= 2 and mc == 0:
                        ctx.wait_mark(v, P + "ex", ("sq", ts - 2))
                    if oT is not None:
                        j = ts * 4 + mc
                        ctx.wait_mark(v, P + "mm", ("mm", ts, mc))
                        if bias is not None:
                            vsync(v.tensor_add(xsum[:, par, csl],
                                               psAB[j % 2][:],
                                               xr[:, par, csl]))
                            inst = v.tensor_add(xsum[:, par, csl],
                                                xsum[:, par, csl],
                                                psB[:, csl])
                        else:
                            inst = v.tensor_add(xsum[:, par, csl],
                                                psAB[j % 2][:],
                                                xr[:, par, csl])
                    else:
                        inst = v.tensor_add(xsum[:, par, csl],
                                            xp[:, par, csl], xr[:, par, csl])
                    ctx.inc(inst, P + "ev", 1)
                    ctx.mark(P + "ev", ("ev", ts, mc))
                ctx.wait(v, P + "ev")
                vsync(v.reduce_sum(stat[:, 0:1], xsum[:, par, :], axis=AX))
                if ts >= 1:
                    ctx.wait_mark(v, P + "ex", ("sq", ts - 1))
                v.tensor_scalar(stat[:, 1:2], stat[:, 0:1], 1.0 / DM, None,
                                ALU.mult)
                inst = v.tensor_scalar(stat[:, 2:3], stat[:, 0:1],
                                       -1.0 / DM, None, ALU.mult)
                ctx.inc(inst, P + "ev", 1)
                ctx.mark(P + "ev", ("mn", ts))
                ctx.wait_mark(v, P + "ex", ("sq", ts))
                inst = v.tensor_scalar(stat[:, 4:5], stat[:, 3:4], 1.0 / DM,
                                       EPS, ALU.mult, ALU.add)
                ctx.inc(inst, P + "ev", 1)
                ctx.mark(P + "ev", ("veps", ts))
                ctx.wait_mark(v, P + "ex", ("sd", ts))
                vsync(v.reciprocal(stat[:, 6:7], stat[:, 5:6]))
                if ts >= 2:
                    ctx.wait_mark(v, P + f"s{par}", ("sf", ts - 2))
                inst = v.tensor_scalar(x1f[:, par, :], xsum[:, par, :],
                                       stat[:, 1:2], stat[:, 6:7],
                                       ALU.subtract, ALU.mult)
                ctx.inc(inst, P + "ev", 1)
                ctx.mark(P + "ev", ("lnf", ts))
                if ts >= 2:
                    ctx.wait_mark(v, P + "mm", ("tp", ts - 2, 15))
                ctx.wait(v, P + "ev")
                inst = v.tensor_copy(x1b[:, par, :], x1f[:, par, :])
                ctx.inc(inst, P + "ev", 1)
                ctx.mark(P + "ev", ("lnb", ts))
                for cs in range(16):
                    ctx.wait_mark(v, P + "mm", ("tp", ts, cs))
                    if out_T is not None and ts >= 2 and cs == 0:
                        ctx.wait_mark(v, P + f"q{par}", ("qf", ts - 2))
                    inst = v.tensor_copy(xTb[:, par, cs, :],
                                         psTT[(ts * 16 + cs) % 2][:])
                    ctx.inc(inst, P + "ev", 1)
                    ctx.mark(P + "ev", ("tb", ts, cs))

        @blk.scalar
        def _(s):
            for ts in range(NTS):
                par = ts % 2
                ctx.wait_mark(s, P + "ev", ("mn", ts))
                if sqalias:
                    if ts >= 2:
                        ctx.wait_mark(s, P + f"s{par}", ("sf", ts - 2))
                    sqout = x1f[:, par, :]
                else:
                    sqout = sq[:]
                inst = s.activation(out=sqout, in_=xsum[:, par, :],
                                    func=AF.Square, bias=stat[:, 2:3],
                                    accum_out=stat[:, 3:4])
                ctx.inc(inst, P + "ex", 1)
                ctx.mark(P + "ex", ("sq", ts))
                ctx.wait_mark(s, P + "ev", ("veps", ts))
                inst = s.activation(out=stat[:, 5:6], in_=stat[:, 4:5],
                                    func=AF.Sqrt)
                ctx.inc(inst, P + "ex", 1)
                ctx.mark(P + "ex", ("sd", ts))



def p4_block_schedule():
    blocks = []
    for j in range(22):
        blocks.append(("gw", HS + j * 128, j * 128, j, None))
    for e in range(NEXP):
        for j in range(11):
            blocks.append(("up", j * 128, HF + j * 128, 22 + e * 11 + j, e))
    return blocks


def stage_wblk(inp):
    gwT = inp["L0_ffn_shared_gw"].T
    upT = inp["L0_ffn_up_w"].transpose(0, 2, 1)
    out = np.empty((128, NHS, 2, 16, 128), np.float32)
    for (wsrc, gcol, ucol, hsidx, e) in p4_block_schedule():
        src = gwT if wsrc == "gw" else upT[e]
        for gi, col in ((0, gcol), (1, ucol)):
            t = src[:, col:col + 128].reshape(16, 128, 128)
            out[:, hsidx, gi, :, :] = t.transpose(1, 0, 2)
    return np.ascontiguousarray(out)


def emit_p4(nc, ctx, io, ten):
    x1T, wblk = io["x1T_s"], io["wblk"]
    gateT, gbias = io["gateT"], io["gbias"]
    iota8, iotaB8, ident = io["iota8"], io["iotaB8"], io["ident"]
    hacts = io["hacts_s"]
    NTS = NT // 128

    blocks = p4_block_schedule()
    if P4_LIMIT is not None:
        blocks = [b for b in blocks if P4_LIMIT(b)]
    presentE = sorted({b[4] for b in blocks if b[4] is not None})
    eidx = {e: i for i, e in enumerate(presentE)}

    from contextlib import ExitStack
    with ExitStack() as _st:
        xsb = _st.enter_context(nc.sbuf_tensor("p4x", [128, 16, NT], BF16))
        gsb = _st.enter_context(nc.sbuf_tensor("p4g", [128, 16, 8], BF16))
        wg = _st.enter_context(nc.sbuf_tensor("p4wg", [128, 2, 16, 128], BF16))
        wu = _st.enter_context(nc.sbuf_tensor("p4wu", [128, 2, 16, 128], BF16))
        idsb = _st.enter_context(nc.sbuf_tensor("p4id", [128, 128], F32))
        on1 = _st.enter_context(nc.sbuf_tensor("p4on", [1, 128], F32))
        gbrow = _st.enter_context(nc.sbuf_tensor("p4gb", [1, 8], F32))
        iorow = _st.enter_context(nc.sbuf_tensor("p4io", [1, 16], F32))
        scoT = _st.enter_context(nc.sbuf_tensor("p4sc", [8, NT], F32))
        weTt = _st.enter_context(nc.sbuf_tensor("p4we", [8, 2, 128], F32))
        werow = _st.enter_context(nc.sbuf_tensor("p4wr", [1, 2, NT], F32))
        we_d = nc.dram_tensor("p4we_d", [8, NT], F32, kind="Internal").ap()
        gbb = _st.enter_context(nc.sbuf_tensor("p4gbb", [128, 8], F32))
        iob = _st.enter_context(nc.sbuf_tensor("p4iob", [128, 16], F32))
        rt = _st.enter_context(nc.sbuf_tensor("p4rt", [128, 12, 8], F32))
        rst = _st.enter_context(nc.sbuf_tensor("p4rst", [128, 8], F32))
        webc = _st.enter_context(nc.sbuf_tensor("p4wb", [128, NT], F32))
        t1 = _st.enter_context(nc.sbuf_tensor("p4t1", [128, 2, 512], F32))
        t2 = _st.enter_context(nc.sbuf_tensor("p4t2", [128, 2, 512], F32))
        hbuf = _st.enter_context(nc.sbuf_tensor("p4h", [128, 2, 512], BF16))
        psG0 = _st.enter_context(nc.psum_tensor("p4pg0", [128, 512], F32))
        psG1 = _st.enter_context(nc.psum_tensor("p4pg1", [128, 512], F32))
        psU0 = _st.enter_context(nc.psum_tensor("p4pu0", [128, 512], F32))
        psU1 = _st.enter_context(nc.psum_tensor("p4pu1", [128, 512], F32))
        psL = _st.enter_context(nc.psum_tensor("p4pl", [8, 512], F32))
        psR = _st.enter_context(nc.psum_tensor("p4pr", [128, 16], F32))
        psR2 = _st.enter_context(nc.psum_tensor("p4pr2", [8, 128], F32))
        psW = _st.enter_context(nc.psum_tensor("p4pw", [128, 512], F32))
        blk = _st.enter_context(nc.Block("p4"))
        psG = (psG0, psG1)
        psU = (psU0, psU1)

        @blk.gpsimd
        def _(g):
            for s_ in range(16):
                ctx.inc(g.dma_start(out=gsb[:, s_, :],
                                    in_=gateT[s_ * 128:(s_ + 1) * 128, :]),
                        "p4gld", 16)
            ctx.mark("p4gld", "gate")
            for bi, (wsrc, gcol, ucol, hsidx, e) in enumerate(blocks):
                par = bi % 2
                if bi >= 2:
                    ctx.wait_mark(g, "p4mm", ("u", bi - 2, 3))
                ctx.inc(g.dma_start(out=wg[:, par, :, :],
                                    in_=wblk[:, hsidx, 0, :, :]),
                        f"p4w{par}", 16)
                ctx.inc(g.dma_start(out=wu[:, par, :, :],
                                    in_=wblk[:, hsidx, 1, :, :]),
                        f"p4w{par}", 16)

        @blk.sync
        def _(s):
            ctx.inc(s.dma_start(out=idsb[:], in_=ident[:, :]), "p4c", 16)
            ctx.inc(s.dma_start(out=gbrow[:], in_=gbias[:, :]), "p4c", 16)
            ctx.inc(s.dma_start(out=iorow[0:1, 0:8], in_=iota8[:, :]),
                    "p4c", 16)
            ctx.inc(s.dma_start(out=iorow[0:1, 8:16], in_=iotaB8[:, :]),
                    "p4c", 16)
            for ds in range(16):
                ctx.inc(s.dma_start(out=xsb[:, ds, :],
                                    in_=x1T[ds * 128:(ds + 1) * 128, :]),
                        "p4c", 16)
            ctx.mark("p4c", "done")
            for ts in range(NTS):
                ctx.wait_mark(s, "p4ev", ("rt1", ts))
                ctx.inc(s.dma_start(out=we_d[:, ts * 128:(ts + 1) * 128],
                                    in_=weTt[:, ts % 2, :]),
                        f"p4wd{ts % 2}", 16)
            ctx.wait(s, "p4wd0")
            ctx.wait(s, "p4wd1")
            for i_, e in enumerate(presentE):
                if i_ >= 2:
                    ctx.wait_mark(s, "p4mm", ("wb", presentE[i_ - 2], 3))
                ctx.inc(s.dma_start(out=werow[0:1, i_ % 2, :],
                                    in_=we_d[e:e + 1, :]),
                        f"p4wr{i_ % 2}", 16)
            for bi in range(len(blocks)):
                hsidx = blocks[bi][3]
                for tc in range(4):
                    j = bi * 4 + tc
                    ctx.wait_mark(s, "p4ev", ("hb", bi, tc))
                    ctx.inc(s.dma_start(
                        out=hacts[hsidx, :, tc * 512:(tc + 1) * 512],
                        in_=hbuf[:, j % 2, :]), f"p4s{j % 2}", 16)
                    ctx.mark(f"p4s{j % 2}", ("st", bi, tc))
            ctx.wait(s, "p4s0")
            ctx.wait(s, "p4s1")

        @blk.tensor
        def _(t):
            ctx.wait_mark(t, "p4c", "done")
            ctx.wait_mark(t, "p4gld", "gate")
            ctx.wait_mark(t, "p4ev", "on1")
            for tc in range(4):
                if tc >= 1:
                    ctx.wait_mark(t, "p4ex", ("sg", tc - 1))
                for s_ in range(16):
                    inst = t.matmul(psL[:], gsb[:, s_, :],
                                    xsb[:, s_, tc * 512:(tc + 1) * 512],
                                    start=(s_ == 0), stop=(s_ == 15))
                ctx.inc(inst, "p4mm", 1)
                ctx.mark("p4mm", ("lg", tc))
            inst = t.matmul(psR[:, 0:8], on1[0:1, :], gbrow[0:1, :],
                            start=True, stop=True)
            ctx.inc(inst, "p4mm", 1)
            ctx.mark("p4mm", ("gbb",))
            ctx.wait_mark(t, "p4ev", ("gbbc",))
            inst = t.matmul(psR[:], on1[0:1, :], iorow[0:1, 0:16],
                            start=True, stop=True)
            ctx.inc(inst, "p4mm", 1)
            ctx.mark("p4mm", ("iob",))
            for ts in range(NTS):
                ctx.wait_mark(t, "p4ex", ("sg", 3))
                if ts >= 1:
                    ctx.wait_mark(t, "p4ev", ("rt1", ts - 1))
                else:
                    ctx.wait_mark(t, "p4ev", ("iobc",))
                inst = t.transpose(psR[:, 0:8],
                                   scoT[0:8, ts * 128:(ts + 1) * 128],
                                   idsb[0:8, 0:8])
                ctx.inc(inst, "p4mm", 1)
                ctx.mark("p4mm", ("rtp", ts))
                ctx.wait_mark(t, "p4ev", ("wet", ts))
                inst = t.transpose(psR2[:], rt[:, 11, :], idsb[:])
                ctx.inc(inst, "p4mm", 1)
                ctx.mark("p4mm", ("rtp2", ts))
            lastex = None
            for bi, (wsrc, gcol, ucol, hsidx, e) in enumerate(blocks):
                par = bi % 2
                if e is not None and (bi == 0 or blocks[bi - 1][4] != e):
                    i_ = eidx[e]
                    t.wait_ge(ctx.sem(f"p4wr{i_ % 2}"), 16 * (i_ // 2 + 1))
                    for tc in range(4):
                        if not (eidx[e] == 0 and tc == 0):
                            ctx.wait_mark(t, "p4ev", ("wbc", e, tc - 1)
                                          if tc >= 1 else
                                          ("wbc", presentE[eidx[e] - 1], 3))
                        inst = t.matmul(
                            psW[:], on1[0:1, :],
                            werow[0:1, eidx[e] % 2, tc * 512:(tc + 1) * 512],
                            start=True, stop=True)
                        ctx.inc(inst, "p4mm", 1)
                        ctx.mark("p4mm", ("wb", e, tc))
                t.wait_ge(ctx.sem(f"p4w{par}"), 32 * (bi // 2 + 1))
                for tc in range(4):
                    j = bi * 4 + tc
                    if j >= 2:
                        ctx.wait_mark(t, "p4ev", ("hb", (j - 2) // 4,
                                                  (j - 2) % 4))
                    for s_ in range(16):
                        inst = t.matmul(psG[j % 2][:], wg[:, par, s_, :],
                                        xsb[:, s_, tc * 512:(tc + 1) * 512],
                                        start=(s_ == 0), stop=(s_ == 15))
                    ctx.inc(inst, "p4mm", 1)
                    ctx.mark("p4mm", ("g", bi, tc))
                    for s_ in range(16):
                        inst = t.matmul(psU[j % 2][:], wu[:, par, s_, :],
                                        xsb[:, s_, tc * 512:(tc + 1) * 512],
                                        start=(s_ == 0), stop=(s_ == 15))
                    ctx.inc(inst, "p4mm", 1)
                    ctx.mark("p4mm", ("u", bi, tc))

        @blk.scalar
        def _(s):
            for tc in range(4):
                ctx.wait_mark(s, "p4mm", ("lg", tc))
                inst = s.activation(out=scoT[0:8, tc * 512:(tc + 1) * 512],
                                    in_=psL[:], func=AF.Sigmoid)
                ctx.inc(inst, "p4ex", 1)
                ctx.mark("p4ex", ("sg", tc))
            for bi in range(len(blocks)):
                for tc in range(4):
                    j = bi * 4 + tc
                    ctx.wait_mark(s, "p4mm", ("g", bi, tc))
                    if j >= 2:
                        ctx.wait_mark(s, "p4ev", ("t2", (j - 2) // 4,
                                                  (j - 2) % 4))
                    inst = s.activation(out=t1[:, j % 2, :], in_=psG[j % 2][:],
                                        func=AF.Silu)
                    ctx.inc(inst, "p4ex", 1)
                    ctx.mark("p4ex", ("sl", bi, tc))

        @blk.vector
        def _(v):
            def vsync(inst):
                ctx.inc(inst, "p4vs", 1)
                v.wait_ge(ctx.sem("p4vs"), ctx.cnt["p4vs"])
            inst = v.memset(on1[:], 1.0)
            ctx.inc(inst, "p4ev", 1)
            ctx.mark("p4ev", "on1")
            ctx.wait_mark(v, "p4mm", ("gbb",))
            inst = v.tensor_copy(gbb[:], psR[:, 0:8])
            ctx.inc(inst, "p4ev", 1)
            ctx.mark("p4ev", ("gbbc",))
            ctx.wait_mark(v, "p4mm", ("iob",))
            inst = v.tensor_copy(iob[:], psR[:, 0:16])
            ctx.inc(inst, "p4ev", 1)
            ctx.mark("p4ev", ("iobc",))
            for ts in range(NTS):
                ctx.wait_mark(v, "p4mm", ("rtp", ts))
                vsync(v.tensor_copy(rt[:, 0, :], psR[:, 0:8]))
                vsync(v.tensor_add(rt[:, 1, :], rt[:, 0, :], gbb[:]))
                vsync(v.reduce_max(rst[:, 0:1], rt[:, 1, :], axis=AX))
                vsync(v.tensor_scalar(rt[:, 2, :], rt[:, 1, :],
                                      rst[:, 0:1], None, ALU.is_equal))
                vsync(v.scalar_tensor_tensor(rt[:, 3, :], rt[:, 2, :],
                                             -100.0, iob[:, 8:16],
                                             ALU.mult, ALU.add))
                vsync(v.tensor_reduce(rst[:, 1:2], rt[:, 3, :], AX,
                                      ALU.min))
                vsync(v.tensor_scalar(rt[:, 4, :], iob[:, 0:8],
                                      rst[:, 1:2], None, ALU.is_equal))
                vsync(v.scalar_tensor_tensor(rt[:, 5, :], rt[:, 4, :],
                                             -100.0, rt[:, 1, :],
                                             ALU.mult, ALU.add))
                vsync(v.reduce_max(rst[:, 2:3], rt[:, 5, :], axis=AX))
                vsync(v.tensor_scalar(rt[:, 6, :], rt[:, 5, :],
                                      rst[:, 2:3], None, ALU.is_equal))
                vsync(v.scalar_tensor_tensor(rt[:, 7, :], rt[:, 6, :],
                                             -100.0, iob[:, 8:16],
                                             ALU.mult, ALU.add))
                vsync(v.tensor_reduce(rst[:, 3:4], rt[:, 7, :], AX,
                                      ALU.min))
                vsync(v.tensor_scalar(rt[:, 8, :], iob[:, 0:8],
                                      rst[:, 3:4], None, ALU.is_equal))
                vsync(v.tensor_add(rt[:, 9, :], rt[:, 4, :], rt[:, 8, :]))
                vsync(v.tensor_mul(rt[:, 10, :], rt[:, 9, :], rt[:, 0, :]))
                vsync(v.reduce_sum(rst[:, 4:5], rt[:, 10, :], axis=AX))
                vsync(v.reciprocal(rst[:, 5:6], rst[:, 4:5]))
                inst = v.tensor_scalar(rt[:, 11, :], rt[:, 10, :],
                                       rst[:, 5:6], None, ALU.mult)
                ctx.inc(inst, "p4ev", 1)
                ctx.mark("p4ev", ("wet", ts))
                ctx.wait_mark(v, "p4mm", ("rtp2", ts))
                if ts >= 2:
                    ctx.wait(v, f"p4wd{ts % 2}", at=16 * (ts // 2))
                inst = v.tensor_copy(weTt[:, ts % 2, :], psR2[:])
                ctx.inc(inst, "p4ev", 1)
                ctx.mark("p4ev", ("rt1", ts))
            ctx.mark("p4ev", ("rte", NTS - 1))
            for bi, (wsrc, gcol, ucol, hsidx, e) in enumerate(blocks):
                if e is not None and (bi == 0 or blocks[bi - 1][4] != e):
                    for tc in range(4):
                        ctx.wait_mark(v, "p4mm", ("wb", e, tc))
                        inst = v.tensor_copy(
                            webc[:, tc * 512:(tc + 1) * 512], psW[:])
                        ctx.inc(inst, "p4ev", 1)
                        ctx.mark("p4ev", ("wbc", e, tc))
                for tc in range(4):
                    j = bi * 4 + tc
                    csl = slice(tc * 512, (tc + 1) * 512)
                    ctx.wait_mark(v, "p4ex", ("sl", bi, tc))
                    ctx.wait_mark(v, "p4mm", ("u", bi, tc))
                    if j >= 2:
                        ctx.wait_mark(v, f"p4s{j % 2}",
                                      ("st", (j - 2) // 4, (j - 2) % 4))
                    if e is None:
                        inst = v.tensor_mul(hbuf[:, j % 2, :], t1[:, j % 2, :],
                                            psU[j % 2][:])
                    else:
                        vsync(v.tensor_mul(t2[:, j % 2, :], t1[:, j % 2, :],
                                           psU[j % 2][:]))
                        inst = v.tensor_mul(hbuf[:, j % 2, :],
                                            t2[:, j % 2, :], webc[:, csl])
                    ctx.inc(inst, "p4ev", 1)
                    ctx.mark("p4ev", ("t2", bi, tc))
                    ctx.mark("p4ev", ("hb", bi, tc))



def emit_p5(nc, ctx, io, ten):
    hacts, sdwT, dnT = io["hacts_s"], io["sdwT"], io["dnT"]
    x2pre = io["x2pre_s"]
    NTS = NT // 128
    CC = 256
    NCC = DM // CC

    def wsrc(hs, cc):
        csl = slice(cc * CC, (cc + 1) * CC)
        if hs < 22:
            return sdwT[hs * 128:(hs + 1) * 128, csl]
        e, j = divmod(hs - 22, 11)
        return dnT[e, j * 128:(j + 1) * 128, csl]

    from contextlib import ExitStack
    with ExitStack() as _st:
        wdn = _st.enter_context(nc.sbuf_tensor("p5w", [128, 2, NHS, CC], BF16))
        hsb = _st.enter_context(nc.sbuf_tensor("p5h", [128, 2, NHS, 128], BF16))
        xb = _st.enter_context(nc.sbuf_tensor("p5x", [128, 2, CC], F32))
        ps0 = _st.enter_context(nc.psum_tensor("p5p0", [128, CC], F32))
        ps1 = _st.enter_context(nc.psum_tensor("p5p1", [128, CC], F32))
        blk = _st.enter_context(nc.Block("p5"))
        psAB = (ps0, ps1)

        @blk.gpsimd
        def _(g):
            for cc in range(NCC):
                par = cc % 2
                if cc >= 2:
                    ctx.wait_mark(g, "p5mm", ("ccdone", cc - 2))
                for hs in range(NHS):
                    ctx.inc(g.dma_start(out=wdn[:, par, hs, :],
                                        in_=wsrc(hs, cc)), f"p5w{par}", 16)

        @blk.sync
        def _(s):
            for cc in range(NCC):
                for ts in range(NTS):
                    gidx = cc * NTS + ts
                    par = gidx % 2
                    if gidx >= 2:
                        ctx.wait_mark(s, "p5mm", ("tsdone", gidx - 2))
                    hsl = hacts[:, :, ts * 128:(ts + 1) * 128]
                    ctx.inc(s.dma_start(
                        out=hsb[:, par, :, :],
                        in_=hsl.rearrange("h p t -> p h t")),
                        f"p5h{par}", 16)
                    if gidx >= 1:
                        g2 = gidx - 1
                        ctx.wait_mark(s, "p5ev", ("ev", g2))
                        cc2, ts2 = divmod(g2, NTS)
                        ctx.inc(s.dma_start(
                            out=x2pre[ts2 * 128:(ts2 + 1) * 128,
                                      cc2 * CC:(cc2 + 1) * CC],
                            in_=xb[:, g2 % 2, :]), f"p5s{g2 % 2}", 16)
                        ctx.mark(f"p5s{g2 % 2}", ("st", g2))
            g2 = NCC * NTS - 1
            ctx.wait_mark(s, "p5ev", ("ev", g2))
            cc2, ts2 = divmod(g2, NTS)
            ctx.inc(s.dma_start(
                out=x2pre[ts2 * 128:(ts2 + 1) * 128,
                          cc2 * CC:(cc2 + 1) * CC],
                in_=xb[:, g2 % 2, :]), f"p5s{g2 % 2}", 16)
            ctx.mark(f"p5s{g2 % 2}", ("st", g2))
            ctx.wait(s, "p5s0")
            ctx.wait(s, "p5s1")

        @blk.tensor
        def _(t):
            for cc in range(NCC):
                par = cc % 2
                t.wait_ge(ctx.sem(f"p5w{par}"), 16 * NHS * (cc // 2 + 1))
                for ts in range(NTS):
                    gidx = cc * NTS + ts
                    hp = gidx % 2
                    t.wait_ge(ctx.sem(f"p5h{hp}"), 16 * (gidx // 2 + 1))
                    if gidx >= 2:
                        ctx.wait_mark(t, "p5ev", ("ev", gidx - 2))
                    for hs in range(NHS):
                        inst = t.matmul(psAB[gidx % 2][:],
                                        hsb[:, hp, hs, :],
                                        wdn[:, par, hs, :],
                                        start=(hs == 0), stop=(hs == NHS - 1))
                    ctx.inc(inst, "p5mm", 1)
                    ctx.mark("p5mm", ("mm", gidx))
                    ctx.mark("p5mm", ("tsdone", gidx))
                ctx.mark("p5mm", ("ccdone", cc))

        @blk.vector
        def _(v):
            for cc in range(NCC):
                for ts in range(NTS):
                    gidx = cc * NTS + ts
                    ctx.wait_mark(v, "p5mm", ("mm", gidx))
                    if gidx >= 2:
                        ctx.wait_mark(v, f"p5s{gidx % 2}", ("st", gidx - 2))
                    inst = v.tensor_copy(xb[:, gidx % 2, :],
                                         psAB[gidx % 2][:])
                    ctx.inc(inst, "p5ev", 1)
                    ctx.mark("p5ev", ("ev", gidx))




def emit_p7(nc, ctx, io, ten):
    x2T, qwT, kwT, vwT = io["x2T_s"], io["qwT"], io["kwT"], io["vwT"]
    qb, kb, vbrow = io["qb"], io["kb"], io["vbrow"]
    QT1, KT1, V1 = io["QT1_s"], io["KT1_s"], io["V1_s"]
    NCH = T // 512

    from contextlib import ExitStack
    with ExitStack() as _st:
        wq = _st.enter_context(nc.sbuf_tensor("p7wq", [128, 16, 1024], BF16))
        wk = _st.enter_context(nc.sbuf_tensor("p7wk", [128, 16, 1024], BF16))
        wv = _st.enter_context(nc.sbuf_tensor("p7wv", [128, 16, 1024], BF16))
        xsb = _st.enter_context(nc.sbuf_tensor("p7x", [128, 2, 16, 512], BF16))
        qbs = _st.enter_context(nc.sbuf_tensor("p7qb", [128, 8], F32))
        kbs = _st.enter_context(nc.sbuf_tensor("p7kb", [128, 8], F32))
        vbr = _st.enter_context(nc.sbuf_tensor("p7vr", [1, 1024], F32))
        on1 = _st.enter_context(nc.sbuf_tensor("p7on", [1, 128], F32))
        vbb = _st.enter_context(nc.sbuf_tensor("p7vb", [128, 1024], F32))
        qko = _st.enter_context(nc.sbuf_tensor("p7o", [128, 2, 512], BF16))
        v1b = _st.enter_context(nc.sbuf_tensor("p7v", [128, 2, 8, 132], BF16))
        ps0 = _st.enter_context(nc.psum_tensor("p7p0", [128, 512], F32))
        ps1 = _st.enter_context(nc.psum_tensor("p7p1", [128, 512], F32))
        psv = _st.enter_context(nc.psum_tensor("p7pv", [128, 512], F32))
        psb = _st.enter_context(nc.psum_tensor("p7pb", [128, 512], F32))
        blk = _st.enter_context(nc.Block("p7"))
        psAB = (ps0, ps1)

        @blk.gpsimd
        def _(g):
            for s_ in range(16):
                r = slice(s_ * 128, (s_ + 1) * 128)
                ctx.inc(g.dma_start(out=wq[:, s_, :], in_=qwT[r, :]),
                        "p7gw", 16)
                ctx.inc(g.dma_start(out=wk[:, s_, :], in_=kwT[r, :]),
                        "p7gw", 16)
                ctx.inc(g.dma_start(out=wv[:, s_, :], in_=vwT[r, :]),
                        "p7gw", 16)
            ctx.mark("p7gw", "done")

        @blk.sync
        def _(s):
            ctx.inc(s.dma_start(out=qbs[:], in_=qb[:, :]), "p7c", 16)
            ctx.inc(s.dma_start(out=kbs[:], in_=kb[:, :]), "p7c", 16)
            ctx.inc(s.dma_start(out=vbr[:], in_=vbrow[:, :]), "p7c", 16)
            ctx.mark("p7c", "done")
            for ch in range(NCH):
                par = ch % 2
                if ch >= 2:
                    ctx.wait_mark(s, "p7mm", ("chdone", ch - 2))
                xc = x2T[:, ch * 512:(ch + 1) * 512].rearrange(
                    "(s p) t -> p s t", p=128)
                ctx.inc(s.dma_start(out=xsb[:, par, :, :], in_=xc[:, :, :]),
                        f"p7x{par}", 16)
                if ch >= 1:
                    p7_stores(nc, ctx, io, s, ch - 1, qko, v1b)
            p7_stores(nc, ctx, io, s, NCH - 1, qko, v1b)
            for nm in ("p7sq0", "p7sq1", "p7sv0", "p7sv1"):
                ctx.wait(s, nm)

        @blk.vector
        def _(v):
            inst = v.memset(on1[:], 1.0)
            ctx.inc(inst, "p7ev", 1)
            ctx.mark("p7ev", "on1")
            inst = v.memset(v1b[:, 0, :, 128:132], 1.0)
            v.wait_ge(ctx.sem("p7ev"), ctx.cnt["p7ev"])
            inst = v.memset(v1b[:, 1, :, 128:132], 1.0)
            ctx.inc(inst, "p7ev", 1)
            ctx.mark("p7ev", "vones")
            for c2 in range(2):
                ctx.wait_mark(v, "p7mm", ("vbb", c2))
                inst = v.tensor_copy(vbb[:, c2 * 512:(c2 + 1) * 512], psb[:])
                ctx.inc(inst, "p7ev", 1)
                ctx.mark("p7ev", ("vbc", c2))
            for ch in range(NCH):
                par = ch % 2
                for qk in range(16):
                    j = ch * 16 + qk
                    dbs = (qbs if qk < 8 else kbs)[:, qk % 8:qk % 8 + 1]
                    ctx.wait_mark(v, "p7mm", ("qk", ch, qk))
                    if j >= 2:
                        ctx.wait_mark(v, f"p7sq{j % 2}", ("sq", j - 2))
                    inst = v.tensor_scalar(qko[:, j % 2, :], psAB[j % 2][:],
                                           dbs, None, ALU.add)
                    ctx.inc(inst, "p7ev", 1)
                    ctx.mark("p7ev", ("qke", ch, qk))
                for ts in range(4):
                    gt = ch * 4 + ts
                    if gt >= 2:
                        ctx.wait_mark(v, f"p7sv{gt % 2}", ("sv", gt - 2))
                    for vc in range(2):
                        idx = ch * 8 + ts * 2 + vc
                        ctx.wait_mark(v, "p7mm", ("v", ch, ts, vc))
                        for hh4 in range(4):
                            hidx = vc * 4 + hh4
                            inst = v.scalar_tensor_tensor(
                                v1b[:, gt % 2, hidx, 0:128],
                                psv[:, hh4 * 128:(hh4 + 1) * 128], 1.0,
                                vbb[:, hidx * 128:(hidx + 1) * 128],
                                ALU.mult, ALU.add)
                        ctx.inc(inst, "p7ev", 1)
                        ctx.mark("p7ev", ("ve", ch, ts, vc))

        @blk.tensor
        def _(t):
            ctx.wait_mark(t, "p7gw", "done")
            ctx.wait_mark(t, "p7c", "done")
            ctx.wait_mark(t, "p7ev", "on1")
            for c2 in range(2):
                if c2 >= 1:
                    ctx.wait_mark(t, "p7ev", ("vbc", c2 - 1))
                inst = t.matmul(psb[:], on1[0:1, :],
                                vbr[0:1, c2 * 512:(c2 + 1) * 512],
                                start=True, stop=True)
                ctx.inc(inst, "p7mm", 1)
                ctx.mark("p7mm", ("vbb", c2))
            for ch in range(NCH):
                par = ch % 2
                t.wait_ge(ctx.sem(f"p7x{par}"), 16 * (ch // 2 + 1))
                for qk in range(16):
                    j = ch * 16 + qk
                    w = wq if qk < 8 else wk
                    db = qk % 8
                    if j >= 2:
                        ctx.wait_mark(t, "p7ev", ("qke", (j - 2) // 16,
                                                  (j - 2) % 16))
                    for s_ in range(16):
                        inst = t.matmul(
                            psAB[j % 2][:],
                            w[:, s_, db * 128:(db + 1) * 128],
                            xsb[:, par, s_, :],
                            start=(s_ == 0), stop=(s_ == 15))
                    ctx.inc(inst, "p7mm", 1)
                    ctx.mark("p7mm", ("qk", ch, qk))
                for ts in range(4):
                    for vc in range(2):
                        idx = ch * 8 + ts * 2 + vc
                        if idx >= 1:
                            pr = divmod(idx - 1, 8)
                            ctx.wait_mark(t, "p7ev", ("ve", pr[0],
                                                      pr[1] // 2, pr[1] % 2))
                        for s_ in range(16):
                            inst = t.matmul(
                                psv[:],
                                xsb[:, par, s_, ts * 128:(ts + 1) * 128],
                                wv[:, s_, vc * 512:(vc + 1) * 512],
                                start=(s_ == 0), stop=(s_ == 15))
                        ctx.inc(inst, "p7mm", 1)
                        ctx.mark("p7mm", ("v", ch, ts, vc))
                ctx.mark("p7mm", ("chdone", ch))


def p7_stores(nc, ctx, io, s, ch, qko, v1b):
    QT1, KT1, V1 = io["QT1_s"], io["KT1_s"], io["V1_s"]
    csl = slice(ch * 512, (ch + 1) * 512)
    for qk in range(16):
        j = ch * 16 + qk
        dst = QT1 if qk < 8 else KT1
        row = (qk % 8) * 128
        ctx.wait_mark(s, "p7ev", ("qke", ch, qk))
        ctx.inc(s.dma_start(out=dst[row:row + 128, csl],
                            in_=qko[:, j % 2, :]), f"p7sq{j % 2}", 16)
        ctx.mark(f"p7sq{j % 2}", ("sq", j))
    for ts in range(4):
        gt = ch * 4 + ts
        ctx.wait_mark(s, "p7ev", ("ve", ch, ts, 1))
        ctx.wait_mark(s, "p7ev", "vones")
        ctx.inc(s.dma_start(out=V1[gt * 128:(gt + 1) * 128, :, :],
                            in_=v1b[:, gt % 2, :, :]), f"p7sv{gt % 2}", 16)
        ctx.mark(f"p7sv{gt % 2}", ("sv", gt))



def emit_p8(nc, ctx, io, ten):
    QT1, KT1, V1 = io["QT1_s"], io["KT1_s"], io["V1_s"]
    l1delta, ident = io["l1delta"], io["ident"]
    oT1 = io["oT1_s"]

    from contextlib import ExitStack
    with ExitStack() as _st:
        qsb = _st.enter_context(nc.sbuf_tensor("p8q", [128, 2, T], BF16))
        ksb = _st.enter_context(nc.sbuf_tensor("p8k", [128, 2, T], BF16))
        vsb = _st.enter_context(nc.sbuf_tensor("p8v", [128, 2, 32, 132], BF16))
        dts = _st.enter_context(nc.sbuf_tensor("p8d", [128, 6, 512], F32))
        idb = _st.enter_context(nc.sbuf_tensor("p8id", [128, 128], BF16))
        zb = _st.enter_context(nc.sbuf_tensor("p8z", [128, 2, 512], F32))
        eb = _st.enter_context(nc.sbuf_tensor("p8e", [128, 2, 512], BF16))
        ob = _st.enter_context(nc.sbuf_tensor("p8o", [128, 128], BF16))
        otb = _st.enter_context(nc.sbuf_tensor("p8ot", [128, 2, 128], BF16))
        den = _st.enter_context(nc.sbuf_tensor("p8dn", [128, 2], F32))
        psS = _st.enter_context(nc.psum_tensor("p8ps", [128, 512], F32))
        psO0 = _st.enter_context(nc.psum_tensor("p8o0", [128, 132], F32))
        psO1 = _st.enter_context(nc.psum_tensor("p8o1", [128, 132], F32))
        psO2 = _st.enter_context(nc.psum_tensor("p8o2", [128, 132], F32))
        psO3 = _st.enter_context(nc.psum_tensor("p8o3", [128, 132], F32))
        psT = _st.enter_context(nc.psum_tensor("p8t", [128, 128], BF16))
        blk = _st.enter_context(nc.Block("p8"))
        psO = (psO0, psO1, psO2, psO3)

        def jbs_of(ci):
            c0 = ci * 4
            return [jb for jb in range(max(0, c0 - 2), min(32, c0 + 4))]

        @blk.gpsimd
        def _(g):
            ctx.inc(g.dma_start(out=idb[:], in_=ident[:, :]), "p8g", 16)
            ctx.mark("p8g", "done")

        @blk.sync
        def _(s):
            for r in range(6):
                ctx.inc(s.dma_start(out=dts[:, r, :], in_=l1delta[r, :, :]),
                        "p8c", 16)
            ctx.mark("p8c", "done")
            ot_ctr = 0
            for h in range(8):
                hp = h % 2
                if h >= 2:
                    ctx.wait_mark(s, "p8mm", ("hdone", h - 2))
                ctx.inc(s.dma_start(out=qsb[:, hp, :],
                                    in_=QT1[h * 128:(h + 1) * 128, :]),
                        f"p8lq{hp}", 16)
                ctx.inc(s.dma_start(out=ksb[:, hp, :],
                                    in_=KT1[h * 128:(h + 1) * 128, :]),
                        f"p8lk{hp}", 16)
                vr = V1[:, h, :].rearrange("(s p) w -> p s w", p=128)
                ctx.inc(s.dma_start(out=vsb[:, hp, :, :], in_=vr[:, :, :]),
                        f"p8lv{hp}", 16)
                if h >= 1:
                    for ib in range(32):
                        ctx.wait_mark(s, "p8ev", ("oe", h - 1, ib))
                        ctx.inc(s.dma_start(
                            out=oT1[(h - 1) * 128:h * 128,
                                    ib * 128:(ib + 1) * 128],
                            in_=otb[:, ot_ctr % 2, :]), f"p8s{ot_ctr % 2}", 16)
                        ctx.mark(f"p8s{ot_ctr % 2}", ("st", ot_ctr))
                        ot_ctr += 1
            for ib in range(32):
                ctx.wait_mark(s, "p8ev", ("oe", 7, ib))
                ctx.inc(s.dma_start(
                    out=oT1[7 * 128:8 * 128, ib * 128:(ib + 1) * 128],
                    in_=otb[:, ot_ctr % 2, :]), f"p8s{ot_ctr % 2}", 16)
                ctx.mark(f"p8s{ot_ctr % 2}", ("st", ot_ctr))
                ot_ctr += 1
            ctx.wait(s, "p8s0")
            ctx.wait(s, "p8s1")

        @blk.tensor
        def _(t):
            ctx.wait_mark(t, "p8g", "done")
            ctx.wait_mark(t, "p8c", "done")
            gz = 0
            for h in range(8):
                hp = h % 2
                nld = 16 * (h // 2 + 1)
                t.wait_ge(ctx.sem(f"p8lq{hp}"), nld)
                t.wait_ge(ctx.sem(f"p8lk{hp}"), nld)
                t.wait_ge(ctx.sem(f"p8lv{hp}"), nld)
                for ci in range(8):
                    c0 = ci * 4
                    jl = jbs_of(ci)
                    started = [False] * 4
                    for jn, jb in enumerate(jl):
                        if gz >= 1:
                            ctx.wait_mark(t, "p8ev", ("z", gz - 1))
                        inst = t.matmul(psS[:],
                                        qsb[:, hp, jb * 128:(jb + 1) * 128],
                                        ksb[:, hp, c0 * 128:(c0 + 4) * 128],
                                        start=True, stop=True)
                        ctx.inc(inst, "p8mm", 1)
                        ctx.mark("p8mm", ("s", gz))
                        ctx.wait_mark(t, "p8ex", ("x", gz))
                        for sst in range(4):
                            ib = c0 + sst
                            if jb > ib or jb < ib - 2:
                                continue
                            first = (jb == max(0, ib - 2))
                            last = (jb == ib)
                            if first and ci >= 1 and False:
                                pass
                            if first:
                                ctx.wait_mark(t, "p8ev", ("ob", h, ib - 4)
                                              if ib >= 4 else ("init",))
                            inst = t.matmul(
                                psO[sst][:, 0:129],
                                eb[:, gz % 2, sst * 128:(sst + 1) * 128],
                                vsb[:, hp, jb, 0:129],
                                start=first, stop=last)
                            ctx.inc(inst, "p8mm", 1)
                            if last:
                                ctx.mark("p8mm", ("m", h, ib))
                        gz += 1
                    for sst in range(4):
                        ib = c0 + sst
                        ctx.wait_mark(t, "p8ev", ("obn", h, ib))
                        inst = t.transpose(psT[:], ob[:], idb[:])
                        ctx.inc(inst, "p8mm", 1)
                        ctx.mark("p8mm", ("ot", h, ib))
                ctx.mark("p8mm", ("hdone", h))

        @blk.scalar
        def _(s):
            gz = 0
            for h in range(8):
                for ci in range(8):
                    for jn, jb in enumerate(jbs_of(ci)):
                        ctx.wait_mark(s, "p8ev", ("z", gz))
                        inst = s.activation(out=eb[:, gz % 2, :],
                                            in_=zb[:, gz % 2, :],
                                            func=AF.Exp, scale=SC1)
                        ctx.inc(inst, "p8ex", 1)
                        ctx.mark("p8ex", ("x", gz))
                        gz += 1

        @blk.vector
        def _(v):
            def vsync(inst):
                ctx.inc(inst, "p8vs", 1)
                v.wait_ge(ctx.sem("p8vs"), ctx.cnt["p8vs"])
            inst = v.nop()
            ctx.inc(inst, "p8ev", 1)
            ctx.mark("p8ev", ("init",))
            gz = 0
            ot_ctr = 0
            for h in range(8):
                for ci in range(8):
                    c0 = ci * 4
                    for jn, jb in enumerate(jbs_of(ci)):
                        R = c0 - jb
                        ctx.wait_mark(v, "p8mm", ("s", gz))
                        if gz >= 2:
                            ctx.wait_mark(v, "p8ex", ("x", gz - 2))
                        inst = v.tensor_sub(zb[:, gz % 2, :], psS[:],
                                            dts[:, R + 3, :])
                        ctx.inc(inst, "p8ev", 1)
                        ctx.mark("p8ev", ("z", gz))
                        gz += 1
                    for sst in range(4):
                        ib = c0 + sst
                        ctx.wait_mark(v, "p8mm", ("m", h, ib))
                        vsync(v.reciprocal(den[:, 0:1],
                                           psO[sst][:, 128:129]))
                        if ot_ctr >= 2:
                            ctx.wait(v, f"p8s{ot_ctr % 2}",
                                     at=16 * (ot_ctr // 2))
                        inst = v.tensor_scalar(ob[:], psO[sst][:, 0:128],
                                               den[:, 0:1], None, ALU.mult)
                        ctx.inc(inst, "p8ev", 1)
                        ctx.mark("p8ev", ("obn", h, ib))
                        ctx.mark("p8ev", ("ob", h, ib))
                        ctx.wait_mark(v, "p8mm", ("ot", h, ib))
                        inst = v.tensor_copy(otb[:, ot_ctr % 2, :], psT[:])
                        ctx.inc(inst, "p8ev", 1)
                        ctx.mark("p8ev", ("oe", h, ib))
                        ot_ctr += 1



def emit_p10(nc, ctx, io, ten):
    x3T, x3, AT, BT = io["x3T_s"], io["x3_s"], io["AT"], io["BT"]
    wrT, br8c = io["wrT"], io["br8c"]
    iota8, iotaB8, ident = io["iota8"], io["iotaB8"], io["ident"]
    out_s = io["out_s"]
    NTS = NT // 128

    from contextlib import ExitStack
    with ExitStack() as _st:
        xsb = _st.enter_context(nc.sbuf_tensor("pAx", [128, 2, 16, 512], BF16))
        asb = _st.enter_context(nc.sbuf_tensor("pAa", [128, 16, 8, 64], BF16))
        bsb = _st.enter_context(nc.sbuf_tensor("pAb", [64, 8, DM], BF16))
        wrs = _st.enter_context(nc.sbuf_tensor("pAw", [128, 16, 8], BF16))
        brs = _st.enter_context(nc.sbuf_tensor("pAbr", [8, 1], F32))
        idsb = _st.enter_context(nc.sbuf_tensor("pAid", [128, 128], F32))
        on1 = _st.enter_context(nc.sbuf_tensor("pAon", [1, 128], F32))
        iorow = _st.enter_context(nc.sbuf_tensor("pAio", [1, 16], F32))
        iob = _st.enter_context(nc.sbuf_tensor("pAiob", [128, 16], F32))
        logT = _st.enter_context(nc.sbuf_tensor("pAlg", [8, NT], F32))
        rt = _st.enter_context(nc.sbuf_tensor("pArt", [128, 6, 8], F32))
        rst = _st.enter_context(nc.sbuf_tensor("pArst", [128, 8], F32))
        scTt = _st.enter_context(nc.sbuf_tensor("pAsct", [8, 2, 128], F32))
        scrow = _st.enter_context(nc.sbuf_tensor("pAscr", [1, 2, NT], F32))
        scb = _st.enter_context(nc.sbuf_tensor("pAscb", [64, 512], F32))
        hT = _st.enter_context(nc.sbuf_tensor("pAh", [64, 8, NT], BF16))
        xr = _st.enter_context(nc.sbuf_tensor("pAxr", [128, 2, DM], F32))
        xsum = _st.enter_context(nc.sbuf_tensor("pAxs", [128, 2, DM], F32))
        sq = _st.enter_context(nc.sbuf_tensor("pAsq", [128, DM], F32))
        stat = _st.enter_context(nc.sbuf_tensor("pAst", [128, 8], F32))
        of = _st.enter_context(nc.sbuf_tensor("pAof", [128, 2, DM], F32))
        sc_d = nc.dram_tensor("pAsc_d", [8, NT], F32, kind="Internal").ap()
        psL = _st.enter_context(nc.psum_tensor("pApl", [8, 512], F32))
        psR = _st.enter_context(nc.psum_tensor("pApr", [128, 16], F32))
        psR2 = _st.enter_context(nc.psum_tensor("pApr2", [8, 128], F32))
        psH = _st.enter_context(nc.psum_tensor("pAph", [64, 512], F32))
        psB = _st.enter_context(nc.psum_tensor("pApb", [64, 512], F32))
        psY0 = _st.enter_context(nc.psum_tensor("pAy0", [128, 512], F32))
        psY1 = _st.enter_context(nc.psum_tensor("pAy1", [128, 512], F32))
        blk = _st.enter_context(nc.Block("p10"))
        psY = (psY0, psY1)

        @blk.gpsimd
        def _(g):
            for s_ in range(16):
                r = slice(s_ * 128, (s_ + 1) * 128)
                ctx.inc(g.dma_start(out=wrs[:, s_, :], in_=wrT[r, :]),
                        "pAg", 16)
                for e in range(8):
                    ctx.inc(g.dma_start(out=asb[:, s_, e, :],
                                        in_=AT[e, r, :]), "pAg", 16)
            for e in range(8):
                ctx.inc(g.dma_start(out=bsb[:, e, :], in_=BT[e, :, :]),
                        "pAg", 16)
            ctx.mark("pAg", "done")

        @blk.sync
        def _(s):
            ctx.inc(s.dma_start(out=brs[:], in_=br8c[:, :]), "pAc", 16)
            ctx.inc(s.dma_start(out=idsb[:], in_=ident[:, :]), "pAc", 16)
            ctx.inc(s.dma_start(out=iorow[0:1, 0:8], in_=iota8[:, :]),
                    "pAc", 16)
            ctx.inc(s.dma_start(out=iorow[0:1, 8:16], in_=iotaB8[:, :]),
                    "pAc", 16)
            ctx.mark("pAc", "done")

            def xchunk_load(s, cidx):
                par = cidx % 2
                tc = cidx % 4 if cidx < 4 else (cidx - 4) % 4
                if cidx >= 2:
                    ctx.wait_mark(s, "pAmm", ("xfree", cidx - 2))
                xc = x3T[:, tc * 512:(tc + 1) * 512].rearrange(
                    "(s p) t -> p s t", p=128)
                ctx.inc(s.dma_start(out=xsb[:, par, :, :], in_=xc[:, :, :]),
                        f"pAx{par}", 16)

            for cidx in range(6):
                xchunk_load(s, cidx)
            for ts in range(NTS):
                ctx.wait_mark(s, "pAev", ("rt1", ts))
                ctx.inc(s.dma_start(out=sc_d[:, ts * 128:(ts + 1) * 128],
                                    in_=scTt[:, ts % 2, :]),
                        f"pAwd{ts % 2}", 16)
            ctx.wait(s, "pAwd0")
            ctx.wait(s, "pAwd1")
            def wrow_load(s, e):
                if e >= 2:
                    ctx.wait_mark(s, "pAmm", ("hh", e - 2, 3))
                ctx.inc(s.dma_start(out=scrow[0:1, e % 2, :],
                                    in_=sc_d[e:e + 1, :]), f"pAwr{e % 2}", 16)
            wrow_load(s, 0)
            wrow_load(s, 1)
            for e in range(8):
                for cidx in range(max(6, 4 * e + 4), 4 * e + 8):
                    xchunk_load(s, cidx)
                if e + 2 < 8:
                    wrow_load(s, e + 2)
            for ts in range(NTS):
                par = ts % 2
                if ts >= 2:
                    ctx.wait_mark(s, "pAev", ("lnf", ts - 2))
                ctx.inc(s.dma_start(out=xr[:, par, :],
                                    in_=x3[ts * 128:(ts + 1) * 128, :]),
                        f"pAr{par}", 16)
                if ts >= 1:
                    t2 = ts - 1
                    ctx.wait_mark(s, "pAev", ("lnf", t2))
                    ctx.inc(s.dma_start(
                        out=out_s[t2 * 128:(t2 + 1) * 128, :],
                        in_=of[:, t2 % 2, :]), f"pAs{t2 % 2}", 16)
                    ctx.mark(f"pAs{t2 % 2}", ("sf", t2))
            t2 = NTS - 1
            ctx.wait_mark(s, "pAev", ("lnf", t2))
            ctx.inc(s.dma_start(out=out_s[t2 * 128:(t2 + 1) * 128, :],
                                in_=of[:, t2 % 2, :]), f"pAs{t2 % 2}", 16)
            ctx.mark(f"pAs{t2 % 2}", ("sf", t2))
            ctx.wait(s, "pAs0")
            ctx.wait(s, "pAs1")

        @blk.tensor
        def _(t):
            ctx.wait_mark(t, "pAg", "done")
            ctx.wait_mark(t, "pAc", "done")
            ctx.wait_mark(t, "pAev", "on1")
            for tc in range(4):
                cidx = tc
                t.wait_ge(ctx.sem(f"pAx{cidx % 2}"), 16 * (cidx // 2 + 1))
                if tc >= 1:
                    ctx.wait_mark(t, "pAev", ("lge", tc - 1))
                for s_ in range(16):
                    inst = t.matmul(psL[:], wrs[:, s_, :],
                                    xsb[:, cidx % 2, s_, :],
                                    start=(s_ == 0), stop=(s_ == 15))
                ctx.inc(inst, "pAmm", 1)
                ctx.mark("pAmm", ("lg", tc))
                ctx.mark("pAmm", ("xfree", cidx))
            inst = t.matmul(psR[:], on1[0:1, :], iorow[0:1, 0:16],
                            start=True, stop=True)
            ctx.inc(inst, "pAmm", 1)
            ctx.mark("pAmm", ("iob",))
            for ts in range(NTS):
                ctx.wait_mark(t, "pAev", ("lge", 3))
                if ts >= 1:
                    ctx.wait_mark(t, "pAev", ("rt0", ts - 1))
                else:
                    ctx.wait_mark(t, "pAev", ("iobc",))
                inst = t.transpose(psR[:, 0:8],
                                   logT[0:8, ts * 128:(ts + 1) * 128],
                                   idsb[0:8, 0:8])
                ctx.inc(inst, "pAmm", 1)
                ctx.mark("pAmm", ("rtp", ts))
                ctx.wait_mark(t, "pAev", ("sct", ts))
                inst = t.transpose(psR2[:], rt[:, 5, :], idsb[:])
                ctx.inc(inst, "pAmm", 1)
                ctx.mark("pAmm", ("rtp2", ts))
            for e in range(8):
                t.wait_ge(ctx.sem(f"pAwr{e % 2}"), 16 * (e // 2 + 1))
                for tc in range(4):
                    if not (e == 0 and tc == 0):
                        pe, ptc = (e, tc - 1) if tc >= 1 else (e - 1, 3)
                        ctx.wait_mark(t, "pAev", ("scbc", pe, ptc))
                    inst = t.matmul(psB[:], on1[0:1, 0:64],
                                    scrow[0:1, e % 2,
                                          tc * 512:(tc + 1) * 512],
                                    start=True, stop=True)
                    ctx.inc(inst, "pAmm", 1)
                    ctx.mark("pAmm", ("scb", e, tc))
                    cidx = 4 + e * 4 + tc
                    t.wait_ge(ctx.sem(f"pAx{cidx % 2}"),
                              16 * (cidx // 2 + 1))
                    if not (e == 0 and tc == 0):
                        pe, ptc = (e, tc - 1) if tc >= 1 else (e - 1, 3)
                        ctx.wait_mark(t, "pAev", ("he", pe, ptc))
                    for s_ in range(16):
                        inst = t.matmul(psH[:], asb[:, s_, e, :],
                                        xsb[:, cidx % 2, s_, :],
                                        start=(s_ == 0), stop=(s_ == 15))
                    ctx.inc(inst, "pAmm", 1)
                    ctx.mark("pAmm", ("hh", e, tc))
                    ctx.mark("pAmm", ("xfree", cidx))
            for ts in range(NTS):
                for oc in range(4):
                    j = ts * 4 + oc
                    if j >= 2:
                        ctx.wait_mark(t, "pAev", ("ye", (j - 2) // 4,
                                                  (j - 2) % 4))
                    for e in range(8):
                        if e == 0:
                            ctx.wait_mark(t, "pAev", ("he", 7, 3))
                        inst = t.matmul(
                            psY[j % 2][:],
                            hT[:, e, ts * 128:(ts + 1) * 128],
                            bsb[:, e, oc * 512:(oc + 1) * 512],
                            start=(e == 0), stop=(e == 7))
                    ctx.inc(inst, "pAmm", 1)
                    ctx.mark("pAmm", ("yy", ts, oc))

        @blk.scalar
        def _(s):
            for ts in range(NTS):
                ctx.wait_mark(s, "pAev", ("negm", ts))
                inst = s.activation(out=rt[:, 3, :], in_=rt[:, 0, :],
                                    func=AF.Exp, bias=rst[:, 1:2],
                                    accum_out=rst[:, 2:3])
                ctx.inc(inst, "pAex", 1)
                ctx.mark("pAex", ("exps", ts))

        @blk.vector
        def _(v):
            def vsync(inst):
                ctx.inc(inst, "pAvs", 1)
                v.wait_ge(ctx.sem("pAvs"), ctx.cnt["pAvs"])
            inst = v.memset(on1[:], 1.0)
            ctx.inc(inst, "pAev", 1)
            ctx.mark("pAev", "on1")
            for tc in range(4):
                ctx.wait_mark(v, "pAmm", ("lg", tc))
                inst = v.tensor_scalar(logT[0:8, tc * 512:(tc + 1) * 512],
                                       psL[:], brs[:, 0:1], None, ALU.add)
                ctx.inc(inst, "pAev", 1)
                ctx.mark("pAev", ("lge", tc))
            ctx.wait_mark(v, "pAmm", ("iob",))
            inst = v.tensor_copy(iob[:], psR[:, 0:16])
            ctx.inc(inst, "pAev", 1)
            ctx.mark("pAev", ("iobc",))
            for ts in range(NTS):
                ctx.wait_mark(v, "pAmm", ("rtp", ts))
                inst = v.tensor_copy(rt[:, 0, :], psR[:, 0:8])
                ctx.inc(inst, "pAev", 1)
                ctx.mark("pAev", ("rt0", ts))
                ctx.wait(v, "pAev")
                vsync(v.reduce_max(rst[:, 0:1], rt[:, 0, :], axis=AX))
                inst = v.tensor_scalar(rst[:, 1:2], rst[:, 0:1], -1.0, None,
                                       ALU.mult)
                ctx.inc(inst, "pAev", 1)
                ctx.mark("pAev", ("negm", ts))
                ctx.wait(v, "pAev")
                ctx.wait_mark(v, "pAex", ("exps", ts))
                vsync(v.reciprocal(rst[:, 3:4], rst[:, 2:3]))
                vsync(v.tensor_scalar(rt[:, 1, :], rt[:, 0, :], rst[:, 0:1],
                                      None, ALU.is_equal))
                vsync(v.scalar_tensor_tensor(rt[:, 2, :], rt[:, 1, :], -100.0,
                                             iob[:, 8:16], ALU.mult, ALU.add))
                vsync(v.tensor_reduce(rst[:, 4:5], rt[:, 2, :], AX, ALU.min))
                vsync(v.tensor_scalar(rt[:, 4, :], iob[:, 0:8], rst[:, 4:5],
                                      None, ALU.is_equal))
                inst = v.tensor_scalar(rt[:, 5, :], rt[:, 4, :], rst[:, 3:4],
                                       1.0 / RANK, ALU.mult, ALU.mult)
                ctx.inc(inst, "pAev", 1)
                ctx.mark("pAev", ("sct", ts))
                ctx.wait(v, "pAev")
                ctx.wait_mark(v, "pAmm", ("rtp2", ts))
                if ts >= 2:
                    ctx.wait(v, f"pAwd{ts % 2}", at=16 * (ts // 2))
                inst = v.tensor_copy(scTt[:, ts % 2, :], psR2[:])
                ctx.inc(inst, "pAev", 1)
                ctx.mark("pAev", ("rt1", ts))
            for e in range(8):
                for tc in range(4):
                    csl = slice(tc * 512, (tc + 1) * 512)
                    ctx.wait_mark(v, "pAmm", ("scb", e, tc))
                    ctx.wait(v, "pAev")
                    inst = v.tensor_copy(scb[:], psB[:])
                    ctx.inc(inst, "pAev", 1)
                    ctx.mark("pAev", ("scbc", e, tc))
                    ctx.wait(v, "pAev")
                    ctx.wait_mark(v, "pAmm", ("hh", e, tc))
                    inst = v.tensor_mul(hT[:, e, csl], psH[:], scb[:])
                    ctx.inc(inst, "pAev", 1)
                    ctx.mark("pAev", ("he", e, tc))
            for ts in range(NTS):
                par = ts % 2
                v.wait_ge(ctx.sem(f"pAr{par}"), 16 * (ts // 2 + 1))
                for oc in range(4):
                    csl = slice(oc * 512, (oc + 1) * 512)
                    j = ts * 4 + oc
                    ctx.wait_mark(v, "pAmm", ("yy", ts, oc))
                    if ts >= 2 and oc == 0:
                        ctx.wait_mark(v, "pAex", ("sq", ts - 2))
                    inst = v.tensor_add(xsum[:, par, csl], psY[j % 2][:],
                                        xr[:, par, csl])
                    ctx.inc(inst, "pAev", 1)
                    ctx.mark("pAev", ("ye", ts, oc))
                ctx.wait(v, "pAev")
                vsync(v.reduce_sum(stat[:, 0:1], xsum[:, par, :], axis=AX))
                if ts >= 1:
                    ctx.wait_mark(v, "pAex", ("sq", ts - 1))
                v.tensor_scalar(stat[:, 1:2], stat[:, 0:1], 1.0 / DM, None,
                                ALU.mult)
                inst = v.tensor_scalar(stat[:, 2:3], stat[:, 0:1],
                                       -1.0 / DM, None, ALU.mult)
                ctx.inc(inst, "pAev", 1)
                ctx.mark("pAev", ("mn", ts))
                ctx.wait_mark(v, "pAex", ("sq", ts))
                inst = v.tensor_scalar(stat[:, 4:5], stat[:, 3:4], 1.0 / DM,
                                       EPS, ALU.mult, ALU.add)
                ctx.inc(inst, "pAev", 1)
                ctx.mark("pAev", ("veps", ts))
                ctx.wait_mark(v, "pAex", ("sd", ts))
                vsync(v.reciprocal(stat[:, 6:7], stat[:, 5:6]))
                if ts >= 2:
                    ctx.wait_mark(v, f"pAs{par}", ("sf", ts - 2))
                inst = v.tensor_scalar(of[:, par, :], xsum[:, par, :],
                                       stat[:, 1:2], stat[:, 6:7],
                                       ALU.subtract, ALU.mult)
                ctx.inc(inst, "pAev", 1)
                ctx.mark("pAev", ("lnf", ts))

        @blk.scalar
        def _(s):
            for ts in range(NTS):
                par = ts % 2
                ctx.wait_mark(s, "pAev", ("mn", ts))
                inst = s.activation(out=sq[:], in_=xsum[:, par, :],
                                    func=AF.Square, bias=stat[:, 2:3],
                                    accum_out=stat[:, 3:4])
                ctx.inc(inst, "pAex", 1)
                ctx.mark("pAex", ("sq", ts))
                ctx.wait_mark(s, "pAev", ("veps", ts))
                inst = s.activation(out=stat[:, 5:6], in_=stat[:, 4:5],
                                    func=AF.Sqrt)
                ctx.inc(inst, "pAex", 1)
                ctx.mark("pAex", ("sd", ts))



SCRATCH_SPECS = {
    "dbg_s": ([128, 4096], F32),
    "oT0_s": ([1024, T], BF16),
    "oT0full_s": ([DM, NT], BF16),
    "x1_s": ([NT, DM], F32),
    "x1T_s": ([DM, NT], BF16),
    "hacts_s": ([NHS, 128, NT], BF16),
    "x2pre_s": ([NT, DM], F32),
    "x2_s": ([NT, DM], F32),
    "x2bT_s": ([DM, NT], BF16),
    "x2T_s": ([DM, T], BF16),
    "KT1_s": ([1024, T], BF16),
    "QT1_s": ([1024, T], BF16),
    "V1_s": ([T, 8, 132], BF16),
    "oT1_s": ([1024, T], BF16),
    "oT1full_s": ([DM, NT], BF16),
    "x3_s": ([NT, DM], F32),
    "x3T_s": ([DM, NT], BF16),
    "out_s": ([NT, DM], F32),
    "kT_s": ([1024, T], BF16),
    "qT_s": ([1024, T], BF16),
    "v_s": ([T, 8, 132], BF16),
    "beta_s": ([8, T], F32),
    "betaT_s": ([T, 8], F32),
}

INPUT_SPECS = {
    "xT": ([DM, T], F32),
    "x0h": ([NT, DM], F32),
    "wqkvb": ([DM, 3080], F32),
    "bb8c": ([8, 1], F32),
    "wo": ([DM, DM], F32),
    "gwT": ([DM, 2 * HS], F32),
    "sdwT": ([HS, DM], F32),
    "gateT": ([DM, 8], F32),
    "gbias": ([1, 8], F32),
    "upT": ([8, DM, 2 * HF], F32),
    "wblk": ([128, NHS, 2, 16, 128], F32),
    "dnT": ([8, HF, DM], F32),
    "qwT": ([DM, 1024], F32),
    "kwT": ([DM, 1024], F32),
    "vwT": ([DM, 1024], F32),
    "qb": ([128, 8], F32),
    "kb": ([128, 8], F32),
    "vbrow": ([1, 1024], F32),
    "owT": ([DM, DM], F32),
    "obrow": ([1, DM], F32),
    "AT": ([8, DM, RANK], F32),
    "BT": ([8, RANK, DM], F32),
    "wrT": ([DM, 8], F32),
    "br": ([1, 8], F32),
    "br8c": ([8, 1], F32),
    "cos2": ([128, T], F32),
    "sins": ([128, T], F32),
    "ident": ([128, 128], F32),
    "dstrict": ([128, 128], F32),
    "dincl": ([128, 128], F32),
    "l1delta": ([6, 128, 512], F32),
    "iota8": ([1, 8], F32),
    "iotaB8": ([1, 8], F32),
    "meta": ([1, 8], I32),
}

PHASES = {
    "p1": dict(emit=emit_p1,
               reads=["xT", "wqkvb", "bb8c", "cos2", "sins", "ident"],
               writes=["kT_s", "qT_s", "v_s", "beta_s", "betaT_s"]),
    "p2": dict(emit=emit_p2, twopass=True,
               reads=["kT_s", "qT_s", "v_s", "beta_s", "betaT_s",
                      "dstrict", "dincl", "ident"],
               writes=["oT0_s", "dbg_s"]),
    "p8": dict(emit=emit_p8, twopass=True,
               reads=["QT1_s", "KT1_s", "V1_s", "l1delta", "ident"],
               writes=["oT1_s"]),
    "p10": dict(emit=emit_p10, twopass=True,
                reads=["x3T_s", "x3_s", "AT", "BT", "wrT", "br8c",
                       "iota8", "iotaB8", "ident"],
                writes=["out_s"]),
    "p7": dict(emit=emit_p7, twopass=True,
               reads=["x2T_s", "qwT", "kwT", "vwT", "qb", "kb", "vbrow"],
               writes=["QT1_s", "KT1_s", "V1_s"]),
    "p5": dict(emit=emit_p5, twopass=True,
               reads=["hacts_s", "sdwT", "dnT"],
               writes=["x2pre_s"]),
    "p4": dict(emit=emit_p4, twopass=True,
               reads=["x1T_s", "wblk", "gateT", "gbias",
                      "iota8", "iotaB8", "ident"],
               writes=["hacts_s"]),
    "p3": dict(emit=lambda nc, ctx, io, ten: emit_post(nc, ctx, io, dict(
                   pref="p3", oT="oT0full_s", w="wo", resid="x0h",
                   out_f32="x1_s", out_T="x1T_s")),
               twopass=True,
               reads=["oT0full_s", "wo", "x0h", "ident"],
               writes=["x1_s", "x1T_s"]),
    "p6": dict(emit=lambda nc, ctx, io, ten: emit_post(nc, ctx, io, dict(
                   pref="p6", pre="x2pre_s", resid="x1_s",
                   out_f32="x2_s", out_T="x2bT_s")),
               twopass=True,
               reads=["x2pre_s", "x1_s", "ident"],
               writes=["x2_s", "x2bT_s"]),
    "p9": dict(emit=lambda nc, ctx, io, ten: emit_post(nc, ctx, io, dict(
                   pref="p9", oT="oT1full_s", w="owT", bias="obrow",
                   resid="x2_s", out_f32="x3_s", out_T="x3T_s",
                   sqalias=True)),
               twopass=True,
               reads=["oT1full_s", "owT", "obrow", "x2_s", "ident"],
               writes=["x3_s", "x3T_s"]),
}


def host_pair_merge(envs, srckey, dstkey, mode):
    for c in range(NCORE):
        p, h = c // 2, c % 2
        a, b = envs[2 * p], envs[2 * p + 1]
        tk = slice(h * NT, (h + 1) * NT)
        if mode == "heads":
            envs[c][dstkey] = np.ascontiguousarray(np.concatenate(
                [a[srckey][:, tk], b[srckey][:, tk]], axis=0))
        elif mode == "cols":
            envs[c][dstkey] = np.ascontiguousarray(np.concatenate(
                [a[srckey], b[srckey]], axis=1))
    return envs


def build_single_phase(name):
    ph = PHASES[name]
    nc = bass.Bass("TRN2", target_bir_lowering=False)
    io = {}
    for t in ph["reads"]:
        shape, dt = INPUT_SPECS.get(t) or SCRATCH_SPECS[t]
        io[t] = nc.dram_tensor(t, shape, dt, kind="ExternalInput").ap()
    for t in ph["writes"]:
        shape, dt = SCRATCH_SPECS[t]
        io[t] = nc.dram_tensor(t, shape, dt, kind="ExternalOutput").ap()
    ctx = Ctx(nc)
    ph["emit"](nc, ctx, io, None)
    return nc, ph


def build_single_phase2(name):
    ph = PHASES[name]
    marks = None
    for pss in (1, 2):
        nc = bass.Bass("TRN2", target_bir_lowering=False)
        io = {}
        for t in ph["reads"]:
            shape, dt = INPUT_SPECS.get(t) or SCRATCH_SPECS[t]
            io[t] = nc.dram_tensor(t, shape, dt, kind="ExternalInput").ap()
        for t in ph["writes"]:
            shape, dt = SCRATCH_SPECS[t]
            io[t] = nc.dram_tensor(t, shape, dt, kind="ExternalOutput").ap()
        ctx = Ctx(nc, marks=marks)
        ph["emit"](nc, ctx, io, None)
        ctx.check_pass2()
        marks = ctx.marks
    return nc, ph


def run_phase_unfused(name, envs):
    if PHASES[name].get("twopass"):
        nc, ph = build_single_phase2(name)
    else:
        nc, ph = build_single_phase(name)
    in_maps = [{t: envs[c][t] for t in ph["reads"]} for c in range(NCORE)]
    res = run_bass_kernel_spmd(nc, in_maps, core_ids=list(range(NCORE)))
    for c in range(NCORE):
        for t in ph["writes"]:
            envs[c][t] = res.results[c][t]
    return envs


def run_p4_split(envs, bounds=(0, 33, 66, 99, 110)):
    global P4_LIMIT
    acc = [np.empty((NHS, 128, NT), bf16) for _ in range(NCORE)]
    try:
        for lo, hi in zip(bounds[:-1], bounds[1:]):
            P4_LIMIT = lambda b, lo=lo, hi=hi: lo <= b[3] < hi
            envs = run_phase_unfused("p4", envs)
            for c in range(NCORE):
                acc[c][lo:hi] = envs[c]["hacts_s"][lo:hi]
    finally:
        P4_LIMIT = None
    for c in range(NCORE):
        envs[c]["hacts_s"] = acc[c]
    return envs


def kernel(**inputs):
    rng = np.random.default_rng()
    nonce = int(rng.integers(1, 2 ** 30))
    envs = [make_core_inputs(c, inputs, nonce) for c in range(NCORE)]
    envs = run_phase_unfused("p1", envs)
    envs = run_phase_unfused("p2", envs)
    envs = host_pair_merge(envs, "oT0_s", "oT0full_s", "heads")
    envs = run_phase_unfused("p3", envs)
    envs = run_p4_split(envs)
    envs = run_phase_unfused("p5", envs)
    envs = run_phase_unfused("p6", envs)
    envs = host_pair_merge(envs, "x2bT_s", "x2T_s", "cols")
    envs = run_phase_unfused("p7", envs)
    envs = run_phase_unfused("p8", envs)
    envs = host_pair_merge(envs, "oT1_s", "oT1full_s", "heads")
    envs = run_phase_unfused("p9", envs)
    envs = run_phase_unfused("p10", envs)
    out = np.empty((B, T, DM), np.float32)
    for c in range(NCORE):
        p, h = c // 2, c % 2
        out[p, h * NT:(h + 1) * NT, :] = envs[c]["out_s"]
    return out
